# version 01 (final)
# baseline (speedup 1.0000x reference)
import numpy as np
import ml_dtypes
import concourse.bass as bass
import concourse.mybir as mybir
from concourse.bass_types import AP
from concourse.bass_utils import run_bass_kernel_spmd

F32 = mybir.dt.float32
BF16 = mybir.dt.bfloat16
OP = mybir.AluOpType
AF = mybir.ActivationFunctionType
AX = mybir.AxisListType.X
BF = ml_dtypes.bfloat16

N = 2048
C = 1024
NT = N // 128
CC = C // 128
MAGIC = np.float32(3.0 * 2.0 ** 22)


def _conv1_lhsT(w):
    out = np.zeros((32, 5, 128, 128), np.float32)
    j = np.arange(32)[:, None]
    c = np.arange(128)[None, :]
    r = (32 * j + c // 4) % 128
    for t in range(5):
        out[j, t, r, c] = w[128 * j + c, 0, t]
    return out


def _conv2_lhsT(w):
    out = np.zeros((8, 4, 5, 128, 128), np.float32)
    for j2 in range(8):
        for icc in range(4):
            for t in range(5):
                icl = np.arange(128)
                ic = 512 * j2 + 128 * icc + icl
                ocg = ic // 4
                r = ic % 4
                oc = ocg - 128 * j2
                m = (oc >= 0) & (oc < 128)
                out[j2, icc, t, icl[m], oc[m]] = w[ocg[m], r[m], t]
    return out


def _rope_tables():
    inv = 1.0 / 10000.0 ** (np.arange(0, C, 2, dtype=np.float64) / C)
    freqs = np.arange(N, dtype=np.float64)[:, None] * inv[None, :]
    cos = np.repeat(np.cos(freqs), 2, axis=1).astype(np.float32)
    sin = np.repeat(np.sin(freqs), 2, axis=1).astype(np.float32)
    ssw = sin.copy()
    ssw[:, 1::2] = -sin[:, 1::2]
    return cos.astype(BF), ssw.astype(BF)


def _t5_bucket(d):
    nb = 16
    rel = (d > 0).astype(np.int64) * nb
    rp = np.abs(d)
    max_exact = nb // 2
    small = rp < max_exact
    with np.errstate(divide="ignore"):
        rl = max_exact + (
            np.log(np.maximum(rp, 1) / max_exact) / np.log(128.0 / max_exact) * (nb - max_exact)
        ).astype(np.int64)
    rl = np.minimum(rl, nb - 1)
    return rel + np.where(small, rp, rl)


def _onehotT():
    d = np.arange(4095) - 2047
    b = _t5_bucket(d)
    out = np.zeros((32, 4095), np.float32)
    out[b, np.arange(4095)] = 1.0
    return out


def host_prep(i):
    f = lambda a: np.ascontiguousarray(np.asarray(a), dtype=np.float32)
    T = lambda a: np.ascontiguousarray(np.asarray(a, dtype=np.float32).T)
    m = {}
    m["c1_lhsT"] = np.ascontiguousarray(_conv1_lhsT(f(i["L0_attn_conv1_w"])).transpose(2, 0, 1, 3)).reshape(128, 32 * 5 * 128)
    m["c2_lhsT"] = np.ascontiguousarray(_conv2_lhsT(f(i["L0_attn_conv2_w"])).transpose(3, 0, 1, 2, 4)).reshape(128, 8 * 4 * 5 * 128)
    m["c1_b"] = f(i["L0_attn_conv1_b"]).reshape(32, 128).T.copy()
    m["c2_b"] = f(i["L0_attn_conv2_b"]).reshape(8, 128).T.copy()
    m["ln0_g"] = f(i["L0_attn_ln0_g"]).reshape(1, C)
    m["ln0_b"] = f(i["L0_attn_ln0_b"]).reshape(1, C)
    m["Wq"] = f(i["L0_attn_Wq"])
    m["Wk"] = f(i["L0_attn_Wk"])
    m["Wv"] = f(i["L0_attn_Wv"])
    m["bq"] = f(i["L0_attn_bq"])
    m["bk"] = f(i["L0_attn_bk"])
    m["bv"] = f(i["L0_attn_bv"])
    m["lnA_g"] = f(i["L0_attn_lnA_g"])
    m["lnA_b"] = f(i["L0_attn_lnA_b"])
    m["W1"] = f(i["L0_attn_W1"])
    m["b1"] = f(i["L0_attn_b1"]).reshape(32, 128).T.copy()
    m["W2"] = f(i["L0_attn_W2"])
    m["b2"] = f(i["L0_attn_b2"]).reshape(1, C)
    m["ln2_g"] = f(i["L0_attn_ln2_g"]).reshape(1, C)
    m["ln2_b"] = f(i["L0_attn_ln2_b"]).reshape(1, C)
    m["upT"] = np.ascontiguousarray(np.asarray(i["L0_ffn_up"], dtype=np.float32).transpose(0, 2, 1))
    m["gateT"] = np.ascontiguousarray(np.asarray(i["L0_ffn_gate"], dtype=np.float32).transpose(0, 2, 1))
    m["downT"] = np.ascontiguousarray(np.asarray(i["L0_ffn_down"], dtype=np.float32).transpose(0, 2, 1))
    m["routerT"] = T(i["L0_ffn_router"])
    m["wusT"] = T(i["L0_ffn_w_up_s"])
    m["wgsT"] = T(i["L0_ffn_w_gate_s"])
    m["wdsT"] = T(i["L0_ffn_w_down_s"])
    m["lnw1"] = f(i["L1_attn_ln_weight"]).reshape(1, C)
    m["wq1"] = f(i["L1_attn_wq"])
    m["wk1"] = f(i["L1_attn_wk"])
    m["wv1"] = f(i["L1_attn_wv"])
    m["wo1"] = f(i["L1_attn_wo"])
    m["relb"] = f(i["L1_attn_rel_bias"])
    m["onehotT"] = np.ascontiguousarray(_onehotT()[:, ::-1])
    m["WfcT"] = T(i["L1_ffn_W_fc"])
    m["bfc"] = f(i["L1_ffn_b_fc"]).reshape(32, 128).T.copy()
    m["bfcr"] = f(i["L1_ffn_b_fc"]).reshape(1, 4 * C)
    m["AfcT"] = T(i["L1_ffn_A_fc"])
    m["BfcT"] = T(i["L1_ffn_B_fc"])
    m["WprojT"] = T(i["L1_ffn_W_proj"])
    m["bproj"] = f(i["L1_ffn_b_proj"]).reshape(1, C)
    m["AprojT"] = T(i["L1_ffn_A_proj"])
    m["BprojT"] = T(i["L1_ffn_B_proj"])
    cos, ssw = _rope_tables()
    m["ropeC"] = cos
    m["ropeS"] = ssw
    m["idnbf"] = np.eye(128, dtype=BF)
    m["revbf"] = np.eye(128, dtype=BF)[::-1].copy()
    return m



class Ctr:
    def __init__(self, nc, name):
        self._guard = nc.semaphore(name)
        self.sem = self._guard.__enter__()
        self.n = 0

    def inc(self, inst, k=1):
        inst.then_inc(self.sem, k)
        self.n += k
        return inst

    def dma(self, inst):
        return self.inc(inst, 16)

    def wait(self, eng, n=None):
        eng.wait_ge(self.sem, self.n if n is None else n)

    def bump(self, eng):
        eng.sem_inc(self.sem, 1)
        self.n += 1


def ap2(t, off, pattern):
    a = t if isinstance(t, AP) else t.ap()
    return AP(tensor=a.tensor, offset=off, ap=pattern)


class K:

    def __init__(self, nc, dbg=()):
        self.nc = nc
        self.dbg = set(dbg)
        self.stack = []
        d = self.d = {}
        d["x_in"] = nc.dram_tensor("x_in", [N, C], F32, kind="ExternalInput")
        d["out"] = nc.dram_tensor("out", [N, C], F32, kind="ExternalOutput")
        for name, sh, dt in [
            ("c1_lhsT", [128, 32 * 5 * 128], F32), ("c2_lhsT", [128, 8 * 4 * 5 * 128], F32),
            ("c1_b", [128, 32], F32), ("c2_b", [128, 8], F32),
            ("ln0_g", [1, C], F32), ("ln0_b", [1, C], F32),
            ("Wq", [4, C, C], F32), ("Wk", [4, C, C], F32), ("Wv", [4, C, C], F32),
            ("bq", [4, C], F32), ("bk", [4, C], F32), ("bv", [4, C], F32),
            ("lnA_g", [4, C], F32), ("lnA_b", [4, C], F32),
            ("W1", [C, 4 * C], F32), ("b1", [128, 32], F32),
            ("W2", [4 * C, C], F32), ("b2", [1, C], F32),
            ("ln2_g", [1, C], F32), ("ln2_b", [1, C], F32),
            ("upT", [8, C, 2 * C], F32), ("gateT", [8, C, 2 * C], F32),
            ("downT", [8, 2 * C, C], F32), ("routerT", [C, 8], F32),
            ("wusT", [C, 2 * C], F32), ("wgsT", [C, 2 * C], F32), ("wdsT", [2 * C, C], F32),
            ("lnw1", [1, C], F32),
            ("wq1", [C, C], F32), ("wk1", [C, C], F32), ("wv1", [C, C], F32), ("wo1", [C, C], F32),
            ("relb", [32, 16], F32), ("onehotT", [32, 4095], F32),
            ("WfcT", [C, 4 * C], F32), ("bfc", [128, 32], F32), ("bfcr", [1, 4 * C], F32),
            ("AfcT", [C, 16], F32), ("BfcT", [16, 4 * C], F32),
            ("WprojT", [4 * C, C], F32), ("bproj", [1, C], F32),
            ("AprojT", [4 * C, 16], F32), ("BprojT", [16, C], F32),
            ("ropeC", [N, C], BF16), ("ropeS", [N, C], BF16),
            ("idnbf", [128, 128], BF16), ("revbf", [128, 128], BF16),
        ]:
            d[name] = nc.dram_tensor(name, sh, dt, kind="ExternalInput")
        d["sc_x"] = nc.dram_tensor("sc_x", [N, C], F32, kind="Internal")
        d["sc_tmp"] = nc.dram_tensor("sc_tmp", [N, C], F32, kind="Internal")
        d["sc_nf"] = nc.dram_tensor("sc_nf", [N, C], F32, kind="Internal")
        d["sc_nbf"] = nc.dram_tensor("sc_nbf", [N, C], BF16, kind="Internal")
        d["sc_lbf"] = nc.dram_tensor("sc_lbf", [N, C], BF16, kind="Internal")
        d["sc_bbf"] = nc.dram_tensor("sc_bbf", [N, 4 * C], BF16, kind="Internal")
        d["sc_misc"] = nc.dram_tensor("sc_misc", [64, 4096], F32, kind="Internal")
        d["sc_bias"] = nc.dram_tensor("sc_bias", [16, 4095], F32, kind="Internal")
        self.ld = Ctr(nc, "ld")
        self.lg = Ctr(nc, "lg")
        self.mm = Ctr(nc, "mm")
        self.ev = Ctr(nc, "ev")
        self.sv = Ctr(nc, "sv")
        self.st = Ctr(nc, "st")
        self.x2 = Ctr(nc, "x2")
        self.x3 = Ctr(nc, "x3")

    def dump(self, name, src, rows, cols, dt=F32):
        if name not in self.dbg:
            return
        src_dram = self.d[src]
        o = self.nc.dram_tensor("dbg_" + name, [rows, cols], dt, kind="ExternalOutput")
        st = self.st
        with self.nc.Block() as blk:
            def _(sync):
                st.dma(sync.dma_start(out=o.ap(), in_=src_dram.ap()[0:rows, 0:cols]))
                st.wait(sync)
            blk.sync(_)


def sbuf(nc, name, p, f, dt):
    g = nc.sbuf_tensor(name, [p, f], dt)
    t = g.__enter__()
    return g, t


def psum(nc, name, p, f, dt=F32):
    g = nc.psum_tensor(name, [p, f], dt)
    t = g.__enter__()
    return g, t



_PHASE_IDS = {"ln": 0, "add": 0}

def phase_ln(k, want_f32=False, eps=1e-5, want_lo=False):
    pid = _PHASE_IDS["ln"]; _PHASE_IDS["ln"] += 1
    nc = k.nc
    d = k.d
    gx, X = sbuf(nc, f"lnX{pid}", 128, NT * C, F32)
    gs, S = sbuf(nc, f"lnS{pid}", 128, 4 * NT, F32)
    gb, Xb = sbuf(nc, f"lnXb{pid}", 128, NT * C, BF16)
    gf, Xf = (None, None)
    gl, Xl = (None, None)
    if want_f32 or want_lo:
        want_f32 = True
        gf, Xf = sbuf(nc, f"lnXf{pid}", 128, NT * C, F32)
    if want_lo:
        gl, Xl = sbuf(nc, f"lnXl{pid}", 128, NT * C, BF16)
    ld, ev, sv, st = k.ld, k.ev, k.sv, k.st
    with nc.Block() as blk:
        def _(sync):
            for n in range(NT):
                ld.dma(sync.dma_start(out=X[:, n * C:(n + 1) * C],
                                      in_=d["sc_x"].ap()[n * 128:(n + 1) * 128, :]))
        blk.sync(_)
        def _(v):
            v.memset(S[:, 3 * NT:4 * NT], 0.0)
            ld.wait(v)
            for n in range(NT):
                xa = X[:, n * C:(n + 1) * C]
                inst = v.reduce_sum(S[:, n:n + 1], xa, axis=AX)
            ev.inc(inst); ev.wait(v)
            ev.inc(v.tensor_scalar_mul(S[:, 0:NT], S[:, 0:NT], 1.0 / C)); ev.wait(v)
            for n in range(NT):
                xa = X[:, n * C:(n + 1) * C]
                inst = v.scalar_tensor_tensor(Xb[:, n * C:(n + 1) * C], xa, S[:, n:n + 1], xa,
                                              OP.subtract, OP.mult,
                                              accum_out=S[:, 3 * NT + n:3 * NT + n + 1])
            ev.inc(inst); ev.wait(v)
            inst = v.tensor_scalar(S[:, NT:2 * NT], S[:, 3 * NT:4 * NT], 1.0 / C, eps,
                                   OP.mult, OP.add)
            ev.inc(inst)
        blk.vector(_)
        def _(s):
            ev.wait(s)
            sv.inc(s.activation(S[:, NT:2 * NT], S[:, NT:2 * NT], AF.Ln)); sv.wait(s)
            sv.inc(s.activation(S[:, NT:2 * NT], S[:, NT:2 * NT], AF.Exp, scale=-0.5))
        blk.scalar(_)
        def _(v):
            sv.wait(v)
            ev.inc(v.tensor_tensor(S[:, 2 * NT:3 * NT], S[:, 0:NT], S[:, NT:2 * NT], OP.mult))
            ev.wait(v)
            ev.inc(v.tensor_scalar_mul(S[:, 2 * NT:3 * NT], S[:, 2 * NT:3 * NT], -1.0))
        blk.vector(_)
        def _(s):
            ev.wait(s)
            for n in range(NT):
                xa = X[:, n * C:(n + 1) * C]
                inst = s.activation(Xb[:, n * C:(n + 1) * C], xa, AF.Identity,
                                    bias=S[:, 2 * NT + n:2 * NT + n + 1],
                                    scale=S[:, NT + n:NT + n + 1])
                if want_f32:
                    inst = s.activation(Xf[:, n * C:(n + 1) * C], xa, AF.Identity,
                                        bias=S[:, 2 * NT + n:2 * NT + n + 1],
                                        scale=S[:, NT + n:NT + n + 1])
            sv.inc(inst)
        blk.scalar(_)
        if want_lo:
            def _(v):
                sv.wait(v)
                for n in range(NT):
                    inst = v.tensor_sub(Xl[:, n * C:(n + 1) * C], Xf[:, n * C:(n + 1) * C],
                                        Xb[:, n * C:(n + 1) * C])
                ev.inc(inst)
            blk.vector(_)
        def _(sync):
            sv.wait(sync)
            if want_lo:
                ev.wait(sync)
            for n in range(NT):
                st.dma(sync.dma_start(out=d["sc_nbf"].ap()[n * 128:(n + 1) * 128, :],
                                      in_=Xb[:, n * C:(n + 1) * C]))
                if want_f32:
                    st.dma(sync.dma_start(out=d["sc_nf"].ap()[n * 128:(n + 1) * 128, :],
                                          in_=Xf[:, n * C:(n + 1) * C]))
                if want_lo:
                    st.dma(sync.dma_start(out=d["sc_lbf"].ap()[n * 128:(n + 1) * 128, :],
                                          in_=Xl[:, n * C:(n + 1) * C]))
            st.wait(sync)
        blk.sync(_)
    for g in (gl, gf, gb, gs, gx):
        if g is not None:
            g.__exit__(None, None, None)



def phase_add(k, final=False):
    pid = _PHASE_IDS["add"]; _PHASE_IDS["add"] += 1
    nc, d = k.nc, k.d
    ga, A = sbuf(nc, f"addA{pid}", 128, NT * C, F32)
    gb, Bt = sbuf(nc, f"addB{pid}", 128, NT * C, F32)
    ld, ev, st = k.ld, k.ev, k.st
    dst = d["out"] if final else d["sc_x"]
    with nc.Block() as blk:
        def _(sync):
            for n in range(NT):
                ld.dma(sync.dma_start(out=A[:, n * C:(n + 1) * C],
                                      in_=d["sc_x"].ap()[n * 128:(n + 1) * 128, :]))
                ld.dma(sync.dma_start(out=Bt[:, n * C:(n + 1) * C],
                                      in_=d["sc_tmp"].ap()[n * 128:(n + 1) * 128, :]))
        blk.sync(_)
        def _(v):
            ld.wait(v)
            for n in range(NT):
                inst = v.tensor_add(A[:, n * C:(n + 1) * C], A[:, n * C:(n + 1) * C],
                                    Bt[:, n * C:(n + 1) * C])
            ev.inc(inst)
        blk.vector(_)
        def _(sync):
            ev.wait(sync)
            for n in range(NT):
                st.dma(sync.dma_start(out=dst.ap()[n * 128:(n + 1) * 128, :],
                                      in_=A[:, n * C:(n + 1) * C]))
            st.wait(sync)
        blk.sync(_)
    gb.__exit__(None, None, None)
    ga.__exit__(None, None, None)



def ln_gb_resident(k, blk, src, dst, gbc, bbc, S, garb, eps, dst_bf=None, nt=NT):
    ev, sv = k.ev, k.sv

    def _(v):
        ev.wait(v)
        v.memset(S[:, 3 * nt:4 * nt], 0.0)
        for n in range(nt):
            inst = v.reduce_sum(S[:, n:n + 1], src(n), axis=AX)
        ev.inc(inst); ev.wait(v)
        ev.inc(v.tensor_scalar_mul(S[:, 0:nt], S[:, 0:nt], 1.0 / C)); ev.wait(v)
        for n in range(nt):
            inst = v.scalar_tensor_tensor(garb(n), src(n), S[:, n:n + 1], src(n),
                                          OP.subtract, OP.mult,
                                          accum_out=S[:, 3 * nt + n:3 * nt + n + 1])
        ev.inc(inst); ev.wait(v)
        ev.inc(v.tensor_scalar(S[:, nt:2 * nt], S[:, 3 * nt:4 * nt], 1.0 / C, eps,
                               OP.mult, OP.add))
    blk.vector(_)

    def _(s):
        ev.wait(s)
        sv.inc(s.activation(S[:, nt:2 * nt], S[:, nt:2 * nt], AF.Ln)); sv.wait(s)
        sv.inc(s.activation(S[:, nt:2 * nt], S[:, nt:2 * nt], AF.Exp, scale=-0.5))
    blk.scalar(_)

    def _(v):
        sv.wait(v)
        ev.inc(v.tensor_tensor(S[:, 2 * nt:3 * nt], S[:, 0:nt], S[:, nt:2 * nt], OP.mult))
        ev.wait(v)
        ev.inc(v.tensor_scalar_mul(S[:, 2 * nt:3 * nt], S[:, 2 * nt:3 * nt], -1.0))
    blk.vector(_)

    def _(s):
        ev.wait(s)
        for n in range(nt):
            inst = s.activation(dst(n), src(n), AF.Identity,
                                bias=S[:, 2 * nt + n:2 * nt + n + 1],
                                scale=S[:, nt + n:nt + n + 1])
        sv.inc(inst)
    blk.scalar(_)

    def _(v):
        sv.wait(v)
        inst = None
        if gbc is not None:
            for n in range(nt):
                inst = v.tensor_tensor(dst(n), dst(n), gbc, OP.mult)
            ev.inc(inst); ev.wait(v)
            for n in range(nt):
                inst = v.tensor_tensor(dst(n), dst(n), bbc, OP.add)
        if dst_bf is not None:
            ev.inc(inst); ev.wait(v)
            for n in range(nt):
                inst = v.tensor_copy(dst_bf(n), dst(n))
        if inst is None:
            inst = v.tensor_copy(S[:, 0:1], S[:, 0:1])
        ev.inc(inst)
    blk.vector(_)



FMS = 2080

def phase_a0_conv(k, H, only=None):
    nc, d = k.nc, k.d
    ld, lg, mm, ev, sv, st = k.ld, k.lg, k.mm, k.ev, k.sv, k.st
    gfm, fmA = sbuf(nc, "cfmA", 128, 8 * FMS, BF16)
    gc1, c1w = sbuf(nc, "c1w", 128, 32 * 5 * 128, BF16)
    gc2, c2w = sbuf(nc, "c2w", 128, 8 * 4 * 5 * 128, BF16)
    gb1, c1b = sbuf(nc, "c1b", 128, 32, F32)
    gb2, c2b = sbuf(nc, "c2b", 128, 8, F32)
    gh1, h1 = sbuf(nc, "h1", 128, 4 * FMS, BF16)
    gcf, cfm = sbuf(nc, "cfm", 128, 2 * 2048, BF16)
    gp = [psum(nc, f"cp{i}", 128, 512) for i in range(4)]
    banks = [t for _, t in gp]

    events = []
    for j2 in range(8):
        for j in range(4 * j2, 4 * j2 + 4):
            for p in range(4):
                events.append(("c1", j, p))
        for p in range(4):
            events.append(("c2", j2, p))

    x3 = k.x3
    mm0, sv0, st0, ld0, lg0, ev0, x30 = mm.n, sv.n, st.n, ld.n, lg.n, ev.n, x3.n
    if only == "ln0":
        events = []
    with nc.Block() as blk:
        def _(v):
            v.memset(fmA[:, :], 0.0)
            ev.inc(v.memset(h1[:, :], 0.0))
        blk.vector(_)

        def _(g):
            if only == "ln0":
                return
            lg.dma(g.dma_start(out=c1w[:, :], in_=d["c1_lhsT"].ap()))
            lg.dma(g.dma_start(out=c2w[:, :], in_=d["c2_lhsT"].ap()))
            lg.dma(g.dma_start(out=c1b[:, :], in_=d["c1_b"].ap()))
            lg.dma(g.dma_start(out=c2b[:, :], in_=d["c2_b"].ap()))
        blk.gpsimd(_)

        def _(sync):
            if only == "ln0":
                return
            ev.wait(sync)
            for j in range(8):
                ld.dma(sync.dma_start(
                    out=fmA[:, j * FMS + 16:j * FMS + 16 + 2048],
                    in_=d["sc_nbf"].ap()[:, j * 128:(j + 1) * 128], transpose=True))
        blk.sync(_)

        def _(t):
            if only == "ln0":
                return
            t.wait_ge(ld.sem, ld0 + 8 * 16)
            t.wait_ge(lg.sem, lg0 + 4 * 16)
            for e, evnt in enumerate(events):
                bank = banks[e % 4]
                if e >= 4:
                    t.wait_ge(sv.sem, sv0 + e - 3)
                kind = evnt[0]
                if kind == "c2":
                    t.wait_ge(sv.sem, sv0 + evnt[1] * 20 + 16)
                if kind == "c1":
                    _, j, p = evnt
                    for ti in range(5):
                        inst = t.matmul(bank[:, :],
                                        c1w[:, (j * 5 + ti) * 128:(j * 5 + ti + 1) * 128],
                                        fmA[:, (j // 4) * FMS + 14 + p * 512 + ti:
                                            (j // 4) * FMS + 14 + p * 512 + ti + 512],
                                        start=(ti == 0), stop=(ti == 4))
                else:
                    _, j2, p = evnt
                    first = True
                    for icc in range(4):
                        for ti in range(5):
                            inst = t.matmul(bank[:, :],
                                            c2w[:, ((j2 * 4 + icc) * 5 + ti) * 128:
                                                ((j2 * 4 + icc) * 5 + ti + 1) * 128],
                                            h1[:, icc * FMS + 14 + p * 512 + ti:
                                               icc * FMS + 14 + p * 512 + ti + 512],
                                            start=first, stop=(icc == 3 and ti == 4))
                            first = False
                mm.inc(inst)
        blk.tensor(_)

        def _(s):
            if only == "ln0":
                return
            for e, evnt in enumerate(events):
                bank = banks[e % 4]
                s.wait_ge(mm.sem, mm0 + e + 1)
                kind = evnt[0]
                if kind == "c1":
                    _, j, p = evnt
                    inst = s.activation(h1[:, (j % 4) * FMS + 16 + p * 512:
                                           (j % 4) * FMS + 16 + p * 512 + 512],
                                        bank[:, :], AF.Gelu, bias=c1b[:, j:j + 1])
                else:
                    _, j2, p = evnt
                    if j2 >= 2 and p == 0:
                        s.wait_ge(x3.sem, x30 + j2 - 1)
                    inst = s.activation(cfm[:, (j2 % 2) * 2048 + p * 512:
                                            (j2 % 2) * 2048 + p * 512 + 512],
                                        bank[:, :], AF.Identity, bias=c2b[:, j2:j2 + 1])
                sv.inc(inst)
        blk.scalar(_)

        def _(sync):
            if only == "ln0":
                return
            for j2 in range(8):
                sync.wait_ge(sv.sem, sv0 + j2 * 20 + 20)
                st.dma(sync.dma_start(out=d["sc_bbf"].ap()[128 * j2:128 * (j2 + 1), 0:2048],
                                      in_=cfm[:, (j2 % 2) * 2048:(j2 % 2) * 2048 + 2048]))
                st.wait(sync)
                x3.bump(sync)
        blk.sync(_)

    for g, _t in reversed(gp):
        g.__exit__(None, None, None)
    for g in (gcf, gh1, gb2, gb1, gc2, gc1, gfm):
        g.__exit__(None, None, None)

    if only == "conv":
        for g in (gcf, gh1, gb2, gb1, gc2, gc1, gfm):
            pass
    if only == "conv":
        return
    gct, CT = sbuf(nc, "l0ct", 128, NT * C, BF16)
    gnf, NF = sbuf(nc, "l0nf", 128, NT * C, F32)
    gS, S = sbuf(nc, "l0S", 128, 4 * NT, F32)
    gg, GB = sbuf(nc, "l0gb", 128, 2 * C, F32)
    with nc.Block() as blk:
        def _(sync):
            for n in range(NT):
                ld.dma(sync.dma_start(out=CT[:, n * C:(n + 1) * C],
                                      in_=d["sc_bbf"].ap()[0:1024, n * 128:(n + 1) * 128],
                                      transpose=True))
        blk.sync(_)

        def _(g):
            for n in range(NT):
                lg.dma(g.dma_start(out=NF[:, n * C:(n + 1) * C],
                                   in_=d["sc_nf"].ap()[n * 128:(n + 1) * 128, :]))
            lg.dma(g.dma_start(out=GB[:, 0:C], in_=ap2(d["ln0_g"], 0, [[0, 128], [1, C]])))
            lg.dma(g.dma_start(out=GB[:, C:2 * C], in_=ap2(d["ln0_b"], 0, [[0, 128], [1, C]])))
        blk.gpsimd(_)

        def _(v):
            ld.wait(v)
            lg.wait(v)
            for n in range(NT):
                inst = v.tensor_add(NF[:, n * C:(n + 1) * C], NF[:, n * C:(n + 1) * C],
                                    CT[:, n * C:(n + 1) * C])
            ev.inc(inst); ev.wait(v)
        blk.vector(_)

        ln_gb_resident(k, blk,
                       src=lambda n: NF[:, n * C:(n + 1) * C],
                       dst=lambda n: H[:, n * C:(n + 1) * C],
                       gbc=GB[:, 0:C], bbc=GB[:, C:2 * C],
                       S=S, garb=lambda n: CT[:, n * C:(n + 1) * C], eps=1e-12)
    for g in (gg, gS, gnf, gct):
        g.__exit__(None, None, None)


def phase_a0_layer(k, H, li, blocks=5, skip14=False):
    nc, d = k.nc, k.d
    ld, lg, mm, ev, sv, st, x2 = k.ld, k.lg, k.mm, k.ev, k.sv, k.st, k.x2

    gHT, HT = sbuf(nc, f"a{li}HT", 128, 8 * N, BF16)
    gKV, KV = sbuf(nc, f"a{li}KV", 128, C, BF16)
    gKM, KMB = sbuf(nc, f"a{li}KMB", 128, C, F32)
    gW3, W3 = sbuf(nc, f"a{li}W3", 128, 3 * 8 * C, BF16)
    gBS, BS = sbuf(nc, f"a{li}BS", 1, 3 * C + 128, BF16)
    gOC, OC_ = sbuf(nc, f"a{li}OC", 128, 1, BF16)
    gID, IDN = sbuf(nc, f"a{li}IDN", 128, 128, BF16)
    gQS, QS = sbuf(nc, f"a{li}QS", 128, 2 * C, BF16)
    gKS, KS = sbuf(nc, f"a{li}KS", 128, C, BF16)
    gVS, VS = sbuf(nc, f"a{li}VS", 128, C, BF16)
    gR1, R1 = sbuf(nc, f"a{li}R1", 128, C, BF16)
    gR2, R2 = sbuf(nc, f"a{li}R2", 128, C, BF16)
    gQR, QRB = sbuf(nc, f"a{li}QR", 128, C, BF16)
    gRT, RT = sbuf(nc, f"a{li}RT", 128, 2 * 2 * C, BF16)
    gHB, HBS = sbuf(nc, f"a{li}HBS", 128, 2 * C, BF16)

    pQg = psum(nc, f"a{li}pQ", 128, 1024)
    pTg = [psum(nc, f"a{li}pT{i}", 128, 128, BF16) for i in range(2)]
    pKVg = psum(nc, f"a{li}pKV", 64, 1024)
    pKSg = psum(nc, f"a{li}pKS", 1, 1024)
    pQ = pQg[1]
    pT = [t for _, t in pTg]
    pKV = pKVg[1]
    pKS = pKSg[1]

    def Wsl(wi, c, lo, hi):
        base = (wi * 8 + c) * C
        return W3[:, base + lo:base + hi]


    if not skip14:
        x3 = k.x3
        ev0, st0, x30 = ev.n, st.n, x3.n
        with nc.Block() as blk:
            def _(v):
                for n in range(NT):
                    sl = n % 2
                    if n >= 2:
                        v.wait_ge(x3.sem, x30 + n - 1)
                    ev.inc(v.tensor_copy(HBS[:, sl * C:(sl + 1) * C], H[:, n * C:(n + 1) * C]))
                v.memset(BS[:, 3 * C:3 * C + 128], 1.0)
                v.memset(OC_[:, :], 1.0)
            blk.vector(_)

            def _(sync):
                for n in range(NT):
                    sl = n % 2
                    sync.wait_ge(ev.sem, ev0 + n + 1)
                    st.dma(sync.dma_start(out=d["sc_bbf"].ap()[n * 128:(n + 1) * 128, 0:C],
                                          in_=HBS[:, sl * C:(sl + 1) * C]))
                    st.wait(sync)
                    x3.bump(sync)
            blk.sync(_)

            def _(g):
                for wi, wname in enumerate(("Wq", "Wk", "Wv")):
                    src = d[wname].ap()[li]
                    lg.dma(g.dma_start(
                        out=W3[:, wi * 8 * C:(wi + 1) * 8 * C],
                        in_=ap2(src, src.offset, [[1024, 128], [128 * 1024, 8], [1, 1024]])))
                for bi, bname in enumerate(("bq", "bk", "bv")):
                    lg.dma(g.dma_start(out=BS[:, bi * C:(bi + 1) * C],
                                       in_=d[bname].ap()[li:li + 1, :]))
                lg.dma(g.dma_start(out=IDN[:, :], in_=d["idnbf"].ap()))
                lg.wait(g)
            blk.gpsimd(_)

        with nc.Block() as blk:
            def _(sync):
                for c in range(8):
                    ld.dma(sync.dma_start(out=HT[:, c * N:(c + 1) * N],
                                          in_=d["sc_bbf"].ap()[:, c * 128:(c + 1) * 128],
                                          transpose=True))
                ld.wait(sync)
            blk.sync(_)

        mm0, sv0, ev0, st0, ld0, x20, x30 = mm.n, sv.n, ev.n, st.n, ld.n, x2.n, x3.n
        with nc.Block() as blk:
            def _(sync):
                for t in range(NT):
                    sl = t % 2
                    if t >= 2:
                        sync.wait_ge(ev.sem, ev0 + 6 * (t - 2) + 6)
                    ld.dma(sync.dma_start(out=RT[:, sl * 2 * C:sl * 2 * C + C],
                                          in_=d["ropeC"].ap()[t * 128:(t + 1) * 128, :]))
                    ld.dma(sync.dma_start(out=RT[:, sl * 2 * C + C:sl * 2 * C + 2 * C],
                                          in_=d["ropeS"].ap()[t * 128:(t + 1) * 128, :]))
                    sync.wait_ge(ev.sem, ev0 + 6 * t + 1)
                    st.dma(sync.dma_start(out=d["sc_nbf"].ap()[t * 128:(t + 1) * 128, :],
                                          in_=QS[:, sl * C:(sl + 1) * C]))
                    st.wait(sync)
                    x3.bump(sync)
            blk.sync(_)

            def _(t_):
                for t in range(NT):
                    if t > 0:
                        t_.wait_ge(sv.sem, sv0 + 3 * t)
                    for pg in range(2):
                        for c in range(8):
                            t_.matmul(pQ[:, pg * 512:(pg + 1) * 512],
                                      HT[:, c * N + t * 128:c * N + (t + 1) * 128],
                                      Wsl(0, c, pg * 512, (pg + 1) * 512),
                                      start=(c == 0), stop=False)
                        inst = t_.matmul(pQ[:, pg * 512:(pg + 1) * 512],
                                         BS[0:1, 3 * C:3 * C + 128],
                                         BS[0:1, 0 * C + pg * 512:0 * C + (pg + 1) * 512],
                                         start=False, stop=True)
                    mm.inc(inst)
                    t_.wait_ge(sv.sem, sv0 + 3 * t + 1)
                    for pg in range(2):
                        for c in range(8):
                            t_.matmul(pQ[:, pg * 512:(pg + 1) * 512],
                                      HT[:, c * N + t * 128:c * N + (t + 1) * 128],
                                      Wsl(1, c, pg * 512, (pg + 1) * 512),
                                      start=(c == 0), stop=False)
                        inst = t_.matmul(pQ[:, pg * 512:(pg + 1) * 512],
                                         BS[0:1, 3 * C:3 * C + 128],
                                         BS[0:1, 1 * C + pg * 512:1 * C + (pg + 1) * 512],
                                         start=False, stop=True)
                    mm.inc(inst)
                    t_.wait_ge(sv.sem, sv0 + 3 * t + 2)
                    for pg in range(2):
                        for c in range(8):
                            t_.matmul(pQ[:, pg * 512:(pg + 1) * 512],
                                      HT[:, c * N + t * 128:c * N + (t + 1) * 128],
                                      Wsl(2, c, pg * 512, (pg + 1) * 512),
                                      start=(c == 0), stop=False)
                        inst = t_.matmul(pQ[:, pg * 512:(pg + 1) * 512],
                                         BS[0:1, 3 * C:3 * C + 128],
                                         BS[0:1, 2 * C + pg * 512:2 * C + (pg + 1) * 512],
                                         start=False, stop=True)
                    mm.inc(inst)
                    t_.wait_ge(ev.sem, ev0 + 6 * t + 3)
                    for c in range(8):
                        if 8 * t + c - 1 > 0:
                            t_.wait_ge(x2.sem, x20 + 8 * t + c - 1)
                        mm.inc(t_.transpose(pT[c % 2][:, :], QRB[:, c * 128:(c + 1) * 128],
                                            IDN[:, :]))
                    t_.wait_ge(ev.sem, ev0 + 6 * t + 6)
                    t_.wait_ge(sv.sem, sv0 + 3 * t + 3)
                    for h in range(16):
                        t_.matmul(pKV[0:64, h * 64:(h + 1) * 64],
                                  R1[:, h * 64:(h + 1) * 64],
                                  VS[:, h * 64:(h + 1) * 64],
                                  start=(t == 0 and h in (0, 8)), stop=(t == NT - 1),
                                  skip_group_check=True)
                        inst = t_.matmul(pKS[0:1, h * 64:(h + 1) * 64],
                                         OC_[:, 0:1],
                                         KS[:, h * 64:(h + 1) * 64],
                                         start=(t == 0 and h in (0, 8)), stop=(t == NT - 1),
                                         skip_group_check=True)
                    mm.inc(inst)
            blk.tensor(_)

            def _(s):
                for t in range(NT):
                    sl = t % 2
                    s.wait_ge(mm.sem, mm0 + 12 * t + 1)
                    if t >= 2:
                        s.wait_ge(x3.sem, x30 + t - 1)
                    sv.inc(s.activation(QS[:, sl * C:(sl + 1) * C], pQ[:, :], AF.Gelu))
                    s.wait_ge(mm.sem, mm0 + 12 * t + 2)
                    sv.inc(s.activation(KS[:, :], pQ[:, :], AF.Gelu))
                    s.wait_ge(mm.sem, mm0 + 12 * t + 3)
                    sv.inc(s.activation(VS[:, :], pQ[:, :], AF.Identity))
            blk.scalar(_)

            def _(v):
                for t in range(NT):
                    sl = t % 2
                    qs = QS[:, sl * C:(sl + 1) * C]
                    cosT = RT[:, sl * 2 * C:sl * 2 * C + C]
                    sswT = RT[:, sl * 2 * C + C:sl * 2 * C + 2 * C]
                    v.wait_ge(ld.sem, ld0 + 32 * t + 32)
                    v.wait_ge(sv.sem, sv0 + 3 * t + 1)
                    ev.inc(v.tensor_scalar_add(qs, qs, 0.21)); v.wait_ge(ev.sem, ev0 + 6 * t + 1)
                    v.tensor_tensor(R1[:, :], qs, cosT, OP.mult)
                    ev.inc(v.tensor_tensor(R2[:, :], qs, sswT, OP.mult))
                    v.wait_ge(ev.sem, ev0 + 6 * t + 2)
                    sw = AP(tensor=R2[:, :].tensor, offset=R2[:, :].offset + 1,
                            ap=list(R2[:, :].ap[:-1]) + [[2, C // 2], [-1, 2]])
                    ev.inc(v.tensor_tensor(QRB[:, :].rearrange("p (a b) -> p a b", b=2),
                                           R1[:, :].rearrange("p (a b) -> p a b", b=2),
                                           sw, OP.add))
                    v.wait_ge(sv.sem, sv0 + 3 * t + 2)
                    ev.inc(v.tensor_scalar_add(KS[:, :], KS[:, :], 0.21))
                    for c in range(8):
                        v.wait_ge(mm.sem, mm0 + 12 * t + 4 + c)
                        x2.inc(v.tensor_copy(HT[:, c * N + t * 128:c * N + (t + 1) * 128],
                                             pT[c % 2][:, :]))
                    v.wait_ge(ev.sem, ev0 + 6 * t + 4)
                    v.tensor_tensor(R2[:, :], KS[:, :], sswT, OP.mult)
                    ev.inc(v.tensor_tensor(QRB[:, :], KS[:, :], cosT, OP.mult))
                    v.wait_ge(ev.sem, ev0 + 6 * t + 5)
                    sw2 = AP(tensor=R2[:, :].tensor, offset=R2[:, :].offset + 1,
                             ap=list(R2[:, :].ap[:-1]) + [[2, C // 2], [-1, 2]])
                    ev.inc(v.tensor_tensor(R1[:, :].rearrange("p (a b) -> p a b", b=2),
                                           QRB[:, :].rearrange("p (a b) -> p a b", b=2),
                                           sw2, OP.add))
            blk.vector(_)

        if blocks < 4 or blocks == 4:
            pass
        if blocks < 4:
            pKSg[0].__exit__(None, None, None)
            pKVg[0].__exit__(None, None, None)
            for g, _t in reversed(pTg):
                g.__exit__(None, None, None)
            pQg[0].__exit__(None, None, None)
            for g in (gHB, gRT, gQR, gR2, gR1, gVS, gKS, gQS, gID, gOC, gBS, gW3):
                g.__exit__(None, None, None)
            for g in (gKM, gKV, gHT):
                g.__exit__(None, None, None)
            return
        with nc.Block() as blk:
            def _(v):
                ev.inc(v.memset(KV[:, :], 0.0))
            blk.vector(_)
            def _(s):
                mm.wait(s)
                ev.wait(s)
                for h in range(16):
                    sv.inc(s.activation(KV[64 * (h % 2):64 * (h % 2) + 64,
                                           h * 64:h * 64 + 64],
                                        pKV[0:64, h * 64:(h + 1) * 64],
                                        AF.Identity, scale=1.0 / N))
                sv.inc(s.activation(KMB[0:1, 0:C], pKS[0:1, :], AF.Identity, scale=1.0 / N))
            blk.scalar(_)
            def _(sync):
                sv.wait(sync)
                st.dma(sync.dma_start(out=d["sc_misc"].ap()[0:1, 0:C], in_=KMB[0:1, 0:C]))
                st.wait(sync)
            blk.sync(_)
            def _(g):
                st.wait(g)
                src = d["sc_misc"].ap()
                lg.dma(g.dma_start(out=KMB[:, :], in_=ap2(src, 0, [[0, 128], [1, C]])))
                lg.wait(g)
            blk.gpsimd(_)

        pKSg[0].__exit__(None, None, None)
        pKVg[0].__exit__(None, None, None)
        for g, _t in reversed(pTg):
            g.__exit__(None, None, None)
        pQg[0].__exit__(None, None, None)
        for g in (gHB, gRT, gQR, gR2, gR1, gVS, gKS, gQS, gID, gOC, gBS, gW3):
            g.__exit__(None, None, None)
        if blocks < 5:
            for g in (gKM, gKV, gHT):
                g.__exit__(None, None, None)
            return

    if skip14:
        pKSg[0].__exit__(None, None, None)
        pKVg[0].__exit__(None, None, None)
        for g_ in reversed(pTg):
            g_[0].__exit__(None, None, None)
        pQg[0].__exit__(None, None, None)
        for g_ in (gHB, gRT, gQR, gR2, gR1, gVS, gKS, gQS, gID, gOC, gBS, gW3):
            g_.__exit__(None, None, None)
    gOF, OF = sbuf(nc, f"a{li}OF", 128, NT * C, F32)
    gZT, ZT = sbuf(nc, f"a{li}ZT", 128, C, F32)
    gZC, ZC = sbuf(nc, f"a{li}ZC", 128, 32, F32)
    gS2, S2 = sbuf(nc, f"a{li}S2", 128, 4 * NT, F32)
    gGB2, GB2 = sbuf(nc, f"a{li}GB2", 128, 2 * C, F32)
    gQ2, Q2 = sbuf(nc, f"a{li}Q2", 128, 2 * C, BF16)
    pOg = psum(nc, f"a{li}pO", 128, 1024)
    pO = pOg[1]
    mm0, sv0, ev0, st0, lg0, x20b = mm.n, sv.n, ev.n, st.n, lg.n, x2.n
    with nc.Block() as blk:
        if blocks == 41:
            def _(sync):
                ld.dma(sync.dma_start(out=Q2[:, 0:C],
                                      in_=d["sc_nbf"].ap()[0:128, :]))
                ld.wait(sync)
                x2.bump(sync)
            blk.sync(_)
            def _(v):
                v.wait_ge(x2.sem, x2.n)
                ev.inc(v.tensor_tensor(ZT[:, :], Q2[:, 0:C], KMB[:, :], OP.mult))
            blk.vector(_)
            def _(sync):
                ev.wait(sync)
                st.dma(sync.dma_start(out=d["sc_tmp"].ap()[0:128, :], in_=ZT[:, :]))
                st.dma(sync.dma_start(out=d["sc_tmp"].ap()[128:256, :], in_=KMB[:, :]))
                st.wait(sync)
            blk.sync(_)
            def _(g):
                st.wait(g)
                lg.dma(g.dma_start(out=d["sc_tmp"].ap()[256:384, :], in_=Q2[:, 0:C]))
                lg.wait(g)
            blk.gpsimd(_)
        if blocks == 41:
            pass
        else:
          _dummy41 = None
        def _(sync):
            if blocks == 41:
                return
            for t in range(NT):
                sl = t % 2
                if t >= 2:
                    sync.wait_ge(ev.sem, ev0 + (t - 1) if blocks == 42 else ev0 + 6 * (t - 2) + 1)
                ld.dma(sync.dma_start(out=Q2[:, sl * C:(sl + 1) * C],
                                      in_=d["sc_nbf"].ap()[t * 128:(t + 1) * 128, :]))
                ld.wait(sync)
                x2.bump(sync)
            if blocks == 5:
                ga = d["lnA_g"].ap()[li:li + 1, :]
                ba = d["lnA_b"].ap()[li:li + 1, :]
                ld.dma(sync.dma_start(out=GB2[:, 0:C], in_=ap2(ga, ga.offset, [[0, 128], [1, C]])))
                ld.dma(sync.dma_start(out=GB2[:, C:2 * C], in_=ap2(ba, ba.offset, [[0, 128], [1, C]])))
                ld.wait(sync)
        blk.sync(_)

        def _(v):
            if blocks == 41:
                return
            for t in range(NT):
                sl = t % 2
                v.wait_ge(x2.sem, x20b + t + 1)
                if blocks == 42:
                    ev.inc(v.tensor_copy(OF[:, t * C:(t + 1) * C], Q2[:, sl * C:(sl + 1) * C]))
                    continue
                ev.inc(v.tensor_tensor(ZT[:, :], Q2[:, sl * C:(sl + 1) * C], KMB[:, :], OP.mult))
                v.wait_ge(ev.sem, ev0 + 6 * t + 1)
                ev.inc(v.tensor_reduce(ZC[:, 0:16].rearrange("p (h u) -> p h u", u=1),
                                       ZT[:, :].rearrange("p (h e) -> p h e", e=64),
                                       AX, OP.add))
                v.wait_ge(ev.sem, ev0 + 6 * t + 2)
                ev.inc(v.tensor_scalar_add(ZC[:, 0:16], ZC[:, 0:16], 1e-6))
                v.wait_ge(ev.sem, ev0 + 6 * t + 3)
                ev.inc(v.reciprocal(ZC[:, 16:32], ZC[:, 0:16]))
                if blocks in (43, 44):
                    ev.inc(v.tensor_copy(ZC[:, 0:1], ZC[:, 0:1]))
                    v.wait_ge(ev.sem, ev0 + 6 * t + 5)
                    ev.inc(v.tensor_copy(ZC[:, 1:2], ZC[:, 1:2]))
                    continue
                v.wait_ge(mm.sem, mm0 + t + 1)
                for h in range(16):
                    inst = v.tensor_scalar(OF[:, t * C + h * 64:t * C + (h + 1) * 64],
                                           pO[:, h * 64:(h + 1) * 64],
                                           ZC[:, 16 + h:17 + h], 0.0, OP.mult, OP.add)
                ev.inc(inst)
                v.wait_ge(ev.sem, ev0 + 6 * t + 5)
                ev.inc(v.tensor_add(OF[:, t * C:(t + 1) * C], OF[:, t * C:(t + 1) * C],
                                    H[:, t * C:(t + 1) * C]))
        blk.vector(_)

        def _(t_):
            if blocks in (41, 42, 43):
                return
            for t in range(NT):
                t_.wait_ge(ev.sem, ev0 + 6 * t + 4)
                if t > 0:
                    t_.wait_ge(ev.sem, ev0 + 6 * (t - 1) + 5)
                for h in range(16):
                    inst = t_.matmul(pO[:, h * 64:(h + 1) * 64],
                                     HT[:, (h // 2) * N + t * 128:(h // 2) * N + (t + 1) * 128],
                                     KV[:, h * 64:(h + 1) * 64],
                                     start=(h in (0, 8)), stop=(h in (7, 15)),
                                     skip_group_check=True)
                mm.inc(inst)
        blk.tensor(_)

        if blocks == 5:
            def _(v):
                ld.wait(v)
            blk.vector(_)
            ln_gb_resident(k, blk,
                           src=lambda n: OF[:, n * C:(n + 1) * C],
                           dst=lambda n: H[:, n * C:(n + 1) * C],
                           gbc=GB2[:, 0:C], bbc=GB2[:, C:2 * C],
                           S=S2, garb=lambda n: H[:, n * C:(n + 1) * C], eps=1e-12)
        elif blocks == 45:
            def _(sync):
                sync.wait_ge(ev.sem, ev.n)
                for n in range(NT):
                    st.dma(sync.dma_start(out=d["sc_tmp"].ap()[n * 128:(n + 1) * 128, :],
                                          in_=OF[:, n * C:(n + 1) * C]))
                st.wait(sync)
            blk.sync(_)
        elif blocks == 41:
            pass
        elif blocks == 42:
            def _(sync):
                sync.wait_ge(ev.sem, ev.n)
                for t in range(NT):
                    st.dma(sync.dma_start(out=d["sc_tmp"].ap()[t * 128:(t + 1) * 128, :],
                                          in_=OF[:, t * C:(t + 1) * C]))
                st.wait(sync)
            blk.sync(_)
        else:
            def _(sync):
                sync.wait_ge(ev.sem, ev.n)
                st.dma(sync.dma_start(out=d["sc_misc"].ap()[2:3, 0:32], in_=ZC[:, 0:32][0:1, :]))
                st.dma(sync.dma_start(out=d["sc_tmp"].ap()[0:128, :], in_=ZT[:, :]))
                st.dma(sync.dma_start(out=d["sc_tmp"].ap()[128:256, :], in_=KMB[:, :]))
                st.wait(sync)
            blk.sync(_)
    pOg[0].__exit__(None, None, None)
    for g in (gQ2, gGB2, gS2, gZC, gZT, gOF):
        g.__exit__(None, None, None)
    for g in (gKM, gKV, gHT):
        g.__exit__(None, None, None)



def phase_a0_mlp(k, H):
    nc, d = k.nc, k.d
    ld, lg, mm, ev, sv, st, x2, x3 = k.ld, k.lg, k.mm, k.ev, k.sv, k.st, k.x2, k.x3

    gHT, HT = sbuf(nc, "mHT", 128, 8 * N, BF16)
    gGB, GB = sbuf(nc, "mGB", 128, 2 * C, F32)
    gB1, B1 = sbuf(nc, "mB1", 128, 32, F32)
    gB2, B2 = sbuf(nc, "mB2", 1, C + 128, BF16)
    gMID, MID = sbuf(nc, "mMID", 128, 16 * 1024, BF16)
    gOFh, OFh = sbuf(nc, "mOF", 128, 8 * C, F32)
    gW1, W1s = sbuf(nc, "mW1", 128, 2 * 8 * 128, BF16)
    gW2, W2g = sbuf(nc, "mW2", 128, 8 * 1024, BF16)
    gS3, S3 = sbuf(nc, "mS3", 128, 4 * 8, F32)
    gHB, HBS = sbuf(nc, "mHBS", 128, 2 * C, BF16)
    pMg = [psum(nc, f"mpM{i}", 128, 512) for i in range(2)]
    pOg = [psum(nc, f"mpO{i}", 128, 1024) for i in range(2)]
    pM = [t for _, t in pMg]
    pOC = [t for _, t in pOg]

    x30 = x3.n
    ev0, st0 = ev.n, st.n
    with nc.Block() as blk:
        def _(v):
            for n in range(NT):
                sl = n % 2
                if n >= 2:
                    v.wait_ge(x3.sem, x30 + n - 1)
                ev.inc(v.tensor_copy(HBS[:, sl * C:(sl + 1) * C], H[:, n * C:(n + 1) * C]))
        blk.vector(_)
        def _(sync):
            for n in range(NT):
                sl = n % 2
                sync.wait_ge(ev.sem, ev0 + n + 1)
                st.dma(sync.dma_start(out=d["sc_bbf"].ap()[n * 128:(n + 1) * 128, 0:C],
                                      in_=HBS[:, sl * C:(sl + 1) * C]))
                st.wait(sync)
                x3.bump(sync)
            for c in range(8):
                ld.dma(sync.dma_start(out=HT[:, c * N:(c + 1) * N],
                                      in_=d["sc_bbf"].ap()[:, c * 128:(c + 1) * 128],
                                      transpose=True))
            ga = d["ln2_g"].ap()
            ba = d["ln2_b"].ap()
            ld.dma(sync.dma_start(out=GB[:, 0:C], in_=ap2(ga, 0, [[0, 128], [1, C]])))
            ld.dma(sync.dma_start(out=GB[:, C:2 * C], in_=ap2(ba, 0, [[0, 128], [1, C]])))
            ld.wait(sync)
        blk.sync(_)
        def _(g):
            lg.dma(g.dma_start(out=B1[:, :], in_=d["b1"].ap()))
            lg.dma(g.dma_start(out=B2[0:1, 0:C], in_=d["b2"].ap()[0:1, :]))
            lg.wait(g)
        blk.gpsimd(_)
        def _(v):
            ev.inc(v.memset(B2[0:1, C:C + 128], 1.0))
        blk.vector(_)

    W1ap = d["W1"].ap()
    W2ap = d["W2"].ap()
    for half in range(2):
        hb = half * 1024
        mm0, sv0, ev0, x20, x30 = mm.n, sv.n, ev.n, x2.n, x3.n

        def mid_pos(j):
            jg = j // 8
            return jg * 24 + 2 * (j % 8) + 2

        with nc.Block() as blk:
            def _(g):
                for jg in range(4):
                    for jj in range(8):
                        j = jg * 8 + jj
                        if j >= 2:
                            g.wait_ge(mm.sem, mm0 + mid_pos(j - 2))
                        lg.dma(g.dma_start(
                            out=W1s[:, (j % 2) * 1024:(j % 2) * 1024 + 1024],
                            in_=ap2(W1ap, 128 * j, [[4096, 128], [128 * 4096, 8], [1, 128]])))
                        lg.wait(g)
                        x3.bump(g)
                    if jg >= 1:
                        g.wait_ge(mm.sem, mm0 + jg * 24)
                    lg.dma(g.dma_start(
                        out=W2g[:, :],
                        in_=ap2(W2ap, jg * 1024 * 1024, [[1024, 128], [131072, 8], [1, 1024]])))
                    lg.wait(g)
                    x2.bump(g)
            blk.gpsimd(_)

            def _(t_):
                for jg in range(4):
                    for jj in range(8):
                        j = jg * 8 + jj
                        t_.wait_ge(x3.sem, x30 + j + 1)
                        for pg in range(2):
                            e = 2 * j + pg
                            if e >= 2:
                                t_.wait_ge(sv.sem, sv0 + e - 1)
                            for c in range(8):
                                inst = t_.matmul(
                                    pM[e % 2][:, :],
                                    W1s[:, (j % 2) * 1024 + c * 128:(j % 2) * 1024 + (c + 1) * 128],
                                    HT[:, c * N + hb + pg * 512:c * N + hb + (pg + 1) * 512],
                                    start=(c == 0), stop=(c == 7))
                                if c == 7:
                                    mm.inc(inst)
                    t_.wait_ge(x2.sem, x20 + jg + 1)
                    t_.wait_ge(sv.sem, sv0 + 16 * (jg + 1))
                    for nt in range(8):
                        if jg > 0 or nt >= 2:
                            prev = jg * 8 + nt - 2
                            t_.wait_ge(ev.sem, ev0 + 1 + prev + 1)
                        for cop in range(2):
                            for jj in range(8):
                                inst = t_.matmul(
                                    pOC[nt % 2][:, cop * 512:(cop + 1) * 512],
                                    MID[:, (jg % 2) * 8192 + jj * 1024 + nt * 128:
                                        (jg % 2) * 8192 + jj * 1024 + nt * 128 + 128],
                                    W2g[:, jj * 1024 + cop * 512:jj * 1024 + (cop + 1) * 512],
                                    start=(jj == 0), stop=(jg != 0 and jj == 7))
                            if jg == 0:
                                inst = t_.matmul(pOC[nt % 2][:, cop * 512:(cop + 1) * 512],
                                                 B2[0:1, C:C + 128],
                                                 B2[0:1, cop * 512:(cop + 1) * 512],
                                                 start=False, stop=True)
                        mm.inc(inst)
            blk.tensor(_)

            def _(s):
                for jg in range(4):
                    for jj in range(8):
                        j = jg * 8 + jj
                        for pg in range(2):
                            e = 2 * j + pg
                            s.wait_ge(mm.sem, mm0 + jg * 24 + 2 * jj + pg + 1)
                            sv.inc(s.activation(
                                MID[:, (jg % 2) * 8192 + jj * 1024 + pg * 512:
                                    (jg % 2) * 8192 + jj * 1024 + (pg + 1) * 512],
                                pM[e % 2][:, :], AF.Gelu, bias=B1[:, j:j + 1]))
            blk.scalar(_)

            def _(v):
                for i in range(8):
                    inst = v.tensor_copy(OFh[:, i * C:(i + 1) * C],
                                         H[:, (half * 8 + i) * C:(half * 8 + i + 1) * C])
                ev.inc(inst)
                v.wait_ge(ev.sem, ev0 + 1)
                for jg in range(4):
                    for nt in range(8):
                        v.wait_ge(mm.sem, mm0 + jg * 24 + 16 + nt + 1)
                        ev.inc(v.tensor_tensor(OFh[:, nt * C:(nt + 1) * C],
                                               OFh[:, nt * C:(nt + 1) * C],
                                               pOC[nt % 2][:, :], OP.add))
            blk.vector(_)

            ln_gb_resident(k, blk,
                           src=lambda i: OFh[:, i * C:(i + 1) * C],
                           dst=lambda i: H[:, (half * 8 + i) * C:(half * 8 + i + 1) * C],
                           gbc=GB[:, 0:C], bbc=GB[:, C:2 * C],
                           S=S3, garb=lambda i: MID[:, i * 1024:(i + 1) * 1024], eps=1e-12,
                           nt=8)

    for g_, _t in reversed(pOg):
        g_.__exit__(None, None, None)
    for g_, _t in reversed(pMg):
        g_.__exit__(None, None, None)
    for g_ in (gHB, gS3, gW2, gW1, gOFh, gMID, gB2, gB1, gGB, gHT):
        g_.__exit__(None, None, None)



def phase_f0(k):
    nc, d = k.nc, k.d
    ld, lg, mm, ev, sv, st, x2, x3 = k.ld, k.lg, k.mm, k.ev, k.sv, k.st, k.x2, k.x3
    H2 = 2048

    gXT, XT = sbuf(nc, "fXT", 128, 8 * N, BF16)
    gMK, MK = sbuf(nc, "fMK", 128, NT * 8, F32)
    gE8, E8 = sbuf(nc, "fE8", 128, 8, F32)
    gRW, RW = sbuf(nc, "fRW", 128, 2 * 64, BF16)
    gRF, RF = sbuf(nc, "fRF", 128, 64, F32)
    gXL, XLT = sbuf(nc, "fXLT", 128, 8 * N, BF16)
    gLG, LGt = sbuf(nc, "fLG", 128, 4 * 8, F32)
    gMX, MX = sbuf(nc, "fMX", 128, 4, F32)
    gMID, MID = sbuf(nc, "fMID", 128, 16 * 1024, BF16)
    gOFh, OFh = sbuf(nc, "fOF", 128, 8 * C, F32)
    gWG, WGs = sbuf(nc, "fWG", 128, 2 * 1024, BF16)
    gWU, WUs = sbuf(nc, "fWU", 128, 2 * 1024, BF16)
    gWD, WDg = sbuf(nc, "fWD", 128, 8 * 1024, BF16)
    gSG, SG = sbuf(nc, "fSG", 128, 2 * 512, BF16)
    pLg = psum(nc, "fpL", 128, 8)
    pL = pLg[1]

    ld0, lg0, mm0, ev0, sv0 = ld.n, lg.n, mm.n, ev.n, sv.n
    with nc.Block() as blk:
        def _(sync):
            for c in range(8):
                ld.dma(sync.dma_start(out=XT[:, c * N:(c + 1) * N],
                                      in_=d["sc_nbf"].ap()[:, c * 128:(c + 1) * 128],
                                      transpose=True))
            for c in range(8):
                ld.dma(sync.dma_start(out=XLT[:, c * N:(c + 1) * N],
                                      in_=d["sc_lbf"].ap()[:, c * 128:(c + 1) * 128],
                                      transpose=True))
            ld.wait(sync)
        blk.sync(_)
        def _(g):
            x3.inc(g.iota(E8[:, :], pattern=[[1, 8]], base=0, channel_multiplier=0,
                          allow_small_or_imprecise_dtypes=True))
            lg.dma(g.dma_start(out=RF[:, :],
                               in_=ap2(d["routerT"].ap(), 0, [[8, 128], [1024, 8], [1, 8]])))
            lg.wait(g)
        blk.gpsimd(_)
        def _(v):
            lg.wait(v)
            ev.inc(v.tensor_copy(RW[:, 0:64], RF[:, :])); ev.wait(v)
            ev.inc(v.tensor_sub(RW[:, 64:128], RF[:, :], RW[:, 0:64]))
        blk.vector(_)
        def _(t_):
            t_.wait_ge(ld.sem, ld0 + 16 * 16)
            t_.wait_ge(ev.sem, ev0 + 2)
            for t in range(NT):
                if t >= 1:
                    t_.wait_ge(sv.sem, sv0 + t)
                for c in range(8):
                    t_.matmul(pL[:, :],
                              XT[:, c * N + t * 128:c * N + (t + 1) * 128],
                              RW[:, c * 8:(c + 1) * 8],
                              start=(c == 0), stop=False)
                for c in range(8):
                    t_.matmul(pL[:, :],
                              XLT[:, c * N + t * 128:c * N + (t + 1) * 128],
                              RW[:, c * 8:(c + 1) * 8],
                              start=False, stop=False)
                for c in range(8):
                    inst = t_.matmul(pL[:, :],
                                     XT[:, c * N + t * 128:c * N + (t + 1) * 128],
                                     RW[:, 64 + c * 8:64 + (c + 1) * 8],
                                     start=False, stop=(c == 7))
                mm.inc(inst)
        blk.tensor(_)
        def _(s):
            for t in range(NT):
                s.wait_ge(mm.sem, mm0 + t + 1)
                if t >= 1:
                    s.wait_ge(ev.sem, ev0 + 2 + 6 * t)
                sv.inc(s.activation(LGt[:, 0:8], pL[:, :], AF.Identity))
        blk.scalar(_)
        def _(v):
            v.wait_ge(x3.sem, x3.n)
            for t in range(NT):
                v.wait_ge(sv.sem, sv0 + t + 1)
                ev.inc(v.reduce_max(MX[:, 0:1], LGt[:, 0:8], axis=AX))
                v.wait_ge(ev.sem, ev0 + 2 + 6 * t + 1)
                ev.inc(v.tensor_scalar(LGt[:, 8:16], LGt[:, 0:8], MX[:, 0:1], None, OP.is_equal))
                v.wait_ge(ev.sem, ev0 + 2 + 6 * t + 2)
                ev.inc(v.tensor_scalar(LGt[:, 16:24], LGt[:, 8:16], -1000.0, 1000.0, OP.mult, OP.add))
                v.wait_ge(ev.sem, ev0 + 2 + 6 * t + 3)
                ev.inc(v.tensor_add(LGt[:, 16:24], LGt[:, 16:24], E8[:, :]))
                v.wait_ge(ev.sem, ev0 + 2 + 6 * t + 4)
                ev.inc(v.tensor_reduce(MX[:, 1:2], LGt[:, 16:24], AX, OP.min))
                v.wait_ge(ev.sem, ev0 + 2 + 6 * t + 5)
                ev.inc(v.tensor_scalar(MK[:, t * 8:(t + 1) * 8], LGt[:, 16:24], MX[:, 1:2],
                                       None, OP.is_equal))
        blk.vector(_)
    pLg[0].__exit__(None, None, None)
    pGg = [psum(nc, f"fpG{i}", 128, 512) for i in range(2)]
    pUg = [psum(nc, f"fpU{i}", 128, 512) for i in range(2)]
    pOg = [psum(nc, f"fpO{i}", 128, 1024) for i in range(2)]
    pG = [t for _, t in pGg]
    pU = [t for _, t in pUg]
    pOC = [t for _, t in pOg]

    gaps = {"gate": d["gateT"].ap(), "up": d["upT"].ap(), "down": d["downT"].ap(),
            "gs": d["wgsT"].ap(), "us": d["wusT"].ap(), "ds": d["wdsT"].ap()}

    for half in range(2):
        hb = half * 1024
        for e in range(9):
            routed = e < 8
            mm0, sv0, ev0, x20, x30 = mm.n, sv.n, ev.n, x2.n, x3.n

            def mid_pos(j):
                jg = j // 8
                return jg * 40 + 4 * (j % 8) + 4

            with nc.Block() as blk:
                def _(g):
                    for jg in range(2):
                        for jj in range(8):
                            j = jg * 8 + jj
                            if j >= 2:
                                g.wait_ge(mm.sem, mm0 + mid_pos(j - 2))
                            if routed:
                                gsrc = ap2(gaps["gate"], e * C * H2 + 128 * j,
                                           [[2048, 128], [128 * 2048, 8], [1, 128]])
                                usrc = ap2(gaps["up"], e * C * H2 + 128 * j,
                                           [[2048, 128], [128 * 2048, 8], [1, 128]])
                            else:
                                gsrc = ap2(gaps["gs"], 128 * j,
                                           [[2048, 128], [128 * 2048, 8], [1, 128]])
                                usrc = ap2(gaps["us"], 128 * j,
                                           [[2048, 128], [128 * 2048, 8], [1, 128]])
                            lg.dma(g.dma_start(out=WGs[:, (j % 2) * 1024:(j % 2 + 1) * 1024], in_=gsrc))
                            lg.dma(g.dma_start(out=WUs[:, (j % 2) * 1024:(j % 2 + 1) * 1024], in_=usrc))
                            lg.wait(g)
                            x3.bump(g)
                        if jg >= 1:
                            g.wait_ge(mm.sem, mm0 + jg * 40)
                        dsrc = (ap2(gaps["down"], e * H2 * C + jg * 1024 * C,
                                    [[1024, 128], [131072, 8], [1, 1024]]) if routed else
                                ap2(gaps["ds"], jg * 1024 * C,
                                    [[1024, 128], [131072, 8], [1, 1024]]))
                        lg.dma(g.dma_start(out=WDg[:, :], in_=dsrc))
                        lg.wait(g)
                        x2.bump(g)
                blk.gpsimd(_)

                def _(t_):
                    evb = ev0 + (1 if e == 0 else 0)
                    for jg in range(2):
                        for jj in range(8):
                            j = jg * 8 + jj
                            t_.wait_ge(x3.sem, x30 + j + 1)
                            for pg in range(2):
                                ei = 2 * j + pg
                                if ei >= 2:
                                    t_.wait_ge(sv.sem, sv0 + ei - 1)
                                for c in range(8):
                                    inst = t_.matmul(
                                        pG[ei % 2][:, :],
                                        WGs[:, (j % 2) * 1024 + c * 128:(j % 2) * 1024 + (c + 1) * 128],
                                        XT[:, c * N + hb + pg * 512:c * N + hb + (pg + 1) * 512],
                                        start=(c == 0), stop=(c == 7))
                                    if c == 7:
                                        mm.inc(inst)
                                if ei >= 2:
                                    eprev = ei - 2
                                    t_.wait_ge(ev.sem, evb + (eprev // 16) * 24 + (eprev % 16) + 1)
                                for c in range(8):
                                    inst = t_.matmul(
                                        pU[ei % 2][:, :],
                                        WUs[:, (j % 2) * 1024 + c * 128:(j % 2) * 1024 + (c + 1) * 128],
                                        XT[:, c * N + hb + pg * 512:c * N + hb + (pg + 1) * 512],
                                        start=(c == 0), stop=(c == 7))
                                    if c == 7:
                                        mm.inc(inst)
                        t_.wait_ge(x2.sem, x20 + jg + 1)
                        t_.wait_ge(ev.sem, evb + jg * 24 + 16)
                        for nt in range(8):
                            lin = jg * 8 + nt
                            if lin >= 2:
                                lp = lin - 2
                                t_.wait_ge(ev.sem, evb + (lp // 8) * 24 + 16 + (lp % 8) + 1)
                            for cop in range(2):
                                for jj in range(8):
                                    inst = t_.matmul(
                                        pOC[nt % 2][:, cop * 512:(cop + 1) * 512],
                                        MID[:, (jg % 2) * 8192 + jj * 1024 + nt * 128:
                                            (jg % 2) * 8192 + jj * 1024 + nt * 128 + 128],
                                        WDg[:, jj * 1024 + cop * 512:jj * 1024 + (cop + 1) * 512],
                                        start=(jj == 0), stop=(jj == 7))
                            mm.inc(inst)
                blk.tensor(_)

                def _(s):
                    evb = ev0 + (1 if e == 0 else 0)
                    for jg in range(2):
                        for jj in range(8):
                            j = jg * 8 + jj
                            for pg in range(2):
                                ei = 2 * j + pg
                                s.wait_ge(mm.sem, mm0 + jg * 40 + 4 * jj + 2 * pg + 1)
                                if ei >= 2:
                                    eprev = ei - 2
                                    s.wait_ge(ev.sem, evb + (eprev // 16) * 24 + (eprev % 16) + 1)
                                sv.inc(s.activation(SG[:, (ei % 2) * 512:(ei % 2 + 1) * 512],
                                                    pG[ei % 2][:, :], AF.Silu))
                blk.scalar(_)

                def _(v):
                    if e == 0:
                        for i in range(8):
                            inst = v.memset(OFh[:, i * C:(i + 1) * C], 0.0)
                        ev.inc(inst)
                        v.wait_ge(ev.sem, ev0 + 1)
                    evb = ev0 + (1 if e == 0 else 0)
                    for jg in range(2):
                        for jj in range(8):
                            j = jg * 8 + jj
                            for pg in range(2):
                                ei = 2 * j + pg
                                v.wait_ge(sv.sem, sv0 + ei + 1)
                                v.wait_ge(mm.sem, mm0 + jg * 40 + 4 * jj + 2 * pg + 2)
                                ev.inc(v.tensor_tensor(
                                    MID[:, (jg % 2) * 8192 + jj * 1024 + pg * 512:
                                        (jg % 2) * 8192 + jj * 1024 + (pg + 1) * 512],
                                    SG[:, (ei % 2) * 512:(ei % 2 + 1) * 512],
                                    pU[ei % 2][:, :], OP.mult))
                        for nt in range(8):
                            v.wait_ge(mm.sem, mm0 + jg * 40 + 32 + nt + 1)
                            if jg == 1:
                                v.wait_ge(ev.sem, evb + 16 + nt + 1)
                            mask = MK[:, (half * 8 + nt) * 8 + e:(half * 8 + nt) * 8 + e + 1] \
                                if routed else 1.0
                            ev.inc(v.scalar_tensor_tensor(
                                OFh[:, nt * C:(nt + 1) * C],
                                pOC[nt % 2][:, :], mask,
                                OFh[:, nt * C:(nt + 1) * C],
                                OP.mult, OP.add))
                blk.vector(_)

        with nc.Block() as blk:
            def _(sync):
                for i in range(8):
                    st.dma(sync.dma_start(
                        out=d["sc_tmp"].ap()[(half * 8 + i) * 128:(half * 8 + i + 1) * 128, :],
                        in_=OFh[:, i * C:(i + 1) * C]))
                st.wait(sync)
            blk.sync(_)

    for g_, _t in reversed(pOg):
        g_.__exit__(None, None, None)
    for g_, _t in reversed(pUg):
        g_.__exit__(None, None, None)
    for g_, _t in reversed(pGg):
        g_.__exit__(None, None, None)
    for g_ in (gSG, gWD, gWU, gWG, gOFh, gMID, gMX, gLG, gXL, gRF, gRW, gE8, gMK, gXT):
        g_.__exit__(None, None, None)


def phase_a1(k):
    nc, d = k.nc, k.d
    ld, lg, mm, ev, sv, st, x2, x3 = k.ld, k.lg, k.mm, k.ev, k.sv, k.st, k.x2, k.x3
    SHIFT = 32.0

    l0, m0, s0, t0 = lg.n, mm.n, sv.n, st.n
    gOH, OH = sbuf(nc, "bOH", 32, 8 * 512, BF16)
    gRB, RB = sbuf(nc, "bRB", 32, 16, BF16)
    gBE, BE = sbuf(nc, "bBE", 16, 8 * 512, F32)
    pBg = psum(nc, "bpB", 16, 2 * 512)
    pB = pBg[1]
    with nc.Block() as blk:
        def _(g):
            lg.dma(g.dma_start(out=RB[:, :], in_=d["relb"].ap()))
            for p in range(8):
                w = 512 if p < 7 else 511
                lg.dma(g.dma_start(out=OH[:, p * 512:p * 512 + w],
                                   in_=ap2(d["onehotT"].ap(), p * 512, [[4095, 32], [1, w]])))
            lg.wait(g)
        blk.gpsimd(_)
        def _(t_):
            t_.wait_ge(lg.sem, l0 + 16 * 9)
            for p in range(8):
                w = 512 if p < 7 else 511
                if p >= 2:
                    t_.wait_ge(sv.sem, s0 + p - 1)
                mm.inc(t_.matmul(pB[:, (p % 2) * 512:(p % 2) * 512 + w], RB[:, :],
                                 OH[:, p * 512:p * 512 + w], start=True, stop=True))
        blk.tensor(_)
        def _(s):
            for p in range(8):
                w = 512 if p < 7 else 511
                s.wait_ge(mm.sem, m0 + p + 1)
                sv.inc(s.activation(BE[:, p * 512:p * 512 + w],
                                    pB[:, (p % 2) * 512:(p % 2) * 512 + w], AF.Identity))
        blk.scalar(_)
        def _(sync):
            for p in range(8):
                w = 512 if p < 7 else 511
                sync.wait_ge(sv.sem, s0 + p + 1)
                st.dma(sync.dma_start(
                    out=ap2(d["sc_bias"].ap(), p * 512, [[4095, 16], [1, w]]),
                    in_=BE[0:16, p * 512:p * 512 + w]))
            st.wait(sync)
        blk.sync(_)
    pBg[0].__exit__(None, None, None)
    for g_ in (gBE, gRB, gOH):
        g_.__exit__(None, None, None)

    gW1, W1b = sbuf(nc, "a1W1b", 128, 8 * C, BF16)
    gNT8, NT8 = sbuf(nc, "a1NT8", 128, 8 * N, BF16)
    gLW, LW = sbuf(nc, "a1LW", 128, C, F32)
    gNF2, NF2 = sbuf(nc, "a1NF2", 128, 2 * C, F32)
    gNG, NG = sbuf(nc, "a1NG", 128, 4 * C, F32)
    gNB, NB = sbuf(nc, "a1NB", 128, 2 * C, BF16)
    gS4, SC4 = sbuf(nc, "a1SC4", 128, 4 * NT, F32)
    l0, m0, s0, e0, t0, d0, x30, x20 = lg.n, mm.n, sv.n, ev.n, st.n, ld.n, x3.n, x2.n
    with nc.Block() as blk:
        def _(g):
            lg.dma(g.dma_start(out=LW[:, :], in_=ap2(d["lnw1"].ap(), 0, [[0, 128], [1, C]])))
            for t in range(NT):
                sl = t % 2
                if t >= 2:
                    g.wait_ge(ev.sem, e0 + 3 * (t - 2) + 4)
                lg.dma(g.dma_start(out=NF2[:, sl * C:(sl + 1) * C],
                                   in_=d["sc_nf"].ap()[t * 128:(t + 1) * 128, :]))
                lg.wait(g)
                x3.bump(g)
        blk.gpsimd(_)
        def _(v):
            ev.inc(v.memset(SC4[:, 3 * NT:4 * NT], 0.0))
            v.wait_ge(ev.sem, e0 + 1)
            for t in range(NT):
                sl = t % 2
                v.wait_ge(x3.sem, x30 + t + 1)
                ev.inc(v.scalar_tensor_tensor(
                    NG[:, (2 + sl) * C:(3 + sl) * C], NF2[:, sl * C:(sl + 1) * C], 0.0,
                    NF2[:, sl * C:(sl + 1) * C], OP.subtract, OP.mult,
                    accum_out=SC4[:, 3 * NT + t:3 * NT + t + 1]))
                v.wait_ge(ev.sem, e0 + 3 * t + 2)
                ev.inc(v.tensor_scalar(SC4[:, t:t + 1], SC4[:, 3 * NT + t:3 * NT + t + 1],
                                       1.0 / C, 1e-6, OP.mult, OP.add))
                v.wait_ge(sv.sem, s0 + 3 * t + 2)
                ev.inc(v.tensor_tensor(NG[:, sl * C:(sl + 1) * C],
                                       NF2[:, sl * C:(sl + 1) * C], LW[:, :], OP.mult))
        blk.vector(_)
        def _(s):
            for t in range(NT):
                sl = t % 2
                s.wait_ge(ev.sem, e0 + 3 * t + 3)
                sv.inc(s.activation(SC4[:, NT + t:NT + t + 1], SC4[:, t:t + 1], AF.Ln))
                s.wait_ge(sv.sem, s0 + 3 * t + 1)
                sv.inc(s.activation(SC4[:, NT + t:NT + t + 1], SC4[:, NT + t:NT + t + 1],
                                    AF.Exp, scale=-0.5))
                s.wait_ge(sv.sem, s0 + 3 * t + 2)
                s.wait_ge(ev.sem, e0 + 3 * t + 4)
                if t >= 2:
                    s.wait_ge(x2.sem, x20 + t - 1)
                sv.inc(s.activation(NB[:, sl * C:(sl + 1) * C], NG[:, sl * C:(sl + 1) * C],
                                    AF.Identity, scale=SC4[:, NT + t:NT + t + 1]))
        blk.scalar(_)
        def _(sync):
            for t in range(NT):
                sl = t % 2
                sync.wait_ge(sv.sem, s0 + 3 * t + 3)
                st.dma(sync.dma_start(out=d["sc_bbf"].ap()[t * 128:(t + 1) * 128, 0:C],
                                      in_=NB[:, sl * C:(sl + 1) * C]))
                st.wait(sync)
                x2.bump(sync)
            for c in range(8):
                ld.dma(sync.dma_start(out=NT8[:, c * N:(c + 1) * N],
                                      in_=d["sc_bbf"].ap()[:, c * 128:(c + 1) * 128],
                                      transpose=True))
            ld.wait(sync)
        blk.sync(_)
    for g_ in (gS4, gNB, gNG, gNF2, gLW):
        g_.__exit__(None, None, None)

    gQE, QE = sbuf(nc, "a1QE", 128, 2 * 512, BF16)
    pQg = [psum(nc, f"a1pQ{i}", 128, 512) for i in range(2)]
    pQ2 = [t for _, t in pQg]
    l0, m0, s0, t0, x30, x20 = lg.n, mm.n, sv.n, st.n, x3.n, x2.n
    G = 96

    def _dst(g):
        wi = g // 32
        if wi < 2:
            e = g % 32
            co, pg = e // 4, e % 4
            r0 = wi * 1024 + co * 128
            return d["sc_bbf"].ap()[r0:r0 + 128, 2048 + pg * 512:2048 + (pg + 1) * 512]
        t = (g - 64) // 2
        cop = (g - 64) % 2
        return d["sc_lbf"].ap()[t * 128:(t + 1) * 128, cop * 512:(cop + 1) * 512]

    with nc.Block() as blk:
        def _(g):
            for wi, wname in enumerate(("wq1", "wk1", "wv1")):
                if wi > 0:
                    g.wait_ge(mm.sem, m0 + 32 * wi)
                lg.dma(g.dma_start(out=W1b[:, :],
                                   in_=ap2(d[wname].ap(), 0,
                                           [[1024, 128], [128 * 1024, 8], [1, 1024]])))
                lg.wait(g)
                x3.bump(g)
        blk.gpsimd(_)
        def _(t_):
            for g in range(G):
                wi = g // 32
                t_.wait_ge(x3.sem, x30 + wi + 1)
                if g >= 2:
                    t_.wait_ge(sv.sem, s0 + g - 1)
                if wi < 2:
                    e = g % 32
                    co, pg = e // 4, e % 4
                    for c in range(8):
                        inst = t_.matmul(
                            pQ2[g % 2][:, :],
                            W1b[:, c * C + co * 128:c * C + (co + 1) * 128],
                            NT8[:, c * N + pg * 512:c * N + (pg + 1) * 512],
                            start=(c == 0), stop=(c == 7))
                else:
                    tt = (g - 64) // 2
                    cop = (g - 64) % 2
                    for c in range(8):
                        inst = t_.matmul(
                            pQ2[g % 2][:, :],
                            NT8[:, c * N + tt * 128:c * N + (tt + 1) * 128],
                            W1b[:, c * C + cop * 512:c * C + (cop + 1) * 512],
                            start=(c == 0), stop=(c == 7))
                mm.inc(inst)
        blk.tensor(_)
        def _(s):
            for g in range(G):
                s.wait_ge(mm.sem, m0 + g + 1)
                if g >= 2:
                    s.wait_ge(x2.sem, x20 + g - 1)
                sv.inc(s.activation(QE[:, (g % 2) * 512:(g % 2 + 1) * 512], pQ2[g % 2][:, :],
                                    AF.Identity))
        blk.scalar(_)
        def _(sync):
            for g in range(G):
                sync.wait_ge(sv.sem, s0 + g + 1)
                st.dma(sync.dma_start(out=_dst(g), in_=QE[:, (g % 2) * 512:(g % 2 + 1) * 512]))
                st.wait(sync)
                x2.bump(sync)
        blk.sync(_)
    for g_, _t in reversed(pQg):
        g_.__exit__(None, None, None)
    gQE.__exit__(None, None, None)
    gNT8.__exit__(None, None, None)

    gFB, FB = sbuf(nc, "a1FB", 128, 33, F32)
    gQH, QH = sbuf(nc, "a1QH", 64, 2 * N, BF16)
    gKH, KH = sbuf(nc, "a1KH", 64, 2 * N, BF16)
    gVT, VT = sbuf(nc, "a1VT", 128, 16 * 65, BF16)
    gBW, BW = sbuf(nc, "a1BW", 128, 16 * 384, BF16)
    gRV, RV = sbuf(nc, "a1RV", 128, 128, BF16)
    gPT, PTb = sbuf(nc, "a1PT", 128, 2 * N, BF16)
    gRS, RS = sbuf(nc, "a1RS", 1, N, F32)
    gRSB, RSB = sbuf(nc, "a1RSB", 64, N, F32)
    gCE, CTXe = sbuf(nc, "a1CE", 64, 2 * N, BF16)
    pSg = psum(nc, "a1pS", 128, N)
    pCg = psum(nc, "a1pC", 65, N)
    pS, pC = pSg[1], pCg[1]
    l0, m0, s0, e0, t0, d0, x20, x30 = lg.n, mm.n, sv.n, ev.n, st.n, ld.n, x2.n, x3.n

    def _lohi(kt):
        k0 = kt * 128
        return max(0, k0 - 128), min(N, k0 + 256)

    pre_done = x20 + 1
    vtb = {(h, j): x20 + 1 + 4 * h + j + 1 for h in range(16) for j in range(2)}
    rsb_done = {h: x20 + 1 + 4 * h + 3 for h in range(16)}
    ctxs_done = {h: x20 + 1 + 4 * h + 4 for h in range(16)}
    exp_done = {}
    cnt = 0
    for h in range(16):
        for kt in range(16):
            lo, hi = _lohi(kt)
            cnt += (1 if lo > 0 else 0) + 1 + (1 if hi < N else 0)
            exp_done[h * 16 + kt] = s0 + cnt
    recip_done = {h: e0 + 1 + 2 * h + 1 for h in range(16)}
    ctx_done = {h: e0 + 1 + 2 * h + 2 for h in range(16)}
    scr = {lin: m0 + 2 * lin + 1 for lin in range(256)}
    pvd = {lin: m0 + 2 * lin + 2 for lin in range(256)}

    with nc.Block() as blk:
        def _(t_):
            for h in range(16):
                hs = h % 2
                for kt in range(16):
                    lin = h * 16 + kt
                    k0 = kt * 128
                    sl = lin % 2
                    lo, hi = _lohi(kt)
                    if kt == 0:
                        t_.wait_ge(x2.sem, pre_done if h == 0 else vtb[(h - 1, 0)])
                    if lin >= 1:
                        t_.wait_ge(sv.sem, exp_done[lin - 1])
                    t_.wait_ge(x3.sem, x30 + lin // 8 + 1)
                    for pg in range(4):
                        t_.matmul(pS[:, pg * 512:(pg + 1) * 512],
                                  KH[:, hs * N + k0:hs * N + k0 + 128],
                                  QH[:, hs * N + pg * 512:hs * N + (pg + 1) * 512],
                                  start=True, stop=True, skip_group_check=True)
                    a = lo
                    while a < hi:
                        b = min(hi, (a // 512 + 1) * 512)
                        wb = (lin % 16) * 384 + (a - lo)
                        inst = t_.matmul(pS[:, a:b], RV[:, :], BW[:, wb:wb + (b - a)],
                                         start=False, stop=True, skip_group_check=True)
                        a = b
                    mm.inc(inst)
                    t_.wait_ge(sv.sem, exp_done[lin])
                    t_.wait_ge(x2.sem, vtb[(h, kt // 8)])
                    if kt == 0:
                        t_.wait_ge(ev.sem, ctx_done[h - 1] if h >= 1 else e0 + 1)
                    for pg in range(4):
                        inst = t_.matmul(pC[:, pg * 512:(pg + 1) * 512],
                                         VT[:, kt * 65:kt * 65 + 65],
                                         PTb[:, sl * N + pg * 512:sl * N + (pg + 1) * 512],
                                         start=(kt == 0), stop=(kt == 15),
                                         skip_group_check=True)
                    mm.inc(inst)
        blk.tensor(_)

        def _(v):
            v.wait_ge(x2.sem, pre_done)
            v.tensor_scalar_add(FB[:, 0:32], FB[:, 0:32], -SHIFT)
            v.memset(FB[:, 32:33], -SHIFT)
            inst = None
            for s in range(16):
                inst = v.memset(VT[:, s * 65 + 64:s * 65 + 65], 1.0)
            ev.inc(inst)
            for h in range(16):
                v.wait_ge(mm.sem, pvd[h * 16 + 15])
                if h >= 1:
                    v.wait_ge(x2.sem, rsb_done[h - 1])
                ev.inc(v.reciprocal(RS[0:1, :], pC[64:65, :]))
                v.wait_ge(x2.sem, rsb_done[h])
                if h >= 2:
                    v.wait_ge(x2.sem, ctxs_done[h - 2])
                ev.inc(v.scalar_tensor_tensor(CTXe[:, (h % 2) * N:(h % 2 + 1) * N],
                                              pC[0:64, :], 1.0, RSB[:, :],
                                              OP.mult, OP.mult))
        blk.vector(_)

        def _(s):
            s.wait_ge(ev.sem, e0 + 1)
            for h in range(16):
                for kt in range(16):
                    lin = h * 16 + kt
                    sl = lin % 2
                    lo, hi = _lohi(kt)
                    s.wait_ge(mm.sem, scr[lin])
                    if lin >= 2:
                        s.wait_ge(mm.sem, pvd[lin - 2])
                    if lo > 0:
                        sv.inc(s.activation(PTb[:, sl * N:sl * N + lo], pS[:, 0:lo],
                                            AF.Exp, bias=FB[:, 2 * h + 1:2 * h + 2]))
                    sv.inc(s.activation(PTb[:, sl * N + lo:sl * N + hi], pS[:, lo:hi],
                                        AF.Exp, bias=FB[:, 32:33]))
                    if hi < N:
                        sv.inc(s.activation(PTb[:, sl * N + hi:sl * N + N], pS[:, hi:N],
                                            AF.Exp, bias=FB[:, 2 * h:2 * h + 1]))
        blk.scalar(_)

        def _(g):
            for b in range(32):
                if b >= 2:
                    g.wait_ge(mm.sem, scr[8 * (b - 2) + 7])
                for i in range(8):
                    lin = 8 * b + i
                    h, kt = lin // 16, lin % 16
                    k0 = kt * 128
                    lo, hi = _lohi(kt)
                    W = hi - lo
                    lg.dma(g.dma_start(
                        out=BW[:, (lin % 16) * 384:(lin % 16) * 384 + W],
                        in_=ap2(d["sc_bias"].ap(), h * 4095 + 1920 - k0 + lo,
                                [[1, 128], [1, W]])))
                lg.wait(g)
                x3.bump(g)
        blk.gpsimd(_)

        def _(sync):
            for h in range(16):
                ld.dma(sync.dma_start(out=FB[:, 2 * h:2 * h + 1],
                                      in_=ap2(d["relb"].ap(), 15 * 16 + h, [[0, 128], [1, 1]])))
                ld.dma(sync.dma_start(out=FB[:, 2 * h + 1:2 * h + 2],
                                      in_=ap2(d["relb"].ap(), 31 * 16 + h, [[0, 128], [1, 1]])))
            ld.dma(sync.dma_start(out=QH[:, 0:N],
                                  in_=d["sc_bbf"].ap()[0:64, 2048:4096]))
            ld.dma(sync.dma_start(out=KH[:, 0:N],
                                  in_=d["sc_bbf"].ap()[1024:1024 + 64, 2048:4096]))
            ld.dma(sync.dma_start(out=RV[:, :], in_=d["revbf"].ap()))
            ld.wait(sync)
            x2.bump(sync)
            for h in range(16):
                for j in range(2):
                    bidx = 2 * h + j
                    if bidx >= 2:
                        sync.wait_ge(mm.sem, pvd[8 * (bidx - 2) + 7])
                    if j == 0 and h < 15:
                        if h >= 1:
                            sync.wait_ge(mm.sem, scr[(h - 1) * 16 + 15])
                        hn = h + 1
                        hs = hn % 2
                        ld.dma(sync.dma_start(
                            out=QH[:, hs * N:(hs + 1) * N],
                            in_=d["sc_bbf"].ap()[hn * 64:(hn + 1) * 64, 2048:4096]))
                        ld.dma(sync.dma_start(
                            out=KH[:, hs * N:(hs + 1) * N],
                            in_=d["sc_bbf"].ap()[1024 + hn * 64:1024 + (hn + 1) * 64, 2048:4096]))
                    for kt in range(8 * j, 8 * j + 8):
                        ld.dma(sync.dma_start(
                            out=VT[:, kt * 65:kt * 65 + 64],
                            in_=d["sc_lbf"].ap()[kt * 128:(kt + 1) * 128, h * 64:(h + 1) * 64]))
                    ld.wait(sync)
                    x2.bump(sync)
                sync.wait_ge(ev.sem, recip_done[h])
                st.dma(sync.dma_start(out=d["sc_misc"].ap()[8:9, 0:N], in_=RS[0:1, :]))
                st.wait(sync)
                ld.dma(sync.dma_start(out=RSB[:, :],
                                      in_=ap2(d["sc_misc"].ap(), 8 * 4096, [[0, 64], [1, N]])))
                ld.wait(sync)
                x2.bump(sync)
                sync.wait_ge(ev.sem, ctx_done[h])
                st.dma(sync.dma_start(
                    out=d["sc_bbf"].ap()[1024 + h * 64:1024 + (h + 1) * 64, 2048:4096],
                    in_=CTXe[:, (h % 2) * N:(h % 2 + 1) * N]))
                st.wait(sync)
                x2.bump(sync)
        blk.sync(_)
    assert x2.n == ctxs_done[15], (x2.n, ctxs_done[15])
    assert ev.n == ctx_done[15], (ev.n, ctx_done[15])
    assert sv.n == exp_done[255], (sv.n, exp_done[255])
    pCg[0].__exit__(None, None, None)
    pSg[0].__exit__(None, None, None)
    for g_ in (gCE, gRSB, gRS, gPT, gRV, gBW, gVT, gKH, gQH, gFB):
        g_.__exit__(None, None, None)

    gCT4, CT4 = sbuf(nc, "a1CT4", 128, 8 * N, BF16)
    gNF3, NF3 = sbuf(nc, "a1NF3", 128, 2 * C, F32)
    gOF2, OF2 = sbuf(nc, "a1OF2", 128, 2 * C, F32)
    pOg = [psum(nc, f"a1pO{i}", 128, 1024) for i in range(2)]
    pO2 = [t for _, t in pOg]
    l0, m0, e0, t0, d0, x20, x30 = lg.n, mm.n, ev.n, st.n, ld.n, x2.n, x3.n
    with nc.Block() as blk:
        def _(g):
            lg.dma(g.dma_start(out=W1b[:, :],
                               in_=ap2(d["wo1"].ap(), 0,
                                       [[1024, 128], [128 * 1024, 8], [1, 1024]])))
            lg.wait(g)
            x3.bump(g)
            for t in range(NT):
                sl = t % 2
                if t >= 2:
                    g.wait_ge(ev.sem, e0 + t - 1)
                lg.dma(g.dma_start(out=NF3[:, sl * C:(sl + 1) * C],
                                   in_=d["sc_nf"].ap()[t * 128:(t + 1) * 128, :]))
                lg.wait(g)
                x3.bump(g)
        blk.gpsimd(_)
        def _(sync):
            for cc in range(8):
                ld.dma(sync.dma_start(out=CT4[:, cc * N:(cc + 1) * N],
                                      in_=d["sc_bbf"].ap()[1024 + cc * 128:1024 + (cc + 1) * 128,
                                                           2048:4096]))
            ld.wait(sync)
            x2.bump(sync)
            for t in range(NT):
                sl = t % 2
                sync.wait_ge(ev.sem, e0 + t + 1)
                st.dma(sync.dma_start(out=d["sc_tmp"].ap()[t * 128:(t + 1) * 128, :],
                                      in_=OF2[:, sl * C:(sl + 1) * C]))
                st.wait(sync)
                x2.bump(sync)
        blk.sync(_)
        def _(t_):
            t_.wait_ge(x3.sem, x30 + 1)
            t_.wait_ge(x2.sem, x20 + 1)
            for t in range(NT):
                if t >= 2:
                    t_.wait_ge(ev.sem, e0 + t - 1)
                for cop in range(2):
                    for cc in range(8):
                        inst = t_.matmul(
                            pO2[t % 2][:, cop * 512:(cop + 1) * 512],
                            CT4[:, cc * N + t * 128:cc * N + (t + 1) * 128],
                            W1b[:, cc * C + cop * 512:cc * C + (cop + 1) * 512],
                            start=(cc == 0), stop=(cc == 7))
                mm.inc(inst)
        blk.tensor(_)
        def _(v):
            for t in range(NT):
                sl = t % 2
                v.wait_ge(mm.sem, m0 + t + 1)
                v.wait_ge(x3.sem, x30 + t + 2)
                if t >= 2:
                    v.wait_ge(x2.sem, x20 + t)
                ev.inc(v.tensor_tensor(OF2[:, sl * C:(sl + 1) * C], pO2[t % 2][:, :],
                                       NF3[:, sl * C:(sl + 1) * C], OP.add))
        blk.vector(_)
    for g_, _t in reversed(pOg):
        g_.__exit__(None, None, None)
    for g_ in (gOF2, gNF3, gCT4):
        g_.__exit__(None, None, None)
    gW1.__exit__(None, None, None)



def _f1_ladder(eng, ctr, R, M, Z, WQ, width):
    for kb in range(7, -1, -1):
        p = float(2 ** kb)
        ctr.inc(eng.tensor_scalar(M, R, p, -p, OP.is_ge, OP.mult))
        eng.wait_ge(ctr.sem, ctr.n)
        ctr.inc(eng.tensor_tensor(R, R, M, OP.add))
        eng.wait_ge(ctr.sem, ctr.n)
    ctr.inc(eng.tensor_tensor(Z, Z, R, OP.subtract))
    eng.wait_ge(ctr.sem, ctr.n)
    ctr.inc(eng.tensor_scalar(WQ, Z, 255.0, -128.0, OP.min, OP.add))
    eng.wait_ge(ctr.sem, ctr.n)


def phase_f1(k):
    nc, d = k.nc, k.d
    ld, lg, mm, ev, sv, st, x2, x3 = k.ld, k.lg, k.mm, k.ev, k.sv, k.st, k.x2, k.x3

    gWI, WI = sbuf(nc, "f1WI", 128, 8 * 4096, BF16)
    gSCL, SCL = sbuf(nc, "f1SCL", 128, 4, F32)
    gS16, SCL16 = sbuf(nc, "f1S16", 16, 2, F32)
    gB85, B85 = sbuf(nc, "f1B85", 128, 1, F32)

    gAB, AB = sbuf(nc, "f1AB", 128, 2 * 4096, F32)
    gMX, MX = sbuf(nc, "f1MX", 128, 48, F32)
    gMR, MR = sbuf(nc, "f1MR", 1, 256, F32)
    gSR, SCR = sbuf(nc, "f1SCR", 1, 48, F32)
    e0, t0, d0, x20 = ev.n, st.n, ld.n, x2.n

    def _asrc(j):
        if j < 8:
            return d["WfcT"].ap()[j * 128:(j + 1) * 128, :], 4096
        c = j - 8
        return d["WprojT"].ap()[c * 128:(c + 1) * 128, :], 1024

    with nc.Block() as blk:
        def _(sync):
            for j in range(40):
                src, w = _asrc(j)
                if j >= 2:
                    sync.wait_ge(ev.sem, e0 + j - 1)
                ld.dma(sync.dma_start(out=AB[:, (j % 2) * 4096:(j % 2) * 4096 + w], in_=src))
                ld.wait(sync)
                x2.bump(sync)
            sync.wait_ge(ev.sem, e0 + 44)
            st.dma(sync.dma_start(out=ap2(d["sc_misc"].ap(), 9 * 4096, [[1, 128], [1, 1]]),
                                  in_=MX[:, 40:41]))
            st.dma(sync.dma_start(out=ap2(d["sc_misc"].ap(), 9 * 4096 + 128, [[1, 128], [1, 1]]),
                                  in_=MX[:, 41:42]))
            st.wait(sync)
            ld.dma(sync.dma_start(out=MR[0:1, :], in_=d["sc_misc"].ap()[9:10, 0:256]))
            ld.wait(sync)
            x2.bump(sync)
            sync.wait_ge(ev.sem, e0 + 56)
            st.dma(sync.dma_start(out=d["sc_misc"].ap()[10:11, 0:4], in_=SCR[0:1, 40:44]))
            st.wait(sync)
            for i in range(4):
                ld.dma(sync.dma_start(out=SCL[:, i:i + 1],
                                      in_=ap2(d["sc_misc"].ap(), 10 * 4096 + i, [[0, 128], [1, 1]])))
            ld.dma(sync.dma_start(out=SCL16[:, 0:1],
                                  in_=ap2(d["sc_misc"].ap(), 10 * 4096 + 0, [[0, 16], [1, 1]])))
            ld.dma(sync.dma_start(out=SCL16[:, 1:2],
                                  in_=ap2(d["sc_misc"].ap(), 10 * 4096 + 2, [[0, 16], [1, 1]])))
            ld.wait(sync)
            x2.bump(sync)
        blk.sync(_)
        def _(v):
            for j in range(40):
                src, w = _asrc(j)
                v.wait_ge(x2.sem, x20 + j + 1)
                ev.inc(v.reduce_max(MX[:, j:j + 1], AB[:, (j % 2) * 4096:(j % 2) * 4096 + w],
                                    axis=AX, apply_absolute_value=True))
            v.wait_ge(ev.sem, e0 + 40)
            ev.inc(v.reduce_max(MX[:, 40:41], MX[:, 0:8], axis=AX))
            ev.inc(v.reduce_max(MX[:, 41:42], MX[:, 8:40], axis=AX))
            v.wait_ge(ev.sem, e0 + 42)
            ev.inc(v.memset(B85[:, :], 128.5))
            ev.inc(v.memset(MX[:, 42:43], 0.0))
            v.wait_ge(x2.sem, x20 + 41)
            ev.inc(v.reduce_max(SCR[0:1, 0:1], MR[0:1, 0:128], axis=AX))
            ev.inc(v.reduce_max(SCR[0:1, 1:2], MR[0:1, 128:256], axis=AX))
            v.wait_ge(ev.sem, e0 + 46)
            ev.inc(v.tensor_copy(SCR[0:1, 8:16], ap2(SCR[:, :], SCR[:, :].offset + 0, [[48, 1], [0, 8]])))
            v.wait_ge(ev.sem, e0 + 47)
            ev.inc(v.tensor_copy(SCR[0:1, 16:24], ap2(SCR[:, :], SCR[:, :].offset + 1, [[48, 1], [0, 8]])))
            v.wait_ge(ev.sem, e0 + 48)
            ev.inc(v.reciprocal(SCR[0:1, 24:40], SCR[0:1, 8:24]))
            v.wait_ge(ev.sem, e0 + 49)
            ev.inc(v.tensor_scalar_mul(SCR[0:1, 40:41], SCR[0:1, 24:25], 127.0))
            ev.inc(v.tensor_scalar_mul(SCR[0:1, 41:42], SCR[0:1, 8:9], 1.0 / 127.0))
            ev.inc(v.tensor_scalar_mul(SCR[0:1, 42:43], SCR[0:1, 32:33], 127.0))
            ev.inc(v.tensor_scalar_mul(SCR[0:1, 43:44], SCR[0:1, 16:17], 1.0 / 127.0))
            v.wait_ge(ev.sem, e0 + 53)
            ev.inc(v.tensor_copy(SCR[0:1, 44:45], SCR[0:1, 40:41]))
            v.wait_ge(ev.sem, e0 + 54)
            ev.inc(v.tensor_copy(SCR[0:1, 45:46], SCR[0:1, 40:41]))
            v.wait_ge(ev.sem, e0 + 55)
            ev.inc(v.tensor_copy(SCR[0:1, 46:47], SCR[0:1, 40:41]))
        blk.vector(_)
    for g_ in (gSR, gMR, gMX):
        g_.__exit__(None, None, None)

    def quant_stage(tag, nchunk, width, wname):
        gZ, Z = sbuf(nc, f"f1Z{tag}", 128, 2 * width, F32)
        gR, R = sbuf(nc, f"f1R{tag}", 128, 2 * width, F32)
        gM, M = sbuf(nc, f"f1M{tag}", 128, width, F32)
        half = width // 2
        e0, s0, t0, d0, x20, x30 = ev.n, sv.n, st.n, ld.n, x2.n, x3.n
        scol = 0 if tag == "f" else 2
        with nc.Block() as blk:
            def _(sync):
                for c in range(nchunk):
                    if c >= 2:
                        sync.wait_ge(sv.sem, s0 + 2 * (c - 2) + 2)
                    ld.dma(sync.dma_start(out=AB[:, (c % 2) * 4096:(c % 2) * 4096 + width],
                                          in_=d[wname].ap()[c * 128:(c + 1) * 128, :]))
                    ld.wait(sync)
                    x2.bump(sync)
            blk.sync(_)
            def _(s):
                for c in range(nchunk):
                    sl = c % 2
                    s.wait_ge(x2.sem, x20 + c + 1)
                    if c >= 2:
                        s.wait_ge(ev.sem, e0 + 18 * (c - 1))
                        s.wait_ge(x3.sem, x30 + 18 * (c - 1))
                    sv.inc(s.activation(Z[:, sl * width:sl * width + width],
                                        AB[:, sl * 4096:sl * 4096 + width],
                                        AF.Identity, scale=SCL[:, scol:scol + 1], bias=B85[:, 0:1]))
                    s.wait_ge(sv.sem, s0 + 2 * c + 1)
                    sv.inc(s.activation(R[:, sl * width:sl * width + width],
                                        AB[:, sl * 4096:sl * 4096 + width],
                                        AF.Identity, scale=SCL[:, scol:scol + 1], bias=B85[:, 0:1]))
            blk.scalar(_)
            def _(v):
                for c in range(nchunk):
                    sl = c % 2
                    v.wait_ge(sv.sem, s0 + 2 * c + 2)
                    _f1_ladder(v, ev,
                               R[:, sl * width:sl * width + half],
                               M[:, 0:half],
                               Z[:, sl * width:sl * width + half],
                               WI[:, c * width:c * width + half], half)
            blk.vector(_)
            def _(g):
                for c in range(nchunk):
                    sl = c % 2
                    g.wait_ge(sv.sem, s0 + 2 * c + 2)
                    _f1_ladder(g, x3,
                               R[:, sl * width + half:sl * width + width],
                               M[:, half:width],
                               Z[:, sl * width + half:sl * width + width],
                               WI[:, c * width + half:c * width + width], half)
            blk.gpsimd(_)
        for g_ in (gM, gR, gZ):
            g_.__exit__(None, None, None)

    quant_stage("f", 8, 4096, "WfcT")

    gXT, XT = sbuf(nc, "f1XT", 128, 8 * N, BF16)
    gAF, AFc = sbuf(nc, "f1AF", 128, 8 * 16, BF16)
    gBFf, BFf = sbuf(nc, "f1BFf", 16, 4096, F32)
    gBFS, BFS = sbuf(nc, "f1BFS", 16, 4096, BF16)
    gBRf, BRf = sbuf(nc, "f1BRf", 1, 4096, F32)
    gBFR, BFR = sbuf(nc, "f1BFR", 1, 4096, BF16)
    gONE, ONE = sbuf(nc, "f1ONE", 1, 128, BF16)
    gIDN, IDN = sbuf(nc, "f1IDN", 128, 128, BF16)
    gXAe, XAe = sbuf(nc, "f1XAe", 128, 2 * 16, BF16)
    gXAT, XAT = sbuf(nc, "f1XAT", 16, N, BF16)
    gMD, MD = sbuf(nc, "f1MD", 128, 2 * 512, BF16)
    pAg = psum(nc, "f1pA", 128, 16)
    pTg = psum(nc, "f1pT", 16, 128, BF16)
    pMg = psum(nc, "f1pM", 128, 2 * 512)
    pA, pT, pM = pAg[1], pTg[1], pMg[1]
    m0, s0, e0, t0, d0, l0, x20, x30 = mm.n, sv.n, ev.n, st.n, ld.n, lg.n, x2.n, x3.n
    sp = s0 + 2
    xae = {t: sp + 10 * t + 1 for t in range(NT)}
    xat = {t: sp + 10 * t + 2 for t in range(NT)}
    mdA = {(t, cop): sp + 10 * t + 3 + cop for t in range(NT) for cop in range(8)}
    loraA = {t: m0 + 10 * t + 1 for t in range(NT)}
    trA = {t: m0 + 10 * t + 2 for t in range(NT)}
    grpA = {(t, cop): m0 + 10 * t + 3 + cop for t in range(NT) for cop in range(8)}
    stb = {(t, cop): x20 + 1 + 8 * t + cop + 1 for t in range(NT) for cop in range(8)}
    with nc.Block() as blk:
        def _(sync):
            for c in range(8):
                ld.dma(sync.dma_start(out=XT[:, c * N:(c + 1) * N],
                                      in_=d["sc_nbf"].ap()[:, c * 128:(c + 1) * 128],
                                      transpose=True))
            ld.wait(sync)
            x2.bump(sync)
            for t in range(NT):
                for cop in range(8):
                    sync.wait_ge(sv.sem, mdA[(t, cop)])
                    st.dma(sync.dma_start(
                        out=d["sc_bbf"].ap()[t * 128:(t + 1) * 128, cop * 512:(cop + 1) * 512],
                        in_=MD[:, ((8 * t + cop) % 2) * 512:((8 * t + cop) % 2 + 1) * 512]))
                    st.wait(sync)
                    x2.bump(sync)
        blk.sync(_)
        def _(g):
            lg.dma(g.dma_start(out=AFc[:, :],
                               in_=ap2(d["AfcT"].ap(), 0, [[16, 128], [128 * 16, 8], [1, 16]])))
            lg.dma(g.dma_start(out=BFf[:, :], in_=d["BfcT"].ap()))
            lg.dma(g.dma_start(out=BRf[0:1, :], in_=d["bfcr"].ap()))
            lg.dma(g.dma_start(out=IDN[:, :], in_=d["idnbf"].ap()))
            lg.wait(g)
            x3.bump(g)
        blk.gpsimd(_)
        def _(v):
            ev.inc(v.memset(ONE[0:1, :], 1.0))
        blk.vector(_)
        def _(s):
            s.wait_ge(x3.sem, x30 + 1)
            sv.inc(s.activation(BFS[:, :], BFf[:, :], AF.Identity, scale=SCL16[:, 0:1]))
            sv.inc(s.activation(BFR[0:1, :], BRf[0:1, :], AF.Identity, scale=SCL[0:1, 0:1]))
            for t in range(NT):
                sl = t % 2
                s.wait_ge(mm.sem, loraA[t])
                sv.inc(s.activation(XAe[:, sl * 16:(sl + 1) * 16], pA[:, :],
                                    AF.Identity, scale=2.0))
                s.wait_ge(mm.sem, trA[t])
                sv.inc(s.activation(XAT[:, t * 128:(t + 1) * 128], pT[:, :],
                                    AF.Identity))
                for cop in range(8):
                    tc = 8 * t + cop
                    s.wait_ge(mm.sem, grpA[(t, cop)])
                    if tc >= 2:
                        s.wait_ge(x2.sem, stb[((tc - 2) // 8, (tc - 2) % 8)])
                    sv.inc(s.activation(MD[:, (tc % 2) * 512:(tc % 2 + 1) * 512],
                                        pM[:, (tc % 2) * 512:(tc % 2 + 1) * 512],
                                        AF.Gelu, scale=SCL[:, 1:2]))
        blk.scalar(_)
        def _(t_):
            t_.wait_ge(x2.sem, x20 + 1)
            t_.wait_ge(x3.sem, x30 + 1)
            t_.wait_ge(ev.sem, e0 + 1)
            t_.wait_ge(sv.sem, sp)
            for t in range(NT):
                sl = t % 2
                if t >= 1:
                    t_.wait_ge(sv.sem, xae[t - 1])
                for c in range(8):
                    inst = t_.matmul(pA[:, :], XT[:, c * N + t * 128:c * N + (t + 1) * 128],
                                     AFc[:, c * 16:(c + 1) * 16],
                                     start=(c == 0), stop=(c == 7))
                mm.inc(inst)
                t_.wait_ge(sv.sem, xae[t])
                if t >= 1:
                    t_.wait_ge(sv.sem, xat[t - 1])
                mm.inc(t_.transpose(pT[:, :], XAe[:, sl * 16:(sl + 1) * 16], IDN[:, :]))
                for cop in range(8):
                    tc = 8 * t + cop
                    if tc >= 2:
                        t_.wait_ge(sv.sem, mdA[((tc - 2) // 8, (tc - 2) % 8)])
                    if cop == 0:
                        t_.wait_ge(sv.sem, xat[t])
                    for c in range(8):
                        t_.matmul(pM[:, (tc % 2) * 512:(tc % 2 + 1) * 512],
                                  XT[:, c * N + t * 128:c * N + (t + 1) * 128],
                                  WI[:, c * 4096 + cop * 512:c * 4096 + (cop + 1) * 512],
                                  start=(c == 0), stop=False, skip_group_check=True)
                    t_.matmul(pM[:, (tc % 2) * 512:(tc % 2 + 1) * 512],
                              XAT[:, t * 128:(t + 1) * 128],
                              BFS[:, cop * 512:(cop + 1) * 512],
                              start=False, stop=False, skip_group_check=True)
                    inst = t_.matmul(pM[:, (tc % 2) * 512:(tc % 2 + 1) * 512],
                                     ONE[0:1, :],
                                     BFR[0:1, cop * 512:(cop + 1) * 512],
                                     start=False, stop=True, skip_group_check=True)
                    mm.inc(inst)
        blk.tensor(_)
    for g_, _t in ((pMg, 0), (pTg, 0), (pAg, 0)):
        g_[0].__exit__(None, None, None)
    for g_ in (gMD, gXAT, gXAe, gIDN, gONE, gBFR, gBRf, gBFS, gBFf, gAF, gXT):
        g_.__exit__(None, None, None)

    quant_stage("p", 32, 1024, "WprojT")

    gMT, MT = sbuf(nc, "f1MT", 128, 2 * 4096, BF16)
    gAP2, APJ = sbuf(nc, "f1APJ", 128, 32 * 16, BF16)
    gBPf, BPf = sbuf(nc, "f1BPf", 16, 1024, F32)
    gBPS, BPS = sbuf(nc, "f1BPS", 16, 1024, BF16)
    gBRp, BRp = sbuf(nc, "f1BRp", 1, 1024, F32)
    gBPR, BPR = sbuf(nc, "f1BPR", 1, 1024, BF16)
    gON2, ONE2 = sbuf(nc, "f1ON2", 1, 128, BF16)
    gID2, IDN2 = sbuf(nc, "f1ID2", 128, 128, BF16)
    gXPe, XPe = sbuf(nc, "f1XPe", 128, 2 * 16, BF16)
    gXPT, XPT = sbuf(nc, "f1XPT", 16, N, BF16)
    gOF, OFb = sbuf(nc, "f1OF", 128, 2 * 1024, F32)
    pA2g = psum(nc, "f1pA2", 128, 16)
    pT2g = psum(nc, "f1pT2", 16, 128, BF16)
    pPg = psum(nc, "f1pP", 128, 2 * 512)
    pA2, pT2, pP = pA2g[1], pT2g[1], pPg[1]
    m0, s0, e0, t0, d0, l0, x20, x30 = mm.n, sv.n, ev.n, st.n, ld.n, lg.n, x2.n, x3.n
    sp = s0 + 2
    xpe = {t: sp + 4 * t + 1 for t in range(NT)}
    xpt = {t: sp + 4 * t + 2 for t in range(NT)}
    ofA = {(t, cop): sp + 4 * t + 3 + cop for t in range(NT) for cop in range(2)}
    lorP = {t: m0 + 4 * t + 1 for t in range(NT)}
    trP = {t: m0 + 4 * t + 2 for t in range(NT)}
    grpP = {(t, cop): m0 + 4 * t + 3 + cop for t in range(NT) for cop in range(2)}
    with nc.Block() as blk:
        def _(sync):
            nb = 0
            for t in range(NT):
                if t >= 2:
                    sync.wait_ge(mm.sem, grpP[(t - 2, 1)])
                for c in range(32):
                    ld.dma(sync.dma_start(
                        out=MT[:, (t % 2) * 4096 + c * 128:(t % 2) * 4096 + (c + 1) * 128],
                        in_=d["sc_bbf"].ap()[t * 128:(t + 1) * 128, c * 128:(c + 1) * 128],
                        transpose=True))
                ld.wait(sync)
                nb += 1
                sync.sem_inc(x2.sem, 1)
                if t >= 1:
                    sync.wait_ge(sv.sem, ofA[(t - 1, 1)])
                    st.dma(sync.dma_start(out=d["sc_tmp"].ap()[(t - 1) * 128:t * 128, :],
                                          in_=OFb[:, ((t - 1) % 2) * 1024:((t - 1) % 2 + 1) * 1024]))
                    st.wait(sync)
                    nb += 1
                    sync.sem_inc(x2.sem, 1)
            sync.wait_ge(sv.sem, ofA[(NT - 1, 1)])
            st.dma(sync.dma_start(out=d["sc_tmp"].ap()[(NT - 1) * 128:NT * 128, :],
                                  in_=OFb[:, ((NT - 1) % 2) * 1024:((NT - 1) % 2 + 1) * 1024]))
            st.wait(sync)
            nb += 1
            sync.sem_inc(x2.sem, 1)
            x2.n += nb
        blk.sync(_)
        def _(g):
            lg.dma(g.dma_start(out=APJ[:, :],
                               in_=ap2(d["AprojT"].ap(), 0, [[16, 128], [128 * 16, 32], [1, 16]])))
            lg.dma(g.dma_start(out=BPf[:, :], in_=d["BprojT"].ap()))
            lg.dma(g.dma_start(out=BRp[0:1, :], in_=d["bproj"].ap()))
            lg.dma(g.dma_start(out=IDN2[:, :], in_=d["idnbf"].ap()))
            lg.wait(g)
            x3.bump(g)
        blk.gpsimd(_)
        def _(v):
            ev.inc(v.memset(ONE2[0:1, :], 1.0))
        blk.vector(_)
        def _(s):
            s.wait_ge(x3.sem, x30 + 1)
            sv.inc(s.activation(BPS[:, :], BPf[:, :], AF.Identity, scale=SCL16[:, 1:2]))
            sv.inc(s.activation(BPR[0:1, :], BRp[0:1, :], AF.Identity, scale=SCL[0:1, 2:3]))
            for t in range(NT):
                sl = t % 2
                s.wait_ge(mm.sem, lorP[t])
                sv.inc(s.activation(XPe[:, sl * 16:(sl + 1) * 16], pA2[:, :],
                                    AF.Identity, scale=2.0))
                s.wait_ge(mm.sem, trP[t])
                sv.inc(s.activation(XPT[:, t * 128:(t + 1) * 128], pT2[:, :], AF.Identity))
                for cop in range(2):
                    s.wait_ge(mm.sem, grpP[(t, cop)])
                    if t >= 2 and cop == 0:
                        s.wait_ge(x2.sem, x20 + 2 * t - 1)
                    sv.inc(s.activation(OFb[:, sl * 1024 + cop * 512:sl * 1024 + (cop + 1) * 512],
                                        pP[:, cop * 512:(cop + 1) * 512],
                                        AF.Identity, scale=SCL[:, 3:4]))
        blk.scalar(_)
        def _(t_):
            t_.wait_ge(x3.sem, x30 + 1)
            t_.wait_ge(ev.sem, e0 + 1)
            t_.wait_ge(sv.sem, sp)
            mt_abs = {}
            cur = x20
            for t in range(NT):
                cur += 1
                mt_abs[t] = cur
                if t >= 1:
                    cur += 1
            for t in range(NT):
                sl = t % 2
                t_.wait_ge(x2.sem, mt_abs[t])
                if t >= 1:
                    t_.wait_ge(sv.sem, xpe[t - 1])
                for c in range(32):
                    inst = t_.matmul(pA2[:, :],
                                     MT[:, sl * 4096 + c * 128:sl * 4096 + (c + 1) * 128],
                                     APJ[:, c * 16:(c + 1) * 16],
                                     start=(c == 0), stop=(c == 31))
                mm.inc(inst)
                t_.wait_ge(sv.sem, xpe[t])
                if t >= 1:
                    t_.wait_ge(sv.sem, xpt[t - 1])
                mm.inc(t_.transpose(pT2[:, :], XPe[:, sl * 16:(sl + 1) * 16], IDN2[:, :]))
                for cop in range(2):
                    if t >= 1:
                        t_.wait_ge(sv.sem, ofA[(t - 1, cop)])
                    if cop == 0:
                        t_.wait_ge(sv.sem, xpt[t])
                    for c in range(32):
                        t_.matmul(pP[:, cop * 512:(cop + 1) * 512],
                                  MT[:, sl * 4096 + c * 128:sl * 4096 + (c + 1) * 128],
                                  WI[:, c * 1024 + cop * 512:c * 1024 + (cop + 1) * 512],
                                  start=(c == 0), stop=False, skip_group_check=True)
                    t_.matmul(pP[:, cop * 512:(cop + 1) * 512],
                              XPT[:, t * 128:(t + 1) * 128],
                              BPS[:, cop * 512:(cop + 1) * 512],
                              start=False, stop=False, skip_group_check=True)
                    inst = t_.matmul(pP[:, cop * 512:(cop + 1) * 512],
                                     ONE2[0:1, :],
                                     BPR[0:1, cop * 512:(cop + 1) * 512],
                                     start=False, stop=True, skip_group_check=True)
                    mm.inc(inst)
        blk.tensor(_)
    for g_ in (pPg, pT2g, pA2g):
        g_[0].__exit__(None, None, None)
    for g_ in (gOF, gXPT, gXPe, gID2, gON2, gBPR, gBRp, gBPS, gBPf, gAP2, gMT):
        g_.__exit__(None, None, None)
    gAB.__exit__(None, None, None)
    for g_ in (gB85, gS16, gSCL, gWI):
        g_.__exit__(None, None, None)


def phase_load_x(k):
    nc, d = k.nc, k.d
    st = k.st
    with nc.Block() as blk:
        def _(sync):
            st.dma(sync.dma_start(out=d["sc_x"].ap(), in_=d["x_in"].ap()))
            st.wait(sync)
        blk.sync(_)


def store_tiles(k, T, dst, nt=NT, width=C):
    nc, st = k.nc, k.st
    with nc.Block() as blk:
        def _(sync):
            for n in range(nt):
                st.dma(sync.dma_start(out=k.d[dst].ap()[n * 128:(n + 1) * 128, 0:width],
                                      in_=T[:, n * width:(n + 1) * width]))
            st.wait(sync)
        blk.sync(_)


def build(nc, dbg=(), upto=None):
    _PHASE_IDS["ln"] = 0; _PHASE_IDS["add"] = 0
    k = K(nc, dbg=dbg)
    phase_load_x(k)
    phase_ln(k, want_f32=True)
    k.dump("nf", "sc_nf", N, C)
    if upto == "ln1":
        return k
    gH, H = sbuf(nc, "H", 128, NT * C, F32)
    if upto == "zconv":
        import os as _os
        phase_a0_conv(k, H, only=_os.environ.get("CONVONLY") or None)
        phase_a0_layer(k, H, 0, blocks=42, skip14=True)
        k.dump("l1zt", "sc_tmp", N, C)
        k.dump("nbf1", "sc_nbf", N, C, dt=BF16)
        gH.__exit__(None, None, None)
        return k
    if upto == "zctx":
        phase_a0_layer(k, H, 0, blocks=42, skip14=True)
        k.dump("l1zt", "sc_tmp", N, C)
        k.dump("nbf1", "sc_nbf", N, C, dt=BF16)
        gH.__exit__(None, None, None)
        return k
    phase_a0_conv(k, H)
    if upto == "h0":
        store_tiles(k, H, "sc_tmp")
        k.dump("h0", "sc_tmp", N, C)
        gH.__exit__(None, None, None)
        return k
    if upto == "l1q":
        phase_a0_layer(k, H, 0, blocks=3)
        k.dump("l1q", "sc_nbf", N, C, dt=BF16)
        gH.__exit__(None, None, None)
        return k
    if upto == "l1kv":
        phase_a0_layer(k, H, 0, blocks=4)
        k.dump("l1kv", "sc_misc", 1, C)
        k.dump("l1kv2", "sc_tmp", 256, C)
        gH.__exit__(None, None, None)
        return k
    if upto == "l1om":
        phase_a0_layer(k, H, 0, blocks=44)
        k.dump("l1om", "sc_misc", 1, C)
        gH.__exit__(None, None, None)
        return k
    if upto == "l1loop":
        phase_a0_layer(k, H, 0, blocks=42)
        k.dump("l1zt", "sc_tmp", N, C)
        k.dump("nbf1", "sc_nbf", N, C, dt=BF16)
        gH.__exit__(None, None, None)
        return k
    if upto == "l1mini":
        phase_a0_layer(k, H, 0, blocks=41)
        k.dump("l1zt", "sc_tmp", 384, C)
        gH.__exit__(None, None, None)
        return k
    if upto == "l1z":
        phase_a0_layer(k, H, 0, blocks=43)
        k.dump("l1z", "sc_misc", 4, C)
        k.dump("l1zt", "sc_tmp", 384, C)
        gH.__exit__(None, None, None)
        return k
    if upto == "l1of":
        phase_a0_layer(k, H, 0, blocks=45)
        k.dump("l1of", "sc_tmp", N, C)
        gH.__exit__(None, None, None)
        return k
    for li in range(4):
        phase_a0_layer(k, H, li)
        if upto == f"h{li+1}":
            store_tiles(k, H, "sc_tmp")
            k.dump(f"h{li+1}", "sc_tmp", N, C)
            gH.__exit__(None, None, None)
            return k
    phase_a0_mlp(k, H)
    store_tiles(k, H, "sc_tmp")
    if upto == "a0":
        k.dump("a0", "sc_tmp", N, C)
        gH.__exit__(None, None, None)
        return k
    gH.__exit__(None, None, None)
    phase_add(k)
    if upto == "x1":
        k.dump("x1", "sc_x", N, C)
        return k
    phase_ln(k, want_lo=True)
    phase_f0(k)
    if upto == "f0":
        k.dump("f0", "sc_tmp", N, C)
        return k
    phase_add(k)
    if upto == "x2":
        k.dump("x2", "sc_x", N, C)
        return k
    phase_ln(k, want_f32=True)
    phase_a1(k)
    if upto == "a1":
        k.dump("a1", "sc_tmp", N, C)
        return k
    phase_add(k)
    if upto == "x3":
        k.dump("x3", "sc_x", N, C)
        return k
    phase_ln(k)
    phase_f1(k)
    if upto == "f1":
        k.dump("f1", "sc_tmp", N, C)
        return k
    phase_add(k, final=True)
    return k


def kernel(**inputs):
    nc = bass.Bass("TRN2", target_bir_lowering=False)
    build(nc)
    m = host_prep(inputs)
    x = np.ascontiguousarray(np.asarray(inputs["L0_attn_x"], dtype=np.float32))
    in_maps = []
    for b in range(8):
        mm = dict(m)
        mm["x_in"] = x[b]
        in_maps.append(mm)
    res = run_bass_kernel_spmd(nc, in_maps, core_ids=list(range(8)))
    return np.stack([r["out"] for r in res.results])


# version 02
# speedup vs baseline: 1.0039x; 1.0039x over previous
import numpy as np
import ml_dtypes
import concourse.bass as bass
import concourse.mybir as mybir
from concourse.bass_types import AP
from concourse.bass_utils import run_bass_kernel_spmd

F32 = mybir.dt.float32
BF16 = mybir.dt.bfloat16
OP = mybir.AluOpType
AF = mybir.ActivationFunctionType
AX = mybir.AxisListType.X
BF = ml_dtypes.bfloat16

N = 2048
C = 1024
NT = N // 128
CC = C // 128
MAGIC = np.float32(3.0 * 2.0 ** 22)


def _conv1_lhsT(w):
    out = np.zeros((32, 5, 128, 128), np.float32)
    j = np.arange(32)[:, None]
    c = np.arange(128)[None, :]
    r = (32 * j + c // 4) % 128
    for t in range(5):
        out[j, t, r, c] = w[128 * j + c, 0, t]
    return out


def _conv2_lhsT(w):
    out = np.zeros((8, 4, 5, 128, 128), np.float32)
    for j2 in range(8):
        for icc in range(4):
            for t in range(5):
                icl = np.arange(128)
                ic = 512 * j2 + 128 * icc + icl
                ocg = ic // 4
                r = ic % 4
                oc = ocg - 128 * j2
                m = (oc >= 0) & (oc < 128)
                out[j2, icc, t, icl[m], oc[m]] = w[ocg[m], r[m], t]
    return out


def _rope_tables():
    inv = 1.0 / 10000.0 ** (np.arange(0, C, 2, dtype=np.float64) / C)
    freqs = np.arange(N, dtype=np.float64)[:, None] * inv[None, :]
    cos = np.repeat(np.cos(freqs), 2, axis=1).astype(np.float32)
    sin = np.repeat(np.sin(freqs), 2, axis=1).astype(np.float32)
    ssw = sin.copy()
    ssw[:, 1::2] = -sin[:, 1::2]
    return cos.astype(BF), ssw.astype(BF)


def _t5_bucket(d):
    nb = 16
    rel = (d > 0).astype(np.int64) * nb
    rp = np.abs(d)
    max_exact = nb // 2
    small = rp < max_exact
    with np.errstate(divide="ignore"):
        rl = max_exact + (
            np.log(np.maximum(rp, 1) / max_exact) / np.log(128.0 / max_exact) * (nb - max_exact)
        ).astype(np.int64)
    rl = np.minimum(rl, nb - 1)
    return rel + np.where(small, rp, rl)


def _onehotT():
    d = np.arange(4095) - 2047
    b = _t5_bucket(d)
    out = np.zeros((32, 4095), np.float32)
    out[b, np.arange(4095)] = 1.0
    return out


def host_prep(i):
    f = lambda a: np.ascontiguousarray(np.asarray(a), dtype=np.float32)
    T = lambda a: np.ascontiguousarray(np.asarray(a, dtype=np.float32).T)
    m = {}
    m["c1_lhsT"] = np.ascontiguousarray(_conv1_lhsT(f(i["L0_attn_conv1_w"])).transpose(2, 0, 1, 3)).reshape(128, 32 * 5 * 128)
    m["c2_lhsT"] = np.ascontiguousarray(_conv2_lhsT(f(i["L0_attn_conv2_w"])).transpose(3, 0, 1, 2, 4)).reshape(128, 8 * 4 * 5 * 128)
    m["c1_b"] = f(i["L0_attn_conv1_b"]).reshape(32, 128).T.copy()
    m["c2_b"] = f(i["L0_attn_conv2_b"]).reshape(8, 128).T.copy()
    m["ln0_g"] = f(i["L0_attn_ln0_g"]).reshape(1, C)
    m["ln0_b"] = f(i["L0_attn_ln0_b"]).reshape(1, C)
    m["Wq"] = f(i["L0_attn_Wq"])
    m["Wk"] = f(i["L0_attn_Wk"])
    m["Wv"] = f(i["L0_attn_Wv"])
    m["bq"] = f(i["L0_attn_bq"])
    m["bk"] = f(i["L0_attn_bk"])
    m["bv"] = f(i["L0_attn_bv"])
    m["lnA_g"] = f(i["L0_attn_lnA_g"])
    m["lnA_b"] = f(i["L0_attn_lnA_b"])
    m["W1"] = f(i["L0_attn_W1"])
    m["b1"] = f(i["L0_attn_b1"]).reshape(32, 128).T.copy()
    m["W2"] = f(i["L0_attn_W2"])
    m["b2"] = f(i["L0_attn_b2"]).reshape(1, C)
    m["ln2_g"] = f(i["L0_attn_ln2_g"]).reshape(1, C)
    m["ln2_b"] = f(i["L0_attn_ln2_b"]).reshape(1, C)
    m["upT"] = np.ascontiguousarray(np.asarray(i["L0_ffn_up"], dtype=np.float32).transpose(0, 2, 1))
    m["gateT"] = np.ascontiguousarray(np.asarray(i["L0_ffn_gate"], dtype=np.float32).transpose(0, 2, 1))
    m["downT"] = np.ascontiguousarray(np.asarray(i["L0_ffn_down"], dtype=np.float32).transpose(0, 2, 1))
    m["routerT"] = T(i["L0_ffn_router"])
    m["wusT"] = T(i["L0_ffn_w_up_s"])
    m["wgsT"] = T(i["L0_ffn_w_gate_s"])
    m["wdsT"] = T(i["L0_ffn_w_down_s"])
    m["lnw1"] = f(i["L1_attn_ln_weight"]).reshape(1, C)
    m["wq1"] = f(i["L1_attn_wq"])
    m["wk1"] = f(i["L1_attn_wk"])
    m["wv1"] = f(i["L1_attn_wv"])
    m["wo1"] = f(i["L1_attn_wo"])
    m["relb"] = f(i["L1_attn_rel_bias"])
    m["onehotT"] = np.ascontiguousarray(_onehotT()[:, ::-1])
    m["WfcT"] = T(i["L1_ffn_W_fc"])
    m["bfc"] = f(i["L1_ffn_b_fc"]).reshape(32, 128).T.copy()
    m["bfcr"] = f(i["L1_ffn_b_fc"]).reshape(1, 4 * C)
    m["AfcT"] = T(i["L1_ffn_A_fc"])
    m["BfcT"] = T(i["L1_ffn_B_fc"])
    m["WprojT"] = T(i["L1_ffn_W_proj"])
    m["bproj"] = f(i["L1_ffn_b_proj"]).reshape(1, C)
    m["AprojT"] = T(i["L1_ffn_A_proj"])
    m["BprojT"] = T(i["L1_ffn_B_proj"])
    cos, ssw = _rope_tables()
    m["ropeC"] = cos
    m["ropeS"] = ssw
    m["idnbf"] = np.eye(128, dtype=BF)
    m["revbf"] = np.eye(128, dtype=BF)[::-1].copy()
    return m



class Ctr:
    def __init__(self, nc, name):
        self._guard = nc.semaphore(name)
        self.sem = self._guard.__enter__()
        self.n = 0

    def inc(self, inst, k=1):
        inst.then_inc(self.sem, k)
        self.n += k
        return inst

    def dma(self, inst):
        return self.inc(inst, 16)

    def wait(self, eng, n=None):
        eng.wait_ge(self.sem, self.n if n is None else n)

    def bump(self, eng):
        eng.sem_inc(self.sem, 1)
        self.n += 1


def ap2(t, off, pattern):
    a = t if isinstance(t, AP) else t.ap()
    return AP(tensor=a.tensor, offset=off, ap=pattern)


class K:

    def __init__(self, nc, dbg=()):
        self.nc = nc
        self.dbg = set(dbg)
        self.stack = []
        d = self.d = {}
        d["x_in"] = nc.dram_tensor("x_in", [N, C], F32, kind="ExternalInput")
        d["out"] = nc.dram_tensor("out", [N, C], F32, kind="ExternalOutput")
        for name, sh, dt in [
            ("c1_lhsT", [128, 32 * 5 * 128], F32), ("c2_lhsT", [128, 8 * 4 * 5 * 128], F32),
            ("c1_b", [128, 32], F32), ("c2_b", [128, 8], F32),
            ("ln0_g", [1, C], F32), ("ln0_b", [1, C], F32),
            ("Wq", [4, C, C], F32), ("Wk", [4, C, C], F32), ("Wv", [4, C, C], F32),
            ("bq", [4, C], F32), ("bk", [4, C], F32), ("bv", [4, C], F32),
            ("lnA_g", [4, C], F32), ("lnA_b", [4, C], F32),
            ("W1", [C, 4 * C], F32), ("b1", [128, 32], F32),
            ("W2", [4 * C, C], F32), ("b2", [1, C], F32),
            ("ln2_g", [1, C], F32), ("ln2_b", [1, C], F32),
            ("upT", [8, C, 2 * C], F32), ("gateT", [8, C, 2 * C], F32),
            ("downT", [8, 2 * C, C], F32), ("routerT", [C, 8], F32),
            ("wusT", [C, 2 * C], F32), ("wgsT", [C, 2 * C], F32), ("wdsT", [2 * C, C], F32),
            ("lnw1", [1, C], F32),
            ("wq1", [C, C], F32), ("wk1", [C, C], F32), ("wv1", [C, C], F32), ("wo1", [C, C], F32),
            ("relb", [32, 16], F32), ("onehotT", [32, 4095], F32),
            ("WfcT", [C, 4 * C], F32), ("bfc", [128, 32], F32), ("bfcr", [1, 4 * C], F32),
            ("AfcT", [C, 16], F32), ("BfcT", [16, 4 * C], F32),
            ("WprojT", [4 * C, C], F32), ("bproj", [1, C], F32),
            ("AprojT", [4 * C, 16], F32), ("BprojT", [16, C], F32),
            ("ropeC", [N, C], BF16), ("ropeS", [N, C], BF16),
            ("idnbf", [128, 128], BF16), ("revbf", [128, 128], BF16),
        ]:
            d[name] = nc.dram_tensor(name, sh, dt, kind="ExternalInput")
        d["sc_x"] = nc.dram_tensor("sc_x", [N, C], F32, kind="Internal")
        d["sc_tmp"] = nc.dram_tensor("sc_tmp", [N, C], F32, kind="Internal")
        d["sc_nf"] = nc.dram_tensor("sc_nf", [N, C], F32, kind="Internal")
        d["sc_nbf"] = nc.dram_tensor("sc_nbf", [N, C], BF16, kind="Internal")
        d["sc_lbf"] = nc.dram_tensor("sc_lbf", [N, C], BF16, kind="Internal")
        d["sc_bbf"] = nc.dram_tensor("sc_bbf", [N, 4 * C], BF16, kind="Internal")
        d["sc_misc"] = nc.dram_tensor("sc_misc", [64, 4096], F32, kind="Internal")
        d["sc_bias"] = nc.dram_tensor("sc_bias", [16, 4095], F32, kind="Internal")
        self.ld = Ctr(nc, "ld")
        self.lg = Ctr(nc, "lg")
        self.mm = Ctr(nc, "mm")
        self.ev = Ctr(nc, "ev")
        self.sv = Ctr(nc, "sv")
        self.st = Ctr(nc, "st")
        self.x2 = Ctr(nc, "x2")
        self.x3 = Ctr(nc, "x3")

    def dump(self, name, src, rows, cols, dt=F32):
        if name not in self.dbg:
            return
        src_dram = self.d[src]
        o = self.nc.dram_tensor("dbg_" + name, [rows, cols], dt, kind="ExternalOutput")
        st = self.st
        with self.nc.Block() as blk:
            def _(sync):
                st.dma(sync.dma_start(out=o.ap(), in_=src_dram.ap()[0:rows, 0:cols]))
                st.wait(sync)
            blk.sync(_)


def sbuf(nc, name, p, f, dt):
    g = nc.sbuf_tensor(name, [p, f], dt)
    t = g.__enter__()
    return g, t


def psum(nc, name, p, f, dt=F32):
    g = nc.psum_tensor(name, [p, f], dt)
    t = g.__enter__()
    return g, t



_PHASE_IDS = {"ln": 0, "add": 0}

def phase_ln(k, want_f32=False, eps=1e-5, want_lo=False):
    pid = _PHASE_IDS["ln"]; _PHASE_IDS["ln"] += 1
    nc = k.nc
    d = k.d
    gx, X = sbuf(nc, f"lnX{pid}", 128, NT * C, F32)
    gs, S = sbuf(nc, f"lnS{pid}", 128, 4 * NT, F32)
    gb, Xb = sbuf(nc, f"lnXb{pid}", 128, NT * C, BF16)
    gf, Xf = (None, None)
    gl, Xl = (None, None)
    if want_f32 or want_lo:
        want_f32 = True
        gf, Xf = sbuf(nc, f"lnXf{pid}", 128, NT * C, F32)
    if want_lo:
        gl, Xl = sbuf(nc, f"lnXl{pid}", 128, NT * C, BF16)
    ld, ev, sv, st = k.ld, k.ev, k.sv, k.st
    with nc.Block() as blk:
        def _(sync):
            for n in range(NT):
                ld.dma(sync.dma_start(out=X[:, n * C:(n + 1) * C],
                                      in_=d["sc_x"].ap()[n * 128:(n + 1) * 128, :]))
        blk.sync(_)
        def _(v):
            v.memset(S[:, 3 * NT:4 * NT], 0.0)
            ld.wait(v)
            for n in range(NT):
                xa = X[:, n * C:(n + 1) * C]
                inst = v.reduce_sum(S[:, n:n + 1], xa, axis=AX)
            ev.inc(inst); ev.wait(v)
            ev.inc(v.tensor_scalar_mul(S[:, 0:NT], S[:, 0:NT], 1.0 / C)); ev.wait(v)
            for n in range(NT):
                xa = X[:, n * C:(n + 1) * C]
                inst = v.scalar_tensor_tensor(Xb[:, n * C:(n + 1) * C], xa, S[:, n:n + 1], xa,
                                              OP.subtract, OP.mult,
                                              accum_out=S[:, 3 * NT + n:3 * NT + n + 1])
            ev.inc(inst); ev.wait(v)
            inst = v.tensor_scalar(S[:, NT:2 * NT], S[:, 3 * NT:4 * NT], 1.0 / C, eps,
                                   OP.mult, OP.add)
            ev.inc(inst)
        blk.vector(_)
        def _(s):
            ev.wait(s)
            sv.inc(s.activation(S[:, NT:2 * NT], S[:, NT:2 * NT], AF.Ln)); sv.wait(s)
            sv.inc(s.activation(S[:, NT:2 * NT], S[:, NT:2 * NT], AF.Exp, scale=-0.5))
        blk.scalar(_)
        def _(v):
            sv.wait(v)
            ev.inc(v.tensor_tensor(S[:, 2 * NT:3 * NT], S[:, 0:NT], S[:, NT:2 * NT], OP.mult))
            ev.wait(v)
            ev.inc(v.tensor_scalar_mul(S[:, 2 * NT:3 * NT], S[:, 2 * NT:3 * NT], -1.0))
        blk.vector(_)
        def _(s):
            ev.wait(s)
            for n in range(NT):
                xa = X[:, n * C:(n + 1) * C]
                inst = s.activation(Xb[:, n * C:(n + 1) * C], xa, AF.Identity,
                                    bias=S[:, 2 * NT + n:2 * NT + n + 1],
                                    scale=S[:, NT + n:NT + n + 1])
                if want_f32:
                    inst = s.activation(Xf[:, n * C:(n + 1) * C], xa, AF.Identity,
                                        bias=S[:, 2 * NT + n:2 * NT + n + 1],
                                        scale=S[:, NT + n:NT + n + 1])
            sv.inc(inst)
        blk.scalar(_)
        if want_lo:
            def _(v):
                sv.wait(v)
                for n in range(NT):
                    inst = v.tensor_sub(Xl[:, n * C:(n + 1) * C], Xf[:, n * C:(n + 1) * C],
                                        Xb[:, n * C:(n + 1) * C])
                ev.inc(inst)
            blk.vector(_)
        def _(sync):
            sv.wait(sync)
            if want_lo:
                ev.wait(sync)
            for n in range(NT):
                st.dma(sync.dma_start(out=d["sc_nbf"].ap()[n * 128:(n + 1) * 128, :],
                                      in_=Xb[:, n * C:(n + 1) * C]))
                if want_f32:
                    st.dma(sync.dma_start(out=d["sc_nf"].ap()[n * 128:(n + 1) * 128, :],
                                          in_=Xf[:, n * C:(n + 1) * C]))
                if want_lo:
                    st.dma(sync.dma_start(out=d["sc_lbf"].ap()[n * 128:(n + 1) * 128, :],
                                          in_=Xl[:, n * C:(n + 1) * C]))
            st.wait(sync)
        blk.sync(_)
    for g in (gl, gf, gb, gs, gx):
        if g is not None:
            g.__exit__(None, None, None)



def phase_add(k, final=False):
    pid = _PHASE_IDS["add"]; _PHASE_IDS["add"] += 1
    nc, d = k.nc, k.d
    ga, A = sbuf(nc, f"addA{pid}", 128, NT * C, F32)
    gb, Bt = sbuf(nc, f"addB{pid}", 128, NT * C, F32)
    ld, ev, st = k.ld, k.ev, k.st
    dst = d["out"] if final else d["sc_x"]
    with nc.Block() as blk:
        def _(sync):
            for n in range(NT):
                ld.dma(sync.dma_start(out=A[:, n * C:(n + 1) * C],
                                      in_=d["sc_x"].ap()[n * 128:(n + 1) * 128, :]))
                ld.dma(sync.dma_start(out=Bt[:, n * C:(n + 1) * C],
                                      in_=d["sc_tmp"].ap()[n * 128:(n + 1) * 128, :]))
        blk.sync(_)
        def _(v):
            ld.wait(v)
            for n in range(NT):
                inst = v.tensor_add(A[:, n * C:(n + 1) * C], A[:, n * C:(n + 1) * C],
                                    Bt[:, n * C:(n + 1) * C])
            ev.inc(inst)
        blk.vector(_)
        def _(sync):
            ev.wait(sync)
            for n in range(NT):
                st.dma(sync.dma_start(out=dst.ap()[n * 128:(n + 1) * 128, :],
                                      in_=A[:, n * C:(n + 1) * C]))
            st.wait(sync)
        blk.sync(_)
    gb.__exit__(None, None, None)
    ga.__exit__(None, None, None)



def ln_gb_resident(k, blk, src, dst, gbc, bbc, S, garb, eps, dst_bf=None, nt=NT):
    ev, sv = k.ev, k.sv

    def _(v):
        ev.wait(v)
        v.memset(S[:, 3 * nt:4 * nt], 0.0)
        for n in range(nt):
            inst = v.reduce_sum(S[:, n:n + 1], src(n), axis=AX)
        ev.inc(inst); ev.wait(v)
        ev.inc(v.tensor_scalar_mul(S[:, 0:nt], S[:, 0:nt], 1.0 / C)); ev.wait(v)
        for n in range(nt):
            inst = v.scalar_tensor_tensor(garb(n), src(n), S[:, n:n + 1], src(n),
                                          OP.subtract, OP.mult,
                                          accum_out=S[:, 3 * nt + n:3 * nt + n + 1])
        ev.inc(inst); ev.wait(v)
        ev.inc(v.tensor_scalar(S[:, nt:2 * nt], S[:, 3 * nt:4 * nt], 1.0 / C, eps,
                               OP.mult, OP.add))
    blk.vector(_)

    def _(s):
        ev.wait(s)
        sv.inc(s.activation(S[:, nt:2 * nt], S[:, nt:2 * nt], AF.Ln)); sv.wait(s)
        sv.inc(s.activation(S[:, nt:2 * nt], S[:, nt:2 * nt], AF.Exp, scale=-0.5))
    blk.scalar(_)

    def _(v):
        sv.wait(v)
        ev.inc(v.tensor_tensor(S[:, 2 * nt:3 * nt], S[:, 0:nt], S[:, nt:2 * nt], OP.mult))
        ev.wait(v)
        ev.inc(v.tensor_scalar_mul(S[:, 2 * nt:3 * nt], S[:, 2 * nt:3 * nt], -1.0))
    blk.vector(_)

    def _(s):
        ev.wait(s)
        for n in range(nt):
            inst = s.activation(dst(n), src(n), AF.Identity,
                                bias=S[:, 2 * nt + n:2 * nt + n + 1],
                                scale=S[:, nt + n:nt + n + 1])
        sv.inc(inst)
    blk.scalar(_)

    def _(v):
        sv.wait(v)
        inst = None
        if gbc is not None:
            for n in range(nt):
                inst = v.tensor_tensor(dst(n), dst(n), gbc, OP.mult)
            ev.inc(inst); ev.wait(v)
            for n in range(nt):
                inst = v.tensor_tensor(dst(n), dst(n), bbc, OP.add)
        if dst_bf is not None:
            ev.inc(inst); ev.wait(v)
            for n in range(nt):
                inst = v.tensor_copy(dst_bf(n), dst(n))
        if inst is None:
            inst = v.tensor_copy(S[:, 0:1], S[:, 0:1])
        ev.inc(inst)
    blk.vector(_)



FMS = 2080

def phase_a0_conv(k, H, only=None):
    nc, d = k.nc, k.d
    ld, lg, mm, ev, sv, st = k.ld, k.lg, k.mm, k.ev, k.sv, k.st
    gfm, fmA = sbuf(nc, "cfmA", 128, 8 * FMS, BF16)
    gc1, c1w = sbuf(nc, "c1w", 128, 32 * 5 * 128, BF16)
    gc2, c2w = sbuf(nc, "c2w", 128, 8 * 4 * 5 * 128, BF16)
    gb1, c1b = sbuf(nc, "c1b", 128, 32, F32)
    gb2, c2b = sbuf(nc, "c2b", 128, 8, F32)
    gh1, h1 = sbuf(nc, "h1", 128, 4 * FMS, BF16)
    gcf, cfm = sbuf(nc, "cfm", 128, 2 * 2048, BF16)
    gp = [psum(nc, f"cp{i}", 128, 512) for i in range(4)]
    banks = [t for _, t in gp]

    events = []
    for j2 in range(8):
        for j in range(4 * j2, 4 * j2 + 4):
            for p in range(4):
                events.append(("c1", j, p))
        for p in range(4):
            events.append(("c2", j2, p))

    x3 = k.x3
    mm0, sv0, st0, ld0, lg0, ev0, x30 = mm.n, sv.n, st.n, ld.n, lg.n, ev.n, x3.n
    if only == "ln0":
        events = []
    with nc.Block() as blk:
        def _(v):
            v.memset(fmA[:, :], 0.0)
            ev.inc(v.memset(h1[:, :], 0.0))
        blk.vector(_)

        def _(g):
            if only == "ln0":
                return
            lg.dma(g.dma_start(out=c1w[:, :], in_=d["c1_lhsT"].ap()))
            lg.dma(g.dma_start(out=c2w[:, :], in_=d["c2_lhsT"].ap()))
            lg.dma(g.dma_start(out=c1b[:, :], in_=d["c1_b"].ap()))
            lg.dma(g.dma_start(out=c2b[:, :], in_=d["c2_b"].ap()))
        blk.gpsimd(_)

        def _(sync):
            if only == "ln0":
                return
            ev.wait(sync)
            for j in range(8):
                ld.dma(sync.dma_start(
                    out=fmA[:, j * FMS + 16:j * FMS + 16 + 2048],
                    in_=d["sc_nbf"].ap()[:, j * 128:(j + 1) * 128], transpose=True))
        blk.sync(_)

        def _(t):
            if only == "ln0":
                return
            t.wait_ge(ld.sem, ld0 + 8 * 16)
            t.wait_ge(lg.sem, lg0 + 4 * 16)
            for e, evnt in enumerate(events):
                bank = banks[e % 4]
                if e >= 4:
                    t.wait_ge(sv.sem, sv0 + e - 3)
                kind = evnt[0]
                if kind == "c2":
                    t.wait_ge(sv.sem, sv0 + evnt[1] * 20 + 16)
                if kind == "c1":
                    _, j, p = evnt
                    for ti in range(5):
                        inst = t.matmul(bank[:, :],
                                        c1w[:, (j * 5 + ti) * 128:(j * 5 + ti + 1) * 128],
                                        fmA[:, (j // 4) * FMS + 14 + p * 512 + ti:
                                            (j // 4) * FMS + 14 + p * 512 + ti + 512],
                                        start=(ti == 0), stop=(ti == 4))
                else:
                    _, j2, p = evnt
                    first = True
                    for icc in range(4):
                        for ti in range(5):
                            inst = t.matmul(bank[:, :],
                                            c2w[:, ((j2 * 4 + icc) * 5 + ti) * 128:
                                                ((j2 * 4 + icc) * 5 + ti + 1) * 128],
                                            h1[:, icc * FMS + 14 + p * 512 + ti:
                                               icc * FMS + 14 + p * 512 + ti + 512],
                                            start=first, stop=(icc == 3 and ti == 4))
                            first = False
                mm.inc(inst)
        blk.tensor(_)

        def _(s):
            if only == "ln0":
                return
            for e, evnt in enumerate(events):
                bank = banks[e % 4]
                s.wait_ge(mm.sem, mm0 + e + 1)
                kind = evnt[0]
                if kind == "c1":
                    _, j, p = evnt
                    inst = s.activation(h1[:, (j % 4) * FMS + 16 + p * 512:
                                           (j % 4) * FMS + 16 + p * 512 + 512],
                                        bank[:, :], AF.Gelu, bias=c1b[:, j:j + 1])
                else:
                    _, j2, p = evnt
                    if j2 >= 2 and p == 0:
                        s.wait_ge(x3.sem, x30 + j2 - 1)
                    inst = s.activation(cfm[:, (j2 % 2) * 2048 + p * 512:
                                            (j2 % 2) * 2048 + p * 512 + 512],
                                        bank[:, :], AF.Identity, bias=c2b[:, j2:j2 + 1])
                sv.inc(inst)
        blk.scalar(_)

        def _(sync):
            if only == "ln0":
                return
            for j2 in range(8):
                sync.wait_ge(sv.sem, sv0 + j2 * 20 + 20)
                st.dma(sync.dma_start(out=d["sc_bbf"].ap()[128 * j2:128 * (j2 + 1), 0:2048],
                                      in_=cfm[:, (j2 % 2) * 2048:(j2 % 2) * 2048 + 2048]))
                st.wait(sync)
                x3.bump(sync)
        blk.sync(_)

    for g, _t in reversed(gp):
        g.__exit__(None, None, None)
    for g in (gcf, gh1, gb2, gb1, gc2, gc1, gfm):
        g.__exit__(None, None, None)

    if only == "conv":
        for g in (gcf, gh1, gb2, gb1, gc2, gc1, gfm):
            pass
    if only == "conv":
        return
    gct, CT = sbuf(nc, "l0ct", 128, NT * C, BF16)
    gnf, NF = sbuf(nc, "l0nf", 128, NT * C, F32)
    gS, S = sbuf(nc, "l0S", 128, 4 * NT, F32)
    gg, GB = sbuf(nc, "l0gb", 128, 2 * C, F32)
    with nc.Block() as blk:
        def _(sync):
            for n in range(NT):
                ld.dma(sync.dma_start(out=CT[:, n * C:(n + 1) * C],
                                      in_=d["sc_bbf"].ap()[0:1024, n * 128:(n + 1) * 128],
                                      transpose=True))
        blk.sync(_)

        def _(g):
            for n in range(NT):
                lg.dma(g.dma_start(out=NF[:, n * C:(n + 1) * C],
                                   in_=d["sc_nf"].ap()[n * 128:(n + 1) * 128, :]))
            lg.dma(g.dma_start(out=GB[:, 0:C], in_=ap2(d["ln0_g"], 0, [[0, 128], [1, C]])))
            lg.dma(g.dma_start(out=GB[:, C:2 * C], in_=ap2(d["ln0_b"], 0, [[0, 128], [1, C]])))
        blk.gpsimd(_)

        def _(v):
            ld.wait(v)
            lg.wait(v)
            for n in range(NT):
                inst = v.tensor_add(NF[:, n * C:(n + 1) * C], NF[:, n * C:(n + 1) * C],
                                    CT[:, n * C:(n + 1) * C])
            ev.inc(inst); ev.wait(v)
        blk.vector(_)

        ln_gb_resident(k, blk,
                       src=lambda n: NF[:, n * C:(n + 1) * C],
                       dst=lambda n: H[:, n * C:(n + 1) * C],
                       gbc=GB[:, 0:C], bbc=GB[:, C:2 * C],
                       S=S, garb=lambda n: CT[:, n * C:(n + 1) * C], eps=1e-12)
    for g in (gg, gS, gnf, gct):
        g.__exit__(None, None, None)


def phase_a0_layer(k, H, li, blocks=5, skip14=False):
    nc, d = k.nc, k.d
    ld, lg, mm, ev, sv, st, x2 = k.ld, k.lg, k.mm, k.ev, k.sv, k.st, k.x2

    gHT, HT = sbuf(nc, f"a{li}HT", 128, 8 * N, BF16)
    gKV, KV = sbuf(nc, f"a{li}KV", 128, C, BF16)
    gKM, KMB = sbuf(nc, f"a{li}KMB", 128, C, F32)
    gW3, W3 = sbuf(nc, f"a{li}W3", 128, 3 * 8 * C, BF16)
    gBS, BS = sbuf(nc, f"a{li}BS", 1, 3 * C + 128, BF16)
    gOC, OC_ = sbuf(nc, f"a{li}OC", 128, 1, BF16)
    gID, IDN = sbuf(nc, f"a{li}IDN", 128, 128, BF16)
    gQS, QS = sbuf(nc, f"a{li}QS", 128, 2 * C, BF16)
    gKS, KS = sbuf(nc, f"a{li}KS", 128, C, BF16)
    gVS, VS = sbuf(nc, f"a{li}VS", 128, C, BF16)
    gR1, R1 = sbuf(nc, f"a{li}R1", 128, C, BF16)
    gR2, R2 = sbuf(nc, f"a{li}R2", 128, C, BF16)
    gQR, QRB = sbuf(nc, f"a{li}QR", 128, C, BF16)
    gRT, RT = sbuf(nc, f"a{li}RT", 128, 2 * 2 * C, BF16)
    gHB, HBS = sbuf(nc, f"a{li}HBS", 128, 2 * C, BF16)

    pQg = psum(nc, f"a{li}pQ", 128, 1024)
    pTg = [psum(nc, f"a{li}pT{i}", 128, 128, BF16) for i in range(2)]
    pKVg = psum(nc, f"a{li}pKV", 64, 1024)
    pKSg = psum(nc, f"a{li}pKS", 1, 1024)
    pQ = pQg[1]
    pT = [t for _, t in pTg]
    pKV = pKVg[1]
    pKS = pKSg[1]

    def Wsl(wi, c, lo, hi):
        base = (wi * 8 + c) * C
        return W3[:, base + lo:base + hi]


    if not skip14:
        x3 = k.x3
        ev0, st0, x30 = ev.n, st.n, x3.n
        with nc.Block() as blk:
            def _(v):
                for n in range(NT):
                    sl = n % 2
                    if n >= 2:
                        v.wait_ge(x3.sem, x30 + n - 1)
                    ev.inc(v.tensor_copy(HBS[:, sl * C:(sl + 1) * C], H[:, n * C:(n + 1) * C]))
                v.memset(BS[:, 3 * C:3 * C + 128], 1.0)
                v.memset(OC_[:, :], 1.0)
            blk.vector(_)

            def _(sync):
                for n in range(NT):
                    sl = n % 2
                    sync.wait_ge(ev.sem, ev0 + n + 1)
                    st.dma(sync.dma_start(out=d["sc_bbf"].ap()[n * 128:(n + 1) * 128, 0:C],
                                          in_=HBS[:, sl * C:(sl + 1) * C]))
                    st.wait(sync)
                    x3.bump(sync)
            blk.sync(_)

            def _(g):
                for wi, wname in enumerate(("Wq", "Wk", "Wv")):
                    src = d[wname].ap()[li]
                    lg.dma(g.dma_start(
                        out=W3[:, wi * 8 * C:(wi + 1) * 8 * C],
                        in_=ap2(src, src.offset, [[1024, 128], [128 * 1024, 8], [1, 1024]])))
                for bi, bname in enumerate(("bq", "bk", "bv")):
                    lg.dma(g.dma_start(out=BS[:, bi * C:(bi + 1) * C],
                                       in_=d[bname].ap()[li:li + 1, :]))
                lg.dma(g.dma_start(out=IDN[:, :], in_=d["idnbf"].ap()))
                lg.wait(g)
            blk.gpsimd(_)

        with nc.Block() as blk:
            def _(sync):
                for c in range(8):
                    ld.dma(sync.dma_start(out=HT[:, c * N:(c + 1) * N],
                                          in_=d["sc_bbf"].ap()[:, c * 128:(c + 1) * 128],
                                          transpose=True))
                ld.wait(sync)
            blk.sync(_)

        mm0, sv0, ev0, st0, ld0, x20, x30 = mm.n, sv.n, ev.n, st.n, ld.n, x2.n, x3.n
        with nc.Block() as blk:
            def _(sync):
                for t in range(NT):
                    sl = t % 2
                    if t >= 2:
                        sync.wait_ge(ev.sem, ev0 + 6 * (t - 2) + 6)
                    ld.dma(sync.dma_start(out=RT[:, sl * 2 * C:sl * 2 * C + C],
                                          in_=d["ropeC"].ap()[t * 128:(t + 1) * 128, :]))
                    ld.dma(sync.dma_start(out=RT[:, sl * 2 * C + C:sl * 2 * C + 2 * C],
                                          in_=d["ropeS"].ap()[t * 128:(t + 1) * 128, :]))
                    sync.wait_ge(ev.sem, ev0 + 6 * t + 1)
                    st.dma(sync.dma_start(out=d["sc_nbf"].ap()[t * 128:(t + 1) * 128, :],
                                          in_=QS[:, sl * C:(sl + 1) * C]))
                    st.wait(sync)
                    x3.bump(sync)
            blk.sync(_)

            def _(t_):
                for t in range(NT):
                    if t > 0:
                        t_.wait_ge(sv.sem, sv0 + 3 * t)
                    for pg in range(2):
                        for c in range(8):
                            t_.matmul(pQ[:, pg * 512:(pg + 1) * 512],
                                      HT[:, c * N + t * 128:c * N + (t + 1) * 128],
                                      Wsl(0, c, pg * 512, (pg + 1) * 512),
                                      start=(c == 0), stop=False)
                        inst = t_.matmul(pQ[:, pg * 512:(pg + 1) * 512],
                                         BS[0:1, 3 * C:3 * C + 128],
                                         BS[0:1, 0 * C + pg * 512:0 * C + (pg + 1) * 512],
                                         start=False, stop=True)
                    mm.inc(inst)
                    t_.wait_ge(sv.sem, sv0 + 3 * t + 1)
                    for pg in range(2):
                        for c in range(8):
                            t_.matmul(pQ[:, pg * 512:(pg + 1) * 512],
                                      HT[:, c * N + t * 128:c * N + (t + 1) * 128],
                                      Wsl(1, c, pg * 512, (pg + 1) * 512),
                                      start=(c == 0), stop=False)
                        inst = t_.matmul(pQ[:, pg * 512:(pg + 1) * 512],
                                         BS[0:1, 3 * C:3 * C + 128],
                                         BS[0:1, 1 * C + pg * 512:1 * C + (pg + 1) * 512],
                                         start=False, stop=True)
                    mm.inc(inst)
                    t_.wait_ge(sv.sem, sv0 + 3 * t + 2)
                    for pg in range(2):
                        for c in range(8):
                            t_.matmul(pQ[:, pg * 512:(pg + 1) * 512],
                                      HT[:, c * N + t * 128:c * N + (t + 1) * 128],
                                      Wsl(2, c, pg * 512, (pg + 1) * 512),
                                      start=(c == 0), stop=False)
                        inst = t_.matmul(pQ[:, pg * 512:(pg + 1) * 512],
                                         BS[0:1, 3 * C:3 * C + 128],
                                         BS[0:1, 2 * C + pg * 512:2 * C + (pg + 1) * 512],
                                         start=False, stop=True)
                    mm.inc(inst)
                    t_.wait_ge(ev.sem, ev0 + 6 * t + 3)
                    for c in range(8):
                        if 8 * t + c - 1 > 0:
                            t_.wait_ge(x2.sem, x20 + 8 * t + c - 1)
                        mm.inc(t_.transpose(pT[c % 2][:, :], QRB[:, c * 128:(c + 1) * 128],
                                            IDN[:, :]))
                    t_.wait_ge(ev.sem, ev0 + 6 * t + 6)
                    t_.wait_ge(sv.sem, sv0 + 3 * t + 3)
                    for h in range(16):
                        t_.matmul(pKV[0:64, h * 64:(h + 1) * 64],
                                  R1[:, h * 64:(h + 1) * 64],
                                  VS[:, h * 64:(h + 1) * 64],
                                  start=(t == 0 and h in (0, 8)), stop=(t == NT - 1),
                                  skip_group_check=True)
                        inst = t_.matmul(pKS[0:1, h * 64:(h + 1) * 64],
                                         OC_[:, 0:1],
                                         KS[:, h * 64:(h + 1) * 64],
                                         start=(t == 0 and h in (0, 8)), stop=(t == NT - 1),
                                         skip_group_check=True)
                    mm.inc(inst)
            blk.tensor(_)

            def _(s):
                for t in range(NT):
                    sl = t % 2
                    s.wait_ge(mm.sem, mm0 + 12 * t + 1)
                    if t >= 2:
                        s.wait_ge(x3.sem, x30 + t - 1)
                    sv.inc(s.activation(QS[:, sl * C:(sl + 1) * C], pQ[:, :], AF.Gelu))
                    s.wait_ge(mm.sem, mm0 + 12 * t + 2)
                    sv.inc(s.activation(KS[:, :], pQ[:, :], AF.Gelu))
                    s.wait_ge(mm.sem, mm0 + 12 * t + 3)
                    sv.inc(s.activation(VS[:, :], pQ[:, :], AF.Identity))
            blk.scalar(_)

            def _(v):
                for t in range(NT):
                    sl = t % 2
                    qs = QS[:, sl * C:(sl + 1) * C]
                    cosT = RT[:, sl * 2 * C:sl * 2 * C + C]
                    sswT = RT[:, sl * 2 * C + C:sl * 2 * C + 2 * C]
                    v.wait_ge(ld.sem, ld0 + 32 * t + 32)
                    v.wait_ge(sv.sem, sv0 + 3 * t + 1)
                    ev.inc(v.tensor_scalar_add(qs, qs, 0.21)); v.wait_ge(ev.sem, ev0 + 6 * t + 1)
                    v.tensor_tensor(R1[:, :], qs, cosT, OP.mult)
                    ev.inc(v.tensor_tensor(R2[:, :], qs, sswT, OP.mult))
                    v.wait_ge(ev.sem, ev0 + 6 * t + 2)
                    sw = AP(tensor=R2[:, :].tensor, offset=R2[:, :].offset + 1,
                            ap=list(R2[:, :].ap[:-1]) + [[2, C // 2], [-1, 2]])
                    ev.inc(v.tensor_tensor(QRB[:, :].rearrange("p (a b) -> p a b", b=2),
                                           R1[:, :].rearrange("p (a b) -> p a b", b=2),
                                           sw, OP.add))
                    v.wait_ge(sv.sem, sv0 + 3 * t + 2)
                    ev.inc(v.tensor_scalar_add(KS[:, :], KS[:, :], 0.21))
                    for c in range(8):
                        v.wait_ge(mm.sem, mm0 + 12 * t + 4 + c)
                        x2.inc(v.tensor_copy(HT[:, c * N + t * 128:c * N + (t + 1) * 128],
                                             pT[c % 2][:, :]))
                    v.wait_ge(ev.sem, ev0 + 6 * t + 4)
                    v.tensor_tensor(R2[:, :], KS[:, :], sswT, OP.mult)
                    ev.inc(v.tensor_tensor(QRB[:, :], KS[:, :], cosT, OP.mult))
                    v.wait_ge(ev.sem, ev0 + 6 * t + 5)
                    sw2 = AP(tensor=R2[:, :].tensor, offset=R2[:, :].offset + 1,
                             ap=list(R2[:, :].ap[:-1]) + [[2, C // 2], [-1, 2]])
                    ev.inc(v.tensor_tensor(R1[:, :].rearrange("p (a b) -> p a b", b=2),
                                           QRB[:, :].rearrange("p (a b) -> p a b", b=2),
                                           sw2, OP.add))
            blk.vector(_)

        if blocks < 4 or blocks == 4:
            pass
        if blocks < 4:
            pKSg[0].__exit__(None, None, None)
            pKVg[0].__exit__(None, None, None)
            for g, _t in reversed(pTg):
                g.__exit__(None, None, None)
            pQg[0].__exit__(None, None, None)
            for g in (gHB, gRT, gQR, gR2, gR1, gVS, gKS, gQS, gID, gOC, gBS, gW3):
                g.__exit__(None, None, None)
            for g in (gKM, gKV, gHT):
                g.__exit__(None, None, None)
            return
        with nc.Block() as blk:
            def _(v):
                ev.inc(v.memset(KV[:, :], 0.0))
            blk.vector(_)
            def _(s):
                mm.wait(s)
                ev.wait(s)
                for h in range(16):
                    sv.inc(s.activation(KV[64 * (h % 2):64 * (h % 2) + 64,
                                           h * 64:h * 64 + 64],
                                        pKV[0:64, h * 64:(h + 1) * 64],
                                        AF.Identity, scale=1.0 / N))
                sv.inc(s.activation(KMB[0:1, 0:C], pKS[0:1, :], AF.Identity, scale=1.0 / N))
            blk.scalar(_)
            def _(sync):
                sv.wait(sync)
                st.dma(sync.dma_start(out=d["sc_misc"].ap()[0:1, 0:C], in_=KMB[0:1, 0:C]))
                st.wait(sync)
            blk.sync(_)
            def _(g):
                st.wait(g)
                src = d["sc_misc"].ap()
                lg.dma(g.dma_start(out=KMB[:, :], in_=ap2(src, 0, [[0, 128], [1, C]])))
                lg.wait(g)
            blk.gpsimd(_)

        pKSg[0].__exit__(None, None, None)
        pKVg[0].__exit__(None, None, None)
        for g, _t in reversed(pTg):
            g.__exit__(None, None, None)
        pQg[0].__exit__(None, None, None)
        for g in (gHB, gRT, gQR, gR2, gR1, gVS, gKS, gQS, gID, gOC, gBS, gW3):
            g.__exit__(None, None, None)
        if blocks < 5:
            for g in (gKM, gKV, gHT):
                g.__exit__(None, None, None)
            return

    if skip14:
        pKSg[0].__exit__(None, None, None)
        pKVg[0].__exit__(None, None, None)
        for g_ in reversed(pTg):
            g_[0].__exit__(None, None, None)
        pQg[0].__exit__(None, None, None)
        for g_ in (gHB, gRT, gQR, gR2, gR1, gVS, gKS, gQS, gID, gOC, gBS, gW3):
            g_.__exit__(None, None, None)
    gOF, OF = sbuf(nc, f"a{li}OF", 128, NT * C, F32)
    gZT, ZT = sbuf(nc, f"a{li}ZT", 128, C, F32)
    gZC, ZC = sbuf(nc, f"a{li}ZC", 128, 32, F32)
    gS2, S2 = sbuf(nc, f"a{li}S2", 128, 4 * NT, F32)
    gGB2, GB2 = sbuf(nc, f"a{li}GB2", 128, 2 * C, F32)
    gQ2, Q2 = sbuf(nc, f"a{li}Q2", 128, 2 * C, BF16)
    pOg = psum(nc, f"a{li}pO", 128, 1024)
    pO = pOg[1]
    mm0, sv0, ev0, st0, lg0, x20b = mm.n, sv.n, ev.n, st.n, lg.n, x2.n
    with nc.Block() as blk:
        if blocks == 41:
            def _(sync):
                ld.dma(sync.dma_start(out=Q2[:, 0:C],
                                      in_=d["sc_nbf"].ap()[0:128, :]))
                ld.wait(sync)
                x2.bump(sync)
            blk.sync(_)
            def _(v):
                v.wait_ge(x2.sem, x2.n)
                ev.inc(v.tensor_tensor(ZT[:, :], Q2[:, 0:C], KMB[:, :], OP.mult))
            blk.vector(_)
            def _(sync):
                ev.wait(sync)
                st.dma(sync.dma_start(out=d["sc_tmp"].ap()[0:128, :], in_=ZT[:, :]))
                st.dma(sync.dma_start(out=d["sc_tmp"].ap()[128:256, :], in_=KMB[:, :]))
                st.wait(sync)
            blk.sync(_)
            def _(g):
                st.wait(g)
                lg.dma(g.dma_start(out=d["sc_tmp"].ap()[256:384, :], in_=Q2[:, 0:C]))
                lg.wait(g)
            blk.gpsimd(_)
        if blocks == 41:
            pass
        else:
          _dummy41 = None
        def _(sync):
            if blocks == 41:
                return
            for t in range(NT):
                sl = t % 2
                if t >= 2:
                    sync.wait_ge(ev.sem, ev0 + (t - 1) if blocks == 42 else ev0 + 6 * (t - 2) + 1)
                ld.dma(sync.dma_start(out=Q2[:, sl * C:(sl + 1) * C],
                                      in_=d["sc_nbf"].ap()[t * 128:(t + 1) * 128, :]))
                ld.wait(sync)
                x2.bump(sync)
            if blocks == 5:
                ga = d["lnA_g"].ap()[li:li + 1, :]
                ba = d["lnA_b"].ap()[li:li + 1, :]
                ld.dma(sync.dma_start(out=GB2[:, 0:C], in_=ap2(ga, ga.offset, [[0, 128], [1, C]])))
                ld.dma(sync.dma_start(out=GB2[:, C:2 * C], in_=ap2(ba, ba.offset, [[0, 128], [1, C]])))
                ld.wait(sync)
        blk.sync(_)

        def _(v):
            if blocks == 41:
                return
            for t in range(NT):
                sl = t % 2
                v.wait_ge(x2.sem, x20b + t + 1)
                if blocks == 42:
                    ev.inc(v.tensor_copy(OF[:, t * C:(t + 1) * C], Q2[:, sl * C:(sl + 1) * C]))
                    continue
                ev.inc(v.tensor_tensor(ZT[:, :], Q2[:, sl * C:(sl + 1) * C], KMB[:, :], OP.mult))
                v.wait_ge(ev.sem, ev0 + 6 * t + 1)
                ev.inc(v.tensor_reduce(ZC[:, 0:16].rearrange("p (h u) -> p h u", u=1),
                                       ZT[:, :].rearrange("p (h e) -> p h e", e=64),
                                       AX, OP.add))
                v.wait_ge(ev.sem, ev0 + 6 * t + 2)
                ev.inc(v.tensor_scalar_add(ZC[:, 0:16], ZC[:, 0:16], 1e-6))
                v.wait_ge(ev.sem, ev0 + 6 * t + 3)
                ev.inc(v.reciprocal(ZC[:, 16:32], ZC[:, 0:16]))
                if blocks in (43, 44):
                    ev.inc(v.tensor_copy(ZC[:, 0:1], ZC[:, 0:1]))
                    v.wait_ge(ev.sem, ev0 + 6 * t + 5)
                    ev.inc(v.tensor_copy(ZC[:, 1:2], ZC[:, 1:2]))
                    continue
                v.wait_ge(mm.sem, mm0 + t + 1)
                for h in range(16):
                    inst = v.tensor_scalar(OF[:, t * C + h * 64:t * C + (h + 1) * 64],
                                           pO[:, h * 64:(h + 1) * 64],
                                           ZC[:, 16 + h:17 + h], 0.0, OP.mult, OP.add)
                ev.inc(inst)
                v.wait_ge(ev.sem, ev0 + 6 * t + 5)
                ev.inc(v.tensor_add(OF[:, t * C:(t + 1) * C], OF[:, t * C:(t + 1) * C],
                                    H[:, t * C:(t + 1) * C]))
        blk.vector(_)

        def _(t_):
            if blocks in (41, 42, 43):
                return
            for t in range(NT):
                t_.wait_ge(ev.sem, ev0 + 6 * t + 4)
                if t > 0:
                    t_.wait_ge(ev.sem, ev0 + 6 * (t - 1) + 5)
                for h in range(16):
                    inst = t_.matmul(pO[:, h * 64:(h + 1) * 64],
                                     HT[:, (h // 2) * N + t * 128:(h // 2) * N + (t + 1) * 128],
                                     KV[:, h * 64:(h + 1) * 64],
                                     start=(h in (0, 8)), stop=(h in (7, 15)),
                                     skip_group_check=True)
                mm.inc(inst)
        blk.tensor(_)

        if blocks == 5:
            def _(v):
                ld.wait(v)
            blk.vector(_)
            ln_gb_resident(k, blk,
                           src=lambda n: OF[:, n * C:(n + 1) * C],
                           dst=lambda n: H[:, n * C:(n + 1) * C],
                           gbc=GB2[:, 0:C], bbc=GB2[:, C:2 * C],
                           S=S2, garb=lambda n: H[:, n * C:(n + 1) * C], eps=1e-12)
        elif blocks == 45:
            def _(sync):
                sync.wait_ge(ev.sem, ev.n)
                for n in range(NT):
                    st.dma(sync.dma_start(out=d["sc_tmp"].ap()[n * 128:(n + 1) * 128, :],
                                          in_=OF[:, n * C:(n + 1) * C]))
                st.wait(sync)
            blk.sync(_)
        elif blocks == 41:
            pass
        elif blocks == 42:
            def _(sync):
                sync.wait_ge(ev.sem, ev.n)
                for t in range(NT):
                    st.dma(sync.dma_start(out=d["sc_tmp"].ap()[t * 128:(t + 1) * 128, :],
                                          in_=OF[:, t * C:(t + 1) * C]))
                st.wait(sync)
            blk.sync(_)
        else:
            def _(sync):
                sync.wait_ge(ev.sem, ev.n)
                st.dma(sync.dma_start(out=d["sc_misc"].ap()[2:3, 0:32], in_=ZC[:, 0:32][0:1, :]))
                st.dma(sync.dma_start(out=d["sc_tmp"].ap()[0:128, :], in_=ZT[:, :]))
                st.dma(sync.dma_start(out=d["sc_tmp"].ap()[128:256, :], in_=KMB[:, :]))
                st.wait(sync)
            blk.sync(_)
    pOg[0].__exit__(None, None, None)
    for g in (gQ2, gGB2, gS2, gZC, gZT, gOF):
        g.__exit__(None, None, None)
    for g in (gKM, gKV, gHT):
        g.__exit__(None, None, None)



def phase_a0_mlp(k, H):
    nc, d = k.nc, k.d
    ld, lg, mm, ev, sv, st, x2, x3 = k.ld, k.lg, k.mm, k.ev, k.sv, k.st, k.x2, k.x3

    gHT, HT = sbuf(nc, "mHT", 128, 8 * N, BF16)
    gGB, GB = sbuf(nc, "mGB", 128, 2 * C, F32)
    gB1, B1 = sbuf(nc, "mB1", 128, 32, F32)
    gB2, B2 = sbuf(nc, "mB2", 1, C + 128, BF16)
    gMID, MID = sbuf(nc, "mMID", 128, 16 * 1024, BF16)
    gOFh, OFh = sbuf(nc, "mOF", 128, 8 * C, F32)
    gW1, W1s = sbuf(nc, "mW1", 128, 2 * 8 * 128, BF16)
    gW2, W2g = sbuf(nc, "mW2", 128, 8 * 1024, BF16)
    gS3, S3 = sbuf(nc, "mS3", 128, 4 * 8, F32)
    gHB, HBS = sbuf(nc, "mHBS", 128, 2 * C, BF16)
    pMg = [psum(nc, f"mpM{i}", 128, 512) for i in range(2)]
    pOg = [psum(nc, f"mpO{i}", 128, 1024) for i in range(2)]
    pM = [t for _, t in pMg]
    pOC = [t for _, t in pOg]

    x30 = x3.n
    ev0, st0 = ev.n, st.n
    with nc.Block() as blk:
        def _(v):
            for n in range(NT):
                sl = n % 2
                if n >= 2:
                    v.wait_ge(x3.sem, x30 + n - 1)
                ev.inc(v.tensor_copy(HBS[:, sl * C:(sl + 1) * C], H[:, n * C:(n + 1) * C]))
        blk.vector(_)
        def _(sync):
            for n in range(NT):
                sl = n % 2
                sync.wait_ge(ev.sem, ev0 + n + 1)
                st.dma(sync.dma_start(out=d["sc_bbf"].ap()[n * 128:(n + 1) * 128, 0:C],
                                      in_=HBS[:, sl * C:(sl + 1) * C]))
                st.wait(sync)
                x3.bump(sync)
            for c in range(8):
                ld.dma(sync.dma_start(out=HT[:, c * N:(c + 1) * N],
                                      in_=d["sc_bbf"].ap()[:, c * 128:(c + 1) * 128],
                                      transpose=True))
            ga = d["ln2_g"].ap()
            ba = d["ln2_b"].ap()
            ld.dma(sync.dma_start(out=GB[:, 0:C], in_=ap2(ga, 0, [[0, 128], [1, C]])))
            ld.dma(sync.dma_start(out=GB[:, C:2 * C], in_=ap2(ba, 0, [[0, 128], [1, C]])))
            ld.wait(sync)
        blk.sync(_)
        def _(g):
            lg.dma(g.dma_start(out=B1[:, :], in_=d["b1"].ap()))
            lg.dma(g.dma_start(out=B2[0:1, 0:C], in_=d["b2"].ap()[0:1, :]))
            lg.wait(g)
        blk.gpsimd(_)
        def _(v):
            ev.inc(v.memset(B2[0:1, C:C + 128], 1.0))
        blk.vector(_)

    W1ap = d["W1"].ap()
    W2ap = d["W2"].ap()
    for half in range(2):
        hb = half * 1024
        mm0, sv0, ev0, x20, x30 = mm.n, sv.n, ev.n, x2.n, x3.n

        def mid_pos(j):
            jg = j // 8
            return jg * 24 + 2 * (j % 8) + 2

        with nc.Block() as blk:
            def _(g):
                for jg in range(4):
                    for jj in range(8):
                        j = jg * 8 + jj
                        if j >= 2:
                            g.wait_ge(mm.sem, mm0 + mid_pos(j - 2))
                        lg.dma(g.dma_start(
                            out=W1s[:, (j % 2) * 1024:(j % 2) * 1024 + 1024],
                            in_=ap2(W1ap, 128 * j, [[4096, 128], [128 * 4096, 8], [1, 128]])))
                        lg.wait(g)
                        x3.bump(g)
                    if jg >= 1:
                        g.wait_ge(mm.sem, mm0 + jg * 24)
                    lg.dma(g.dma_start(
                        out=W2g[:, :],
                        in_=ap2(W2ap, jg * 1024 * 1024, [[1024, 128], [131072, 8], [1, 1024]])))
                    lg.wait(g)
                    x2.bump(g)
            blk.gpsimd(_)

            def _(t_):
                for jg in range(4):
                    for jj in range(8):
                        j = jg * 8 + jj
                        t_.wait_ge(x3.sem, x30 + j + 1)
                        for pg in range(2):
                            e = 2 * j + pg
                            if e >= 2:
                                t_.wait_ge(sv.sem, sv0 + e - 1)
                            for c in range(8):
                                inst = t_.matmul(
                                    pM[e % 2][:, :],
                                    W1s[:, (j % 2) * 1024 + c * 128:(j % 2) * 1024 + (c + 1) * 128],
                                    HT[:, c * N + hb + pg * 512:c * N + hb + (pg + 1) * 512],
                                    start=(c == 0), stop=(c == 7))
                                if c == 7:
                                    mm.inc(inst)
                    t_.wait_ge(x2.sem, x20 + jg + 1)
                    t_.wait_ge(sv.sem, sv0 + 16 * (jg + 1))
                    for nt in range(8):
                        if jg > 0 or nt >= 2:
                            prev = jg * 8 + nt - 2
                            t_.wait_ge(ev.sem, ev0 + 1 + prev + 1)
                        for cop in range(2):
                            for jj in range(8):
                                inst = t_.matmul(
                                    pOC[nt % 2][:, cop * 512:(cop + 1) * 512],
                                    MID[:, (jg % 2) * 8192 + jj * 1024 + nt * 128:
                                        (jg % 2) * 8192 + jj * 1024 + nt * 128 + 128],
                                    W2g[:, jj * 1024 + cop * 512:jj * 1024 + (cop + 1) * 512],
                                    start=(jj == 0), stop=(jg != 0 and jj == 7))
                            if jg == 0:
                                inst = t_.matmul(pOC[nt % 2][:, cop * 512:(cop + 1) * 512],
                                                 B2[0:1, C:C + 128],
                                                 B2[0:1, cop * 512:(cop + 1) * 512],
                                                 start=False, stop=True)
                        mm.inc(inst)
            blk.tensor(_)

            def _(s):
                for jg in range(4):
                    for jj in range(8):
                        j = jg * 8 + jj
                        for pg in range(2):
                            e = 2 * j + pg
                            s.wait_ge(mm.sem, mm0 + jg * 24 + 2 * jj + pg + 1)
                            sv.inc(s.activation(
                                MID[:, (jg % 2) * 8192 + jj * 1024 + pg * 512:
                                    (jg % 2) * 8192 + jj * 1024 + (pg + 1) * 512],
                                pM[e % 2][:, :], AF.Gelu, bias=B1[:, j:j + 1]))
            blk.scalar(_)

            def _(v):
                for i in range(8):
                    inst = v.tensor_copy(OFh[:, i * C:(i + 1) * C],
                                         H[:, (half * 8 + i) * C:(half * 8 + i + 1) * C])
                ev.inc(inst)
                v.wait_ge(ev.sem, ev0 + 1)
                for jg in range(4):
                    for nt in range(8):
                        v.wait_ge(mm.sem, mm0 + jg * 24 + 16 + nt + 1)
                        ev.inc(v.tensor_tensor(OFh[:, nt * C:(nt + 1) * C],
                                               OFh[:, nt * C:(nt + 1) * C],
                                               pOC[nt % 2][:, :], OP.add))
            blk.vector(_)

            ln_gb_resident(k, blk,
                           src=lambda i: OFh[:, i * C:(i + 1) * C],
                           dst=lambda i: H[:, (half * 8 + i) * C:(half * 8 + i + 1) * C],
                           gbc=GB[:, 0:C], bbc=GB[:, C:2 * C],
                           S=S3, garb=lambda i: MID[:, i * 1024:(i + 1) * 1024], eps=1e-12,
                           nt=8)

    for g_, _t in reversed(pOg):
        g_.__exit__(None, None, None)
    for g_, _t in reversed(pMg):
        g_.__exit__(None, None, None)
    for g_ in (gHB, gS3, gW2, gW1, gOFh, gMID, gB2, gB1, gGB, gHT):
        g_.__exit__(None, None, None)



def phase_f0(k):
    nc, d = k.nc, k.d
    ld, lg, mm, ev, sv, st, x2, x3 = k.ld, k.lg, k.mm, k.ev, k.sv, k.st, k.x2, k.x3
    H2 = 2048

    gXT, XT = sbuf(nc, "fXT", 128, 8 * N, BF16)
    gMK, MK = sbuf(nc, "fMK", 128, NT * 8, F32)
    gE8, E8 = sbuf(nc, "fE8", 128, 8, F32)
    gRW, RW = sbuf(nc, "fRW", 128, 2 * 64, BF16)
    gRF, RF = sbuf(nc, "fRF", 128, 64, F32)
    gXL, XLT = sbuf(nc, "fXLT", 128, 8 * N, BF16)
    gLG, LGt = sbuf(nc, "fLG", 128, 4 * 8, F32)
    gMX, MX = sbuf(nc, "fMX", 128, 4, F32)
    gMID, MID = sbuf(nc, "fMID", 128, 16 * 1024, BF16)
    gOFh, OFh = sbuf(nc, "fOF", 128, 8 * C, F32)
    gWG, WGs = sbuf(nc, "fWG", 128, 2 * 1024, BF16)
    gWU, WUs = sbuf(nc, "fWU", 128, 2 * 1024, BF16)
    gWD, WDg = sbuf(nc, "fWD", 128, 8 * 1024, BF16)
    gSG, SG = sbuf(nc, "fSG", 128, 2 * 512, BF16)
    pLg = psum(nc, "fpL", 128, 8)
    pL = pLg[1]

    ld0, lg0, mm0, ev0, sv0 = ld.n, lg.n, mm.n, ev.n, sv.n
    with nc.Block() as blk:
        def _(sync):
            for c in range(8):
                ld.dma(sync.dma_start(out=XT[:, c * N:(c + 1) * N],
                                      in_=d["sc_nbf"].ap()[:, c * 128:(c + 1) * 128],
                                      transpose=True))
            for c in range(8):
                ld.dma(sync.dma_start(out=XLT[:, c * N:(c + 1) * N],
                                      in_=d["sc_lbf"].ap()[:, c * 128:(c + 1) * 128],
                                      transpose=True))
            ld.wait(sync)
        blk.sync(_)
        def _(g):
            x3.inc(g.iota(E8[:, :], pattern=[[1, 8]], base=0, channel_multiplier=0,
                          allow_small_or_imprecise_dtypes=True))
            lg.dma(g.dma_start(out=RF[:, :],
                               in_=ap2(d["routerT"].ap(), 0, [[8, 128], [1024, 8], [1, 8]])))
            lg.wait(g)
        blk.gpsimd(_)
        def _(v):
            lg.wait(v)
            ev.inc(v.tensor_copy(RW[:, 0:64], RF[:, :])); ev.wait(v)
            ev.inc(v.tensor_sub(RW[:, 64:128], RF[:, :], RW[:, 0:64]))
        blk.vector(_)
        def _(t_):
            t_.wait_ge(ld.sem, ld0 + 16 * 16)
            t_.wait_ge(ev.sem, ev0 + 2)
            for t in range(NT):
                if t >= 1:
                    t_.wait_ge(sv.sem, sv0 + t)
                for c in range(8):
                    t_.matmul(pL[:, :],
                              XT[:, c * N + t * 128:c * N + (t + 1) * 128],
                              RW[:, c * 8:(c + 1) * 8],
                              start=(c == 0), stop=False)
                for c in range(8):
                    t_.matmul(pL[:, :],
                              XLT[:, c * N + t * 128:c * N + (t + 1) * 128],
                              RW[:, c * 8:(c + 1) * 8],
                              start=False, stop=False)
                for c in range(8):
                    inst = t_.matmul(pL[:, :],
                                     XT[:, c * N + t * 128:c * N + (t + 1) * 128],
                                     RW[:, 64 + c * 8:64 + (c + 1) * 8],
                                     start=False, stop=(c == 7))
                mm.inc(inst)
        blk.tensor(_)
        def _(s):
            for t in range(NT):
                s.wait_ge(mm.sem, mm0 + t + 1)
                if t >= 1:
                    s.wait_ge(ev.sem, ev0 + 2 + 6 * t)
                sv.inc(s.activation(LGt[:, 0:8], pL[:, :], AF.Identity))
        blk.scalar(_)
        def _(v):
            v.wait_ge(x3.sem, x3.n)
            for t in range(NT):
                v.wait_ge(sv.sem, sv0 + t + 1)
                ev.inc(v.reduce_max(MX[:, 0:1], LGt[:, 0:8], axis=AX))
                v.wait_ge(ev.sem, ev0 + 2 + 6 * t + 1)
                ev.inc(v.tensor_scalar(LGt[:, 8:16], LGt[:, 0:8], MX[:, 0:1], None, OP.is_equal))
                v.wait_ge(ev.sem, ev0 + 2 + 6 * t + 2)
                ev.inc(v.tensor_scalar(LGt[:, 16:24], LGt[:, 8:16], -1000.0, 1000.0, OP.mult, OP.add))
                v.wait_ge(ev.sem, ev0 + 2 + 6 * t + 3)
                ev.inc(v.tensor_add(LGt[:, 16:24], LGt[:, 16:24], E8[:, :]))
                v.wait_ge(ev.sem, ev0 + 2 + 6 * t + 4)
                ev.inc(v.tensor_reduce(MX[:, 1:2], LGt[:, 16:24], AX, OP.min))
                v.wait_ge(ev.sem, ev0 + 2 + 6 * t + 5)
                ev.inc(v.tensor_scalar(MK[:, t * 8:(t + 1) * 8], LGt[:, 16:24], MX[:, 1:2],
                                       None, OP.is_equal))
        blk.vector(_)
    pLg[0].__exit__(None, None, None)
    pGg = [psum(nc, f"fpG{i}", 128, 512) for i in range(2)]
    pUg = [psum(nc, f"fpU{i}", 128, 512) for i in range(2)]
    pOg = [psum(nc, f"fpO{i}", 128, 1024) for i in range(2)]
    pG = [t for _, t in pGg]
    pU = [t for _, t in pUg]
    pOC = [t for _, t in pOg]

    gaps = {"gate": d["gateT"].ap(), "up": d["upT"].ap(), "down": d["downT"].ap(),
            "gs": d["wgsT"].ap(), "us": d["wusT"].ap(), "ds": d["wdsT"].ap()}

    for half in range(2):
        hb = half * 1024
        for e in range(9):
            routed = e < 8
            mm0, sv0, ev0, x20, x30 = mm.n, sv.n, ev.n, x2.n, x3.n

            def mid_pos(j):
                jg = j // 8
                return jg * 40 + 4 * (j % 8) + 4

            with nc.Block() as blk:
                def _(g):
                    for jg in range(2):
                        for jj in range(8):
                            j = jg * 8 + jj
                            if j >= 2:
                                g.wait_ge(mm.sem, mm0 + mid_pos(j - 2))
                            if routed:
                                gsrc = ap2(gaps["gate"], e * C * H2 + 128 * j,
                                           [[2048, 128], [128 * 2048, 8], [1, 128]])
                                usrc = ap2(gaps["up"], e * C * H2 + 128 * j,
                                           [[2048, 128], [128 * 2048, 8], [1, 128]])
                            else:
                                gsrc = ap2(gaps["gs"], 128 * j,
                                           [[2048, 128], [128 * 2048, 8], [1, 128]])
                                usrc = ap2(gaps["us"], 128 * j,
                                           [[2048, 128], [128 * 2048, 8], [1, 128]])
                            lg.dma(g.dma_start(out=WGs[:, (j % 2) * 1024:(j % 2 + 1) * 1024], in_=gsrc))
                            lg.dma(g.dma_start(out=WUs[:, (j % 2) * 1024:(j % 2 + 1) * 1024], in_=usrc))
                            lg.wait(g)
                            x3.bump(g)
                        if jg >= 1:
                            g.wait_ge(mm.sem, mm0 + jg * 40)
                        dsrc = (ap2(gaps["down"], e * H2 * C + jg * 1024 * C,
                                    [[1024, 128], [131072, 8], [1, 1024]]) if routed else
                                ap2(gaps["ds"], jg * 1024 * C,
                                    [[1024, 128], [131072, 8], [1, 1024]]))
                        lg.dma(g.dma_start(out=WDg[:, :], in_=dsrc))
                        lg.wait(g)
                        x2.bump(g)
                blk.gpsimd(_)

                def _(t_):
                    evb = ev0 + (1 if e == 0 else 0)
                    for jg in range(2):
                        for jj in range(8):
                            j = jg * 8 + jj
                            t_.wait_ge(x3.sem, x30 + j + 1)
                            for pg in range(2):
                                ei = 2 * j + pg
                                if ei >= 2:
                                    t_.wait_ge(sv.sem, sv0 + ei - 1)
                                for c in range(8):
                                    inst = t_.matmul(
                                        pG[ei % 2][:, :],
                                        WGs[:, (j % 2) * 1024 + c * 128:(j % 2) * 1024 + (c + 1) * 128],
                                        XT[:, c * N + hb + pg * 512:c * N + hb + (pg + 1) * 512],
                                        start=(c == 0), stop=(c == 7))
                                    if c == 7:
                                        mm.inc(inst)
                                if ei >= 2:
                                    eprev = ei - 2
                                    t_.wait_ge(ev.sem, evb + (eprev // 16) * 24 + (eprev % 16) + 1)
                                for c in range(8):
                                    inst = t_.matmul(
                                        pU[ei % 2][:, :],
                                        WUs[:, (j % 2) * 1024 + c * 128:(j % 2) * 1024 + (c + 1) * 128],
                                        XT[:, c * N + hb + pg * 512:c * N + hb + (pg + 1) * 512],
                                        start=(c == 0), stop=(c == 7))
                                    if c == 7:
                                        mm.inc(inst)
                        t_.wait_ge(x2.sem, x20 + jg + 1)
                        t_.wait_ge(ev.sem, evb + jg * 24 + 16)
                        for nt in range(8):
                            lin = jg * 8 + nt
                            if lin >= 2:
                                lp = lin - 2
                                t_.wait_ge(ev.sem, evb + (lp // 8) * 24 + 16 + (lp % 8) + 1)
                            for cop in range(2):
                                for jj in range(8):
                                    inst = t_.matmul(
                                        pOC[nt % 2][:, cop * 512:(cop + 1) * 512],
                                        MID[:, (jg % 2) * 8192 + jj * 1024 + nt * 128:
                                            (jg % 2) * 8192 + jj * 1024 + nt * 128 + 128],
                                        WDg[:, jj * 1024 + cop * 512:jj * 1024 + (cop + 1) * 512],
                                        start=(jj == 0), stop=(jj == 7))
                            mm.inc(inst)
                blk.tensor(_)

                def _(s):
                    evb = ev0 + (1 if e == 0 else 0)
                    for jg in range(2):
                        for jj in range(8):
                            j = jg * 8 + jj
                            for pg in range(2):
                                ei = 2 * j + pg
                                s.wait_ge(mm.sem, mm0 + jg * 40 + 4 * jj + 2 * pg + 1)
                                if ei >= 2:
                                    eprev = ei - 2
                                    s.wait_ge(ev.sem, evb + (eprev // 16) * 24 + (eprev % 16) + 1)
                                sv.inc(s.activation(SG[:, (ei % 2) * 512:(ei % 2 + 1) * 512],
                                                    pG[ei % 2][:, :], AF.Silu))
                blk.scalar(_)

                def _(v):
                    if e == 0:
                        for i in range(8):
                            inst = v.memset(OFh[:, i * C:(i + 1) * C], 0.0)
                        ev.inc(inst)
                        v.wait_ge(ev.sem, ev0 + 1)
                    evb = ev0 + (1 if e == 0 else 0)
                    for jg in range(2):
                        for jj in range(8):
                            j = jg * 8 + jj
                            for pg in range(2):
                                ei = 2 * j + pg
                                v.wait_ge(sv.sem, sv0 + ei + 1)
                                v.wait_ge(mm.sem, mm0 + jg * 40 + 4 * jj + 2 * pg + 2)
                                ev.inc(v.tensor_tensor(
                                    MID[:, (jg % 2) * 8192 + jj * 1024 + pg * 512:
                                        (jg % 2) * 8192 + jj * 1024 + (pg + 1) * 512],
                                    SG[:, (ei % 2) * 512:(ei % 2 + 1) * 512],
                                    pU[ei % 2][:, :], OP.mult))
                        for nt in range(8):
                            v.wait_ge(mm.sem, mm0 + jg * 40 + 32 + nt + 1)
                            if jg == 1:
                                v.wait_ge(ev.sem, evb + 16 + nt + 1)
                            mask = MK[:, (half * 8 + nt) * 8 + e:(half * 8 + nt) * 8 + e + 1] \
                                if routed else 1.0
                            ev.inc(v.scalar_tensor_tensor(
                                OFh[:, nt * C:(nt + 1) * C],
                                pOC[nt % 2][:, :], mask,
                                OFh[:, nt * C:(nt + 1) * C],
                                OP.mult, OP.add))
                blk.vector(_)

        with nc.Block() as blk:
            def _(sync):
                for i in range(8):
                    st.dma(sync.dma_start(
                        out=d["sc_tmp"].ap()[(half * 8 + i) * 128:(half * 8 + i + 1) * 128, :],
                        in_=OFh[:, i * C:(i + 1) * C]))
                st.wait(sync)
            blk.sync(_)

    for g_, _t in reversed(pOg):
        g_.__exit__(None, None, None)
    for g_, _t in reversed(pUg):
        g_.__exit__(None, None, None)
    for g_, _t in reversed(pGg):
        g_.__exit__(None, None, None)
    for g_ in (gSG, gWD, gWU, gWG, gOFh, gMID, gMX, gLG, gXL, gRF, gRW, gE8, gMK, gXT):
        g_.__exit__(None, None, None)


def phase_a1(k):
    nc, d = k.nc, k.d
    ld, lg, mm, ev, sv, st, x2, x3 = k.ld, k.lg, k.mm, k.ev, k.sv, k.st, k.x2, k.x3
    SHIFT = 32.0

    l0, m0, s0, t0 = lg.n, mm.n, sv.n, st.n
    gOH, OH = sbuf(nc, "bOH", 32, 8 * 512, BF16)
    gRB, RB = sbuf(nc, "bRB", 32, 16, BF16)
    gBE, BE = sbuf(nc, "bBE", 16, 8 * 512, F32)
    pBg = psum(nc, "bpB", 16, 2 * 512)
    pB = pBg[1]
    with nc.Block() as blk:
        def _(g):
            lg.dma(g.dma_start(out=RB[:, :], in_=d["relb"].ap()))
            for p in range(8):
                w = 512 if p < 7 else 511
                lg.dma(g.dma_start(out=OH[:, p * 512:p * 512 + w],
                                   in_=ap2(d["onehotT"].ap(), p * 512, [[4095, 32], [1, w]])))
            lg.wait(g)
        blk.gpsimd(_)
        def _(t_):
            t_.wait_ge(lg.sem, l0 + 16 * 9)
            for p in range(8):
                w = 512 if p < 7 else 511
                if p >= 2:
                    t_.wait_ge(sv.sem, s0 + p - 1)
                mm.inc(t_.matmul(pB[:, (p % 2) * 512:(p % 2) * 512 + w], RB[:, :],
                                 OH[:, p * 512:p * 512 + w], start=True, stop=True))
        blk.tensor(_)
        def _(s):
            for p in range(8):
                w = 512 if p < 7 else 511
                s.wait_ge(mm.sem, m0 + p + 1)
                sv.inc(s.activation(BE[:, p * 512:p * 512 + w],
                                    pB[:, (p % 2) * 512:(p % 2) * 512 + w], AF.Identity))
        blk.scalar(_)
        def _(sync):
            for p in range(8):
                w = 512 if p < 7 else 511
                sync.wait_ge(sv.sem, s0 + p + 1)
                st.dma(sync.dma_start(
                    out=ap2(d["sc_bias"].ap(), p * 512, [[4095, 16], [1, w]]),
                    in_=BE[0:16, p * 512:p * 512 + w]))
            st.wait(sync)
        blk.sync(_)
    pBg[0].__exit__(None, None, None)
    for g_ in (gBE, gRB, gOH):
        g_.__exit__(None, None, None)

    gW1, W1b = sbuf(nc, "a1W1b", 128, 8 * C, BF16)
    gNT8, NT8 = sbuf(nc, "a1NT8", 128, 8 * N, BF16)
    gLW, LW = sbuf(nc, "a1LW", 128, C, F32)
    gNF2, NF2 = sbuf(nc, "a1NF2", 128, 2 * C, F32)
    gNG, NG = sbuf(nc, "a1NG", 128, 4 * C, F32)
    gNB, NB = sbuf(nc, "a1NB", 128, 2 * C, BF16)
    gS4, SC4 = sbuf(nc, "a1SC4", 128, 4 * NT, F32)
    l0, m0, s0, e0, t0, d0, x30, x20 = lg.n, mm.n, sv.n, ev.n, st.n, ld.n, x3.n, x2.n
    with nc.Block() as blk:
        def _(g):
            lg.dma(g.dma_start(out=LW[:, :], in_=ap2(d["lnw1"].ap(), 0, [[0, 128], [1, C]])))
            for t in range(NT):
                sl = t % 2
                if t >= 2:
                    g.wait_ge(ev.sem, e0 + 3 * (t - 2) + 4)
                lg.dma(g.dma_start(out=NF2[:, sl * C:(sl + 1) * C],
                                   in_=d["sc_nf"].ap()[t * 128:(t + 1) * 128, :]))
                lg.wait(g)
                x3.bump(g)
        blk.gpsimd(_)
        def _(v):
            ev.inc(v.memset(SC4[:, 3 * NT:4 * NT], 0.0))
            v.wait_ge(ev.sem, e0 + 1)
            for t in range(NT):
                sl = t % 2
                v.wait_ge(x3.sem, x30 + t + 1)
                ev.inc(v.scalar_tensor_tensor(
                    NG[:, (2 + sl) * C:(3 + sl) * C], NF2[:, sl * C:(sl + 1) * C], 0.0,
                    NF2[:, sl * C:(sl + 1) * C], OP.subtract, OP.mult,
                    accum_out=SC4[:, 3 * NT + t:3 * NT + t + 1]))
                v.wait_ge(ev.sem, e0 + 3 * t + 2)
                ev.inc(v.tensor_scalar(SC4[:, t:t + 1], SC4[:, 3 * NT + t:3 * NT + t + 1],
                                       1.0 / C, 1e-6, OP.mult, OP.add))
                v.wait_ge(sv.sem, s0 + 3 * t + 2)
                ev.inc(v.tensor_tensor(NG[:, sl * C:(sl + 1) * C],
                                       NF2[:, sl * C:(sl + 1) * C], LW[:, :], OP.mult))
        blk.vector(_)
        def _(s):
            for t in range(NT):
                sl = t % 2
                s.wait_ge(ev.sem, e0 + 3 * t + 3)
                sv.inc(s.activation(SC4[:, NT + t:NT + t + 1], SC4[:, t:t + 1], AF.Ln))
                s.wait_ge(sv.sem, s0 + 3 * t + 1)
                sv.inc(s.activation(SC4[:, NT + t:NT + t + 1], SC4[:, NT + t:NT + t + 1],
                                    AF.Exp, scale=-0.5))
                s.wait_ge(sv.sem, s0 + 3 * t + 2)
                s.wait_ge(ev.sem, e0 + 3 * t + 4)
                if t >= 2:
                    s.wait_ge(x2.sem, x20 + t - 1)
                sv.inc(s.activation(NB[:, sl * C:(sl + 1) * C], NG[:, sl * C:(sl + 1) * C],
                                    AF.Identity, scale=SC4[:, NT + t:NT + t + 1]))
        blk.scalar(_)
        def _(sync):
            for t in range(NT):
                sl = t % 2
                sync.wait_ge(sv.sem, s0 + 3 * t + 3)
                st.dma(sync.dma_start(out=d["sc_bbf"].ap()[t * 128:(t + 1) * 128, 0:C],
                                      in_=NB[:, sl * C:(sl + 1) * C]))
                st.wait(sync)
                x2.bump(sync)
            for c in range(8):
                ld.dma(sync.dma_start(out=NT8[:, c * N:(c + 1) * N],
                                      in_=d["sc_bbf"].ap()[:, c * 128:(c + 1) * 128],
                                      transpose=True))
            ld.wait(sync)
        blk.sync(_)
    for g_ in (gS4, gNB, gNG, gNF2, gLW):
        g_.__exit__(None, None, None)

    gQE, QE = sbuf(nc, "a1QE", 128, 8 * 512, BF16)
    pQg = [psum(nc, f"a1pQ{i}", 128, 512) for i in range(2)]
    pQ2 = [t for _, t in pQg]
    l0, m0, s0, t0, x30, x20 = lg.n, mm.n, sv.n, st.n, x3.n, x2.n
    G = 96

    def _dst(g):
        wi = g // 32
        if wi < 2:
            e = g % 32
            co, pg = e // 4, e % 4
            r0 = wi * 1024 + co * 128
            return d["sc_bbf"].ap()[r0:r0 + 128, 2048 + pg * 512:2048 + (pg + 1) * 512]
        t = (g - 64) // 2
        cop = (g - 64) % 2
        return d["sc_lbf"].ap()[t * 128:(t + 1) * 128, cop * 512:(cop + 1) * 512]

    with nc.Block() as blk:
        def _(g):
            for wi, wname in enumerate(("wq1", "wk1", "wv1")):
                if wi > 0:
                    g.wait_ge(mm.sem, m0 + 32 * wi)
                lg.dma(g.dma_start(out=W1b[:, :],
                                   in_=ap2(d[wname].ap(), 0,
                                           [[1024, 128], [128 * 1024, 8], [1, 1024]])))
                lg.wait(g)
                x3.bump(g)
        blk.gpsimd(_)
        def _(t_):
            for g in range(G):
                wi = g // 32
                t_.wait_ge(x3.sem, x30 + wi + 1)
                if g >= 2:
                    t_.wait_ge(sv.sem, s0 + g - 1)
                if wi < 2:
                    e = g % 32
                    co, pg = e // 4, e % 4
                    for c in range(8):
                        inst = t_.matmul(
                            pQ2[g % 2][:, :],
                            W1b[:, c * C + co * 128:c * C + (co + 1) * 128],
                            NT8[:, c * N + pg * 512:c * N + (pg + 1) * 512],
                            start=(c == 0), stop=(c == 7))
                else:
                    tt = (g - 64) // 2
                    cop = (g - 64) % 2
                    for c in range(8):
                        inst = t_.matmul(
                            pQ2[g % 2][:, :],
                            NT8[:, c * N + tt * 128:c * N + (tt + 1) * 128],
                            W1b[:, c * C + cop * 512:c * C + (cop + 1) * 512],
                            start=(c == 0), stop=(c == 7))
                mm.inc(inst)
        blk.tensor(_)
        def _(s):
            for g in range(G):
                s.wait_ge(mm.sem, m0 + g + 1)
                if g >= 8:
                    s.wait_ge(x2.sem, x20 + (g - 8) // 4 + 1)
                sv.inc(s.activation(QE[:, (g % 8) * 512:(g % 8 + 1) * 512], pQ2[g % 2][:, :],
                                    AF.Identity))
        blk.scalar(_)
        def _(sync):
            for g in range(G):
                sync.wait_ge(sv.sem, s0 + g + 1)
                st.dma(sync.dma_start(out=_dst(g), in_=QE[:, (g % 8) * 512:(g % 8 + 1) * 512]))
                if g % 4 == 3 or g == G - 1:
                    st.wait(sync)
                    x2.bump(sync)
        blk.sync(_)
    for g_, _t in reversed(pQg):
        g_.__exit__(None, None, None)
    gQE.__exit__(None, None, None)
    gNT8.__exit__(None, None, None)

    gFB, FB = sbuf(nc, "a1FB", 128, 33, F32)
    gQH, QH = sbuf(nc, "a1QH", 64, 2 * N, BF16)
    gKH, KH = sbuf(nc, "a1KH", 64, 2 * N, BF16)
    gVT, VT = sbuf(nc, "a1VT", 128, 16 * 65, BF16)
    gBW, BW = sbuf(nc, "a1BW", 128, 16 * 384, BF16)
    gRV, RV = sbuf(nc, "a1RV", 128, 128, BF16)
    gPT, PTb = sbuf(nc, "a1PT", 128, 2 * N, BF16)
    gRS, RS = sbuf(nc, "a1RS", 1, N, F32)
    gRSB, RSB = sbuf(nc, "a1RSB", 64, N, F32)
    gCE, CTXe = sbuf(nc, "a1CE", 64, 2 * N, BF16)
    pSg = psum(nc, "a1pS", 128, N)
    pCg = psum(nc, "a1pC", 65, N)
    pS, pC = pSg[1], pCg[1]
    l0, m0, s0, e0, t0, d0, x20, x30 = lg.n, mm.n, sv.n, ev.n, st.n, ld.n, x2.n, x3.n

    def _lohi(kt):
        k0 = kt * 128
        return max(0, k0 - 128), min(N, k0 + 256)

    pre_done = x20 + 1
    vtb = {(h, j): x20 + 1 + 4 * h + j + 1 for h in range(16) for j in range(2)}
    rsb_done = {h: x20 + 1 + 4 * h + 3 for h in range(16)}
    ctxs_done = {h: x20 + 1 + 4 * h + 4 for h in range(16)}
    exp_done = {}
    cnt = 0
    for h in range(16):
        for kt in range(16):
            lo, hi = _lohi(kt)
            cnt += (1 if lo > 0 else 0) + 1 + (1 if hi < N else 0)
            exp_done[h * 16 + kt] = s0 + cnt
    recip_done = {h: e0 + 1 + 2 * h + 1 for h in range(16)}
    ctx_done = {h: e0 + 1 + 2 * h + 2 for h in range(16)}
    scr = {lin: m0 + 2 * lin + 1 for lin in range(256)}
    pvd = {lin: m0 + 2 * lin + 2 for lin in range(256)}

    with nc.Block() as blk:
        def _(t_):
            for h in range(16):
                hs = h % 2
                for kt in range(16):
                    lin = h * 16 + kt
                    k0 = kt * 128
                    sl = lin % 2
                    lo, hi = _lohi(kt)
                    if kt == 0:
                        t_.wait_ge(x2.sem, pre_done if h == 0 else vtb[(h - 1, 0)])
                    if lin >= 1:
                        t_.wait_ge(sv.sem, exp_done[lin - 1])
                    t_.wait_ge(x3.sem, x30 + lin // 8 + 1)
                    for pg in range(4):
                        t_.matmul(pS[:, pg * 512:(pg + 1) * 512],
                                  KH[:, hs * N + k0:hs * N + k0 + 128],
                                  QH[:, hs * N + pg * 512:hs * N + (pg + 1) * 512],
                                  start=True, stop=True, skip_group_check=True)
                    a = lo
                    while a < hi:
                        b = min(hi, (a // 512 + 1) * 512)
                        wb = (lin % 16) * 384 + (a - lo)
                        inst = t_.matmul(pS[:, a:b], RV[:, :], BW[:, wb:wb + (b - a)],
                                         start=False, stop=True, skip_group_check=True)
                        a = b
                    mm.inc(inst)
                    t_.wait_ge(sv.sem, exp_done[lin])
                    t_.wait_ge(x2.sem, vtb[(h, kt // 8)])
                    if kt == 0:
                        t_.wait_ge(ev.sem, ctx_done[h - 1] if h >= 1 else e0 + 1)
                    for pg in range(4):
                        inst = t_.matmul(pC[:, pg * 512:(pg + 1) * 512],
                                         VT[:, kt * 65:kt * 65 + 65],
                                         PTb[:, sl * N + pg * 512:sl * N + (pg + 1) * 512],
                                         start=(kt == 0), stop=(kt == 15),
                                         skip_group_check=True)
                    mm.inc(inst)
        blk.tensor(_)

        def _(v):
            v.wait_ge(x2.sem, pre_done)
            v.tensor_scalar_add(FB[:, 0:32], FB[:, 0:32], -SHIFT)
            v.memset(FB[:, 32:33], -SHIFT)
            inst = None
            for s in range(16):
                inst = v.memset(VT[:, s * 65 + 64:s * 65 + 65], 1.0)
            ev.inc(inst)
            for h in range(16):
                v.wait_ge(mm.sem, pvd[h * 16 + 15])
                if h >= 1:
                    v.wait_ge(x2.sem, rsb_done[h - 1])
                ev.inc(v.reciprocal(RS[0:1, :], pC[64:65, :]))
                v.wait_ge(x2.sem, rsb_done[h])
                if h >= 2:
                    v.wait_ge(x2.sem, ctxs_done[h - 2])
                ev.inc(v.scalar_tensor_tensor(CTXe[:, (h % 2) * N:(h % 2 + 1) * N],
                                              pC[0:64, :], 1.0, RSB[:, :],
                                              OP.mult, OP.mult))
        blk.vector(_)

        def _(s):
            s.wait_ge(ev.sem, e0 + 1)
            for h in range(16):
                for kt in range(16):
                    lin = h * 16 + kt
                    sl = lin % 2
                    lo, hi = _lohi(kt)
                    s.wait_ge(mm.sem, scr[lin])
                    if lin >= 2:
                        s.wait_ge(mm.sem, pvd[lin - 2])
                    if lo > 0:
                        sv.inc(s.activation(PTb[:, sl * N:sl * N + lo], pS[:, 0:lo],
                                            AF.Exp, bias=FB[:, 2 * h + 1:2 * h + 2]))
                    sv.inc(s.activation(PTb[:, sl * N + lo:sl * N + hi], pS[:, lo:hi],
                                        AF.Exp, bias=FB[:, 32:33]))
                    if hi < N:
                        sv.inc(s.activation(PTb[:, sl * N + hi:sl * N + N], pS[:, hi:N],
                                            AF.Exp, bias=FB[:, 2 * h:2 * h + 1]))
        blk.scalar(_)

        def _(g):
            for b in range(32):
                if b >= 2:
                    g.wait_ge(mm.sem, scr[8 * (b - 2) + 7])
                for i in range(8):
                    lin = 8 * b + i
                    h, kt = lin // 16, lin % 16
                    k0 = kt * 128
                    lo, hi = _lohi(kt)
                    W = hi - lo
                    lg.dma(g.dma_start(
                        out=BW[:, (lin % 16) * 384:(lin % 16) * 384 + W],
                        in_=ap2(d["sc_bias"].ap(), h * 4095 + 1920 - k0 + lo,
                                [[1, 128], [1, W]])))
                lg.wait(g)
                x3.bump(g)
        blk.gpsimd(_)

        def _(sync):
            for h in range(16):
                ld.dma(sync.dma_start(out=FB[:, 2 * h:2 * h + 1],
                                      in_=ap2(d["relb"].ap(), 15 * 16 + h, [[0, 128], [1, 1]])))
                ld.dma(sync.dma_start(out=FB[:, 2 * h + 1:2 * h + 2],
                                      in_=ap2(d["relb"].ap(), 31 * 16 + h, [[0, 128], [1, 1]])))
            ld.dma(sync.dma_start(out=QH[:, 0:N],
                                  in_=d["sc_bbf"].ap()[0:64, 2048:4096]))
            ld.dma(sync.dma_start(out=KH[:, 0:N],
                                  in_=d["sc_bbf"].ap()[1024:1024 + 64, 2048:4096]))
            ld.dma(sync.dma_start(out=RV[:, :], in_=d["revbf"].ap()))
            ld.wait(sync)
            x2.bump(sync)
            for h in range(16):
                for j in range(2):
                    bidx = 2 * h + j
                    if bidx >= 2:
                        sync.wait_ge(mm.sem, pvd[8 * (bidx - 2) + 7])
                    if j == 0 and h < 15:
                        if h >= 1:
                            sync.wait_ge(mm.sem, scr[(h - 1) * 16 + 15])
                        hn = h + 1
                        hs = hn % 2
                        ld.dma(sync.dma_start(
                            out=QH[:, hs * N:(hs + 1) * N],
                            in_=d["sc_bbf"].ap()[hn * 64:(hn + 1) * 64, 2048:4096]))
                        ld.dma(sync.dma_start(
                            out=KH[:, hs * N:(hs + 1) * N],
                            in_=d["sc_bbf"].ap()[1024 + hn * 64:1024 + (hn + 1) * 64, 2048:4096]))
                    for kt in range(8 * j, 8 * j + 8):
                        ld.dma(sync.dma_start(
                            out=VT[:, kt * 65:kt * 65 + 64],
                            in_=d["sc_lbf"].ap()[kt * 128:(kt + 1) * 128, h * 64:(h + 1) * 64]))
                    ld.wait(sync)
                    x2.bump(sync)
                sync.wait_ge(ev.sem, recip_done[h])
                st.dma(sync.dma_start(out=d["sc_misc"].ap()[8:9, 0:N], in_=RS[0:1, :]))
                st.wait(sync)
                ld.dma(sync.dma_start(out=RSB[:, :],
                                      in_=ap2(d["sc_misc"].ap(), 8 * 4096, [[0, 64], [1, N]])))
                ld.wait(sync)
                x2.bump(sync)
                sync.wait_ge(ev.sem, ctx_done[h])
                st.dma(sync.dma_start(
                    out=d["sc_bbf"].ap()[1024 + h * 64:1024 + (h + 1) * 64, 2048:4096],
                    in_=CTXe[:, (h % 2) * N:(h % 2 + 1) * N]))
                st.wait(sync)
                x2.bump(sync)
        blk.sync(_)
    assert x2.n == ctxs_done[15], (x2.n, ctxs_done[15])
    assert ev.n == ctx_done[15], (ev.n, ctx_done[15])
    assert sv.n == exp_done[255], (sv.n, exp_done[255])
    pCg[0].__exit__(None, None, None)
    pSg[0].__exit__(None, None, None)
    for g_ in (gCE, gRSB, gRS, gPT, gRV, gBW, gVT, gKH, gQH, gFB):
        g_.__exit__(None, None, None)

    gCT4, CT4 = sbuf(nc, "a1CT4", 128, 8 * N, BF16)
    gNF3, NF3 = sbuf(nc, "a1NF3", 128, 2 * C, F32)
    gOF2, OF2 = sbuf(nc, "a1OF2", 128, 2 * C, F32)
    pOg = [psum(nc, f"a1pO{i}", 128, 1024) for i in range(2)]
    pO2 = [t for _, t in pOg]
    l0, m0, e0, t0, d0, x20, x30 = lg.n, mm.n, ev.n, st.n, ld.n, x2.n, x3.n
    with nc.Block() as blk:
        def _(g):
            lg.dma(g.dma_start(out=W1b[:, :],
                               in_=ap2(d["wo1"].ap(), 0,
                                       [[1024, 128], [128 * 1024, 8], [1, 1024]])))
            lg.wait(g)
            x3.bump(g)
            for t in range(NT):
                sl = t % 2
                if t >= 2:
                    g.wait_ge(ev.sem, e0 + t - 1)
                lg.dma(g.dma_start(out=NF3[:, sl * C:(sl + 1) * C],
                                   in_=d["sc_nf"].ap()[t * 128:(t + 1) * 128, :]))
                lg.wait(g)
                x3.bump(g)
        blk.gpsimd(_)
        def _(sync):
            for cc in range(8):
                ld.dma(sync.dma_start(out=CT4[:, cc * N:(cc + 1) * N],
                                      in_=d["sc_bbf"].ap()[1024 + cc * 128:1024 + (cc + 1) * 128,
                                                           2048:4096]))
            ld.wait(sync)
            x2.bump(sync)
            for t in range(NT):
                sl = t % 2
                sync.wait_ge(ev.sem, e0 + t + 1)
                st.dma(sync.dma_start(out=d["sc_tmp"].ap()[t * 128:(t + 1) * 128, :],
                                      in_=OF2[:, sl * C:(sl + 1) * C]))
                st.wait(sync)
                x2.bump(sync)
        blk.sync(_)
        def _(t_):
            t_.wait_ge(x3.sem, x30 + 1)
            t_.wait_ge(x2.sem, x20 + 1)
            for t in range(NT):
                if t >= 2:
                    t_.wait_ge(ev.sem, e0 + t - 1)
                for cop in range(2):
                    for cc in range(8):
                        inst = t_.matmul(
                            pO2[t % 2][:, cop * 512:(cop + 1) * 512],
                            CT4[:, cc * N + t * 128:cc * N + (t + 1) * 128],
                            W1b[:, cc * C + cop * 512:cc * C + (cop + 1) * 512],
                            start=(cc == 0), stop=(cc == 7))
                mm.inc(inst)
        blk.tensor(_)
        def _(v):
            for t in range(NT):
                sl = t % 2
                v.wait_ge(mm.sem, m0 + t + 1)
                v.wait_ge(x3.sem, x30 + t + 2)
                if t >= 2:
                    v.wait_ge(x2.sem, x20 + t)
                ev.inc(v.tensor_tensor(OF2[:, sl * C:(sl + 1) * C], pO2[t % 2][:, :],
                                       NF3[:, sl * C:(sl + 1) * C], OP.add))
        blk.vector(_)
    for g_, _t in reversed(pOg):
        g_.__exit__(None, None, None)
    for g_ in (gOF2, gNF3, gCT4):
        g_.__exit__(None, None, None)
    gW1.__exit__(None, None, None)



def _f1_ladder(eng, ctr, R, M, Z, WQ, width):
    for kb in range(7, -1, -1):
        p = float(2 ** kb)
        ctr.inc(eng.tensor_scalar(M, R, p, -p, OP.is_ge, OP.mult))
        eng.wait_ge(ctr.sem, ctr.n)
        ctr.inc(eng.tensor_tensor(R, R, M, OP.add))
        eng.wait_ge(ctr.sem, ctr.n)
    ctr.inc(eng.tensor_tensor(Z, Z, R, OP.subtract))
    eng.wait_ge(ctr.sem, ctr.n)
    ctr.inc(eng.tensor_scalar(WQ, Z, 255.0, -128.0, OP.min, OP.add))
    eng.wait_ge(ctr.sem, ctr.n)


def phase_f1(k):
    nc, d = k.nc, k.d
    ld, lg, mm, ev, sv, st, x2, x3 = k.ld, k.lg, k.mm, k.ev, k.sv, k.st, k.x2, k.x3

    gWI, WI = sbuf(nc, "f1WI", 128, 8 * 4096, BF16)
    gSCL, SCL = sbuf(nc, "f1SCL", 128, 4, F32)
    gS16, SCL16 = sbuf(nc, "f1S16", 16, 2, F32)
    gB85, B85 = sbuf(nc, "f1B85", 128, 1, F32)

    gAB, AB = sbuf(nc, "f1AB", 128, 2 * 4096, F32)
    gMX, MX = sbuf(nc, "f1MX", 128, 48, F32)
    gMR, MR = sbuf(nc, "f1MR", 1, 256, F32)
    gSR, SCR = sbuf(nc, "f1SCR", 1, 48, F32)
    e0, t0, d0, x20 = ev.n, st.n, ld.n, x2.n

    def _asrc(j):
        if j < 8:
            return d["WfcT"].ap()[j * 128:(j + 1) * 128, :], 4096
        c = j - 8
        return d["WprojT"].ap()[c * 128:(c + 1) * 128, :], 1024

    with nc.Block() as blk:
        def _(sync):
            for j in range(40):
                src, w = _asrc(j)
                if j >= 2:
                    sync.wait_ge(ev.sem, e0 + j - 1)
                ld.dma(sync.dma_start(out=AB[:, (j % 2) * 4096:(j % 2) * 4096 + w], in_=src))
                ld.wait(sync)
                x2.bump(sync)
            sync.wait_ge(ev.sem, e0 + 44)
            st.dma(sync.dma_start(out=ap2(d["sc_misc"].ap(), 9 * 4096, [[1, 128], [1, 1]]),
                                  in_=MX[:, 40:41]))
            st.dma(sync.dma_start(out=ap2(d["sc_misc"].ap(), 9 * 4096 + 128, [[1, 128], [1, 1]]),
                                  in_=MX[:, 41:42]))
            st.wait(sync)
            ld.dma(sync.dma_start(out=MR[0:1, :], in_=d["sc_misc"].ap()[9:10, 0:256]))
            ld.wait(sync)
            x2.bump(sync)
            sync.wait_ge(ev.sem, e0 + 56)
            st.dma(sync.dma_start(out=d["sc_misc"].ap()[10:11, 0:4], in_=SCR[0:1, 40:44]))
            st.wait(sync)
            for i in range(4):
                ld.dma(sync.dma_start(out=SCL[:, i:i + 1],
                                      in_=ap2(d["sc_misc"].ap(), 10 * 4096 + i, [[0, 128], [1, 1]])))
            ld.dma(sync.dma_start(out=SCL16[:, 0:1],
                                  in_=ap2(d["sc_misc"].ap(), 10 * 4096 + 0, [[0, 16], [1, 1]])))
            ld.dma(sync.dma_start(out=SCL16[:, 1:2],
                                  in_=ap2(d["sc_misc"].ap(), 10 * 4096 + 2, [[0, 16], [1, 1]])))
            ld.wait(sync)
            x2.bump(sync)
        blk.sync(_)
        def _(v):
            for j in range(40):
                src, w = _asrc(j)
                v.wait_ge(x2.sem, x20 + j + 1)
                ev.inc(v.reduce_max(MX[:, j:j + 1], AB[:, (j % 2) * 4096:(j % 2) * 4096 + w],
                                    axis=AX, apply_absolute_value=True))
            v.wait_ge(ev.sem, e0 + 40)
            ev.inc(v.reduce_max(MX[:, 40:41], MX[:, 0:8], axis=AX))
            ev.inc(v.reduce_max(MX[:, 41:42], MX[:, 8:40], axis=AX))
            v.wait_ge(ev.sem, e0 + 42)
            ev.inc(v.memset(B85[:, :], 128.5))
            ev.inc(v.memset(MX[:, 42:43], 0.0))
            v.wait_ge(x2.sem, x20 + 41)
            ev.inc(v.reduce_max(SCR[0:1, 0:1], MR[0:1, 0:128], axis=AX))
            ev.inc(v.reduce_max(SCR[0:1, 1:2], MR[0:1, 128:256], axis=AX))
            v.wait_ge(ev.sem, e0 + 46)
            ev.inc(v.tensor_copy(SCR[0:1, 8:16], ap2(SCR[:, :], SCR[:, :].offset + 0, [[48, 1], [0, 8]])))
            v.wait_ge(ev.sem, e0 + 47)
            ev.inc(v.tensor_copy(SCR[0:1, 16:24], ap2(SCR[:, :], SCR[:, :].offset + 1, [[48, 1], [0, 8]])))
            v.wait_ge(ev.sem, e0 + 48)
            ev.inc(v.reciprocal(SCR[0:1, 24:40], SCR[0:1, 8:24]))
            v.wait_ge(ev.sem, e0 + 49)
            ev.inc(v.tensor_scalar_mul(SCR[0:1, 40:41], SCR[0:1, 24:25], 127.0))
            ev.inc(v.tensor_scalar_mul(SCR[0:1, 41:42], SCR[0:1, 8:9], 1.0 / 127.0))
            ev.inc(v.tensor_scalar_mul(SCR[0:1, 42:43], SCR[0:1, 32:33], 127.0))
            ev.inc(v.tensor_scalar_mul(SCR[0:1, 43:44], SCR[0:1, 16:17], 1.0 / 127.0))
            v.wait_ge(ev.sem, e0 + 53)
            ev.inc(v.tensor_copy(SCR[0:1, 44:45], SCR[0:1, 40:41]))
            v.wait_ge(ev.sem, e0 + 54)
            ev.inc(v.tensor_copy(SCR[0:1, 45:46], SCR[0:1, 40:41]))
            v.wait_ge(ev.sem, e0 + 55)
            ev.inc(v.tensor_copy(SCR[0:1, 46:47], SCR[0:1, 40:41]))
        blk.vector(_)
    for g_ in (gSR, gMR, gMX):
        g_.__exit__(None, None, None)

    def quant_stage(tag, nchunk, width, wname):
        gZ, Z = sbuf(nc, f"f1Z{tag}", 128, 2 * width, F32)
        gR, R = sbuf(nc, f"f1R{tag}", 128, 2 * width, F32)
        gM, M = sbuf(nc, f"f1M{tag}", 128, width, F32)
        half = width // 2
        e0, s0, t0, d0, x20, x30 = ev.n, sv.n, st.n, ld.n, x2.n, x3.n
        scol = 0 if tag == "f" else 2
        with nc.Block() as blk:
            def _(sync):
                for c in range(nchunk):
                    if c >= 2:
                        sync.wait_ge(sv.sem, s0 + 2 * (c - 2) + 2)
                    ld.dma(sync.dma_start(out=AB[:, (c % 2) * 4096:(c % 2) * 4096 + width],
                                          in_=d[wname].ap()[c * 128:(c + 1) * 128, :]))
                    ld.wait(sync)
                    x2.bump(sync)
            blk.sync(_)
            def _(s):
                for c in range(nchunk):
                    sl = c % 2
                    s.wait_ge(x2.sem, x20 + c + 1)
                    if c >= 2:
                        s.wait_ge(ev.sem, e0 + 18 * (c - 1))
                        s.wait_ge(x3.sem, x30 + 18 * (c - 1))
                    sv.inc(s.activation(Z[:, sl * width:sl * width + width],
                                        AB[:, sl * 4096:sl * 4096 + width],
                                        AF.Identity, scale=SCL[:, scol:scol + 1], bias=B85[:, 0:1]))
                    s.wait_ge(sv.sem, s0 + 2 * c + 1)
                    sv.inc(s.activation(R[:, sl * width:sl * width + width],
                                        AB[:, sl * 4096:sl * 4096 + width],
                                        AF.Identity, scale=SCL[:, scol:scol + 1], bias=B85[:, 0:1]))
            blk.scalar(_)
            def _(v):
                for c in range(nchunk):
                    sl = c % 2
                    v.wait_ge(sv.sem, s0 + 2 * c + 2)
                    _f1_ladder(v, ev,
                               R[:, sl * width:sl * width + half],
                               M[:, 0:half],
                               Z[:, sl * width:sl * width + half],
                               WI[:, c * width:c * width + half], half)
            blk.vector(_)
            def _(g):
                for c in range(nchunk):
                    sl = c % 2
                    g.wait_ge(sv.sem, s0 + 2 * c + 2)
                    _f1_ladder(g, x3,
                               R[:, sl * width + half:sl * width + width],
                               M[:, half:width],
                               Z[:, sl * width + half:sl * width + width],
                               WI[:, c * width + half:c * width + width], half)
            blk.gpsimd(_)
        for g_ in (gM, gR, gZ):
            g_.__exit__(None, None, None)

    quant_stage("f", 8, 4096, "WfcT")

    gXT, XT = sbuf(nc, "f1XT", 128, 8 * N, BF16)
    gAF, AFc = sbuf(nc, "f1AF", 128, 8 * 16, BF16)
    gBFf, BFf = sbuf(nc, "f1BFf", 16, 4096, F32)
    gBFS, BFS = sbuf(nc, "f1BFS", 16, 4096, BF16)
    gBRf, BRf = sbuf(nc, "f1BRf", 1, 4096, F32)
    gBFR, BFR = sbuf(nc, "f1BFR", 1, 4096, BF16)
    gONE, ONE = sbuf(nc, "f1ONE", 1, 128, BF16)
    gIDN, IDN = sbuf(nc, "f1IDN", 128, 128, BF16)
    gXAe, XAe = sbuf(nc, "f1XAe", 128, 2 * 16, BF16)
    gXAT, XAT = sbuf(nc, "f1XAT", 16, N, BF16)
    gMD, MD = sbuf(nc, "f1MD", 128, 8 * 512, BF16)
    pAg = psum(nc, "f1pA", 128, 16)
    pTg = psum(nc, "f1pT", 16, 128, BF16)
    pMg = psum(nc, "f1pM", 128, 2 * 512)
    pA, pT, pM = pAg[1], pTg[1], pMg[1]
    m0, s0, e0, t0, d0, l0, x20, x30 = mm.n, sv.n, ev.n, st.n, ld.n, lg.n, x2.n, x3.n
    sp = s0 + 2
    xae = {t: sp + 10 * t + 1 for t in range(NT)}
    xat = {t: sp + 10 * t + 2 for t in range(NT)}
    mdA = {(t, cop): sp + 10 * t + 3 + cop for t in range(NT) for cop in range(8)}
    loraA = {t: m0 + 10 * t + 1 for t in range(NT)}
    trA = {t: m0 + 10 * t + 2 for t in range(NT)}
    grpA = {(t, cop): m0 + 10 * t + 3 + cop for t in range(NT) for cop in range(8)}
    with nc.Block() as blk:
        def _(sync):
            for c in range(8):
                ld.dma(sync.dma_start(out=XT[:, c * N:(c + 1) * N],
                                      in_=d["sc_nbf"].ap()[:, c * 128:(c + 1) * 128],
                                      transpose=True))
            ld.wait(sync)
            x2.bump(sync)
            for t in range(NT):
                for cop in range(8):
                    tc = 8 * t + cop
                    sync.wait_ge(sv.sem, mdA[(t, cop)])
                    st.dma(sync.dma_start(
                        out=d["sc_bbf"].ap()[t * 128:(t + 1) * 128, cop * 512:(cop + 1) * 512],
                        in_=MD[:, (tc % 8) * 512:(tc % 8 + 1) * 512]))
                    if tc % 4 == 3:
                        st.wait(sync)
                        x2.bump(sync)
        blk.sync(_)
        def _(g):
            lg.dma(g.dma_start(out=AFc[:, :],
                               in_=ap2(d["AfcT"].ap(), 0, [[16, 128], [128 * 16, 8], [1, 16]])))
            lg.dma(g.dma_start(out=BFf[:, :], in_=d["BfcT"].ap()))
            lg.dma(g.dma_start(out=BRf[0:1, :], in_=d["bfcr"].ap()))
            lg.dma(g.dma_start(out=IDN[:, :], in_=d["idnbf"].ap()))
            lg.wait(g)
            x3.bump(g)
        blk.gpsimd(_)
        def _(v):
            ev.inc(v.memset(ONE[0:1, :], 1.0))
        blk.vector(_)
        def _(s):
            s.wait_ge(x3.sem, x30 + 1)
            sv.inc(s.activation(BFS[:, :], BFf[:, :], AF.Identity, scale=SCL16[:, 0:1]))
            sv.inc(s.activation(BFR[0:1, :], BRf[0:1, :], AF.Identity, scale=SCL[0:1, 0:1]))
            for t in range(NT):
                sl = t % 2
                s.wait_ge(mm.sem, loraA[t])
                sv.inc(s.activation(XAe[:, sl * 16:(sl + 1) * 16], pA[:, :],
                                    AF.Identity, scale=2.0))
                s.wait_ge(mm.sem, trA[t])
                sv.inc(s.activation(XAT[:, t * 128:(t + 1) * 128], pT[:, :],
                                    AF.Identity))
                for cop in range(8):
                    tc = 8 * t + cop
                    s.wait_ge(mm.sem, grpA[(t, cop)])
                    if tc >= 8:
                        s.wait_ge(x2.sem, x20 + 1 + (tc - 8) // 4 + 1)
                    sv.inc(s.activation(MD[:, (tc % 8) * 512:(tc % 8 + 1) * 512],
                                        pM[:, (tc % 2) * 512:(tc % 2 + 1) * 512],
                                        AF.Gelu, scale=SCL[:, 1:2]))
        blk.scalar(_)
        def _(t_):
            t_.wait_ge(x2.sem, x20 + 1)
            t_.wait_ge(x3.sem, x30 + 1)
            t_.wait_ge(ev.sem, e0 + 1)
            t_.wait_ge(sv.sem, sp)
            for t in range(NT):
                sl = t % 2
                if t >= 1:
                    t_.wait_ge(sv.sem, xae[t - 1])
                for c in range(8):
                    inst = t_.matmul(pA[:, :], XT[:, c * N + t * 128:c * N + (t + 1) * 128],
                                     AFc[:, c * 16:(c + 1) * 16],
                                     start=(c == 0), stop=(c == 7))
                mm.inc(inst)
                t_.wait_ge(sv.sem, xae[t])
                if t >= 1:
                    t_.wait_ge(sv.sem, xat[t - 1])
                mm.inc(t_.transpose(pT[:, :], XAe[:, sl * 16:(sl + 1) * 16], IDN[:, :]))
                for cop in range(8):
                    tc = 8 * t + cop
                    if tc >= 2:
                        t_.wait_ge(sv.sem, mdA[((tc - 2) // 8, (tc - 2) % 8)])
                    if cop == 0:
                        t_.wait_ge(sv.sem, xat[t])
                    for c in range(8):
                        t_.matmul(pM[:, (tc % 2) * 512:(tc % 2 + 1) * 512],
                                  XT[:, c * N + t * 128:c * N + (t + 1) * 128],
                                  WI[:, c * 4096 + cop * 512:c * 4096 + (cop + 1) * 512],
                                  start=(c == 0), stop=False, skip_group_check=True)
                    t_.matmul(pM[:, (tc % 2) * 512:(tc % 2 + 1) * 512],
                              XAT[:, t * 128:(t + 1) * 128],
                              BFS[:, cop * 512:(cop + 1) * 512],
                              start=False, stop=False, skip_group_check=True)
                    inst = t_.matmul(pM[:, (tc % 2) * 512:(tc % 2 + 1) * 512],
                                     ONE[0:1, :],
                                     BFR[0:1, cop * 512:(cop + 1) * 512],
                                     start=False, stop=True, skip_group_check=True)
                    mm.inc(inst)
        blk.tensor(_)
    for g_, _t in ((pMg, 0), (pTg, 0), (pAg, 0)):
        g_[0].__exit__(None, None, None)
    for g_ in (gMD, gXAT, gXAe, gIDN, gONE, gBFR, gBRf, gBFS, gBFf, gAF, gXT):
        g_.__exit__(None, None, None)

    quant_stage("p", 32, 1024, "WprojT")

    gMT, MT = sbuf(nc, "f1MT", 128, 2 * 4096, BF16)
    gAP2, APJ = sbuf(nc, "f1APJ", 128, 32 * 16, BF16)
    gBPf, BPf = sbuf(nc, "f1BPf", 16, 1024, F32)
    gBPS, BPS = sbuf(nc, "f1BPS", 16, 1024, BF16)
    gBRp, BRp = sbuf(nc, "f1BRp", 1, 1024, F32)
    gBPR, BPR = sbuf(nc, "f1BPR", 1, 1024, BF16)
    gON2, ONE2 = sbuf(nc, "f1ON2", 1, 128, BF16)
    gID2, IDN2 = sbuf(nc, "f1ID2", 128, 128, BF16)
    gXPe, XPe = sbuf(nc, "f1XPe", 128, 2 * 16, BF16)
    gXPT, XPT = sbuf(nc, "f1XPT", 16, N, BF16)
    gOF, OFb = sbuf(nc, "f1OF", 128, 2 * 1024, F32)
    pA2g = psum(nc, "f1pA2", 128, 16)
    pT2g = psum(nc, "f1pT2", 16, 128, BF16)
    pPg = psum(nc, "f1pP", 128, 2 * 512)
    pA2, pT2, pP = pA2g[1], pT2g[1], pPg[1]
    m0, s0, e0, t0, d0, l0, x20, x30 = mm.n, sv.n, ev.n, st.n, ld.n, lg.n, x2.n, x3.n
    sp = s0 + 2
    xpe = {t: sp + 4 * t + 1 for t in range(NT)}
    xpt = {t: sp + 4 * t + 2 for t in range(NT)}
    ofA = {(t, cop): sp + 4 * t + 3 + cop for t in range(NT) for cop in range(2)}
    lorP = {t: m0 + 4 * t + 1 for t in range(NT)}
    trP = {t: m0 + 4 * t + 2 for t in range(NT)}
    grpP = {(t, cop): m0 + 4 * t + 3 + cop for t in range(NT) for cop in range(2)}
    with nc.Block() as blk:
        def _(sync):
            nb = 0
            for t in range(NT):
                if t >= 2:
                    sync.wait_ge(mm.sem, grpP[(t - 2, 1)])
                for c in range(32):
                    ld.dma(sync.dma_start(
                        out=MT[:, (t % 2) * 4096 + c * 128:(t % 2) * 4096 + (c + 1) * 128],
                        in_=d["sc_bbf"].ap()[t * 128:(t + 1) * 128, c * 128:(c + 1) * 128],
                        transpose=True))
                ld.wait(sync)
                nb += 1
                sync.sem_inc(x2.sem, 1)
                if t >= 1:
                    sync.wait_ge(sv.sem, ofA[(t - 1, 1)])
                    st.dma(sync.dma_start(out=d["sc_tmp"].ap()[(t - 1) * 128:t * 128, :],
                                          in_=OFb[:, ((t - 1) % 2) * 1024:((t - 1) % 2 + 1) * 1024]))
                    st.wait(sync)
                    nb += 1
                    sync.sem_inc(x2.sem, 1)
            sync.wait_ge(sv.sem, ofA[(NT - 1, 1)])
            st.dma(sync.dma_start(out=d["sc_tmp"].ap()[(NT - 1) * 128:NT * 128, :],
                                  in_=OFb[:, ((NT - 1) % 2) * 1024:((NT - 1) % 2 + 1) * 1024]))
            st.wait(sync)
            nb += 1
            sync.sem_inc(x2.sem, 1)
            x2.n += nb
        blk.sync(_)
        def _(g):
            lg.dma(g.dma_start(out=APJ[:, :],
                               in_=ap2(d["AprojT"].ap(), 0, [[16, 128], [128 * 16, 32], [1, 16]])))
            lg.dma(g.dma_start(out=BPf[:, :], in_=d["BprojT"].ap()))
            lg.dma(g.dma_start(out=BRp[0:1, :], in_=d["bproj"].ap()))
            lg.dma(g.dma_start(out=IDN2[:, :], in_=d["idnbf"].ap()))
            lg.wait(g)
            x3.bump(g)
        blk.gpsimd(_)
        def _(v):
            ev.inc(v.memset(ONE2[0:1, :], 1.0))
        blk.vector(_)
        def _(s):
            s.wait_ge(x3.sem, x30 + 1)
            sv.inc(s.activation(BPS[:, :], BPf[:, :], AF.Identity, scale=SCL16[:, 1:2]))
            sv.inc(s.activation(BPR[0:1, :], BRp[0:1, :], AF.Identity, scale=SCL[0:1, 2:3]))
            for t in range(NT):
                sl = t % 2
                s.wait_ge(mm.sem, lorP[t])
                sv.inc(s.activation(XPe[:, sl * 16:(sl + 1) * 16], pA2[:, :],
                                    AF.Identity, scale=2.0))
                s.wait_ge(mm.sem, trP[t])
                sv.inc(s.activation(XPT[:, t * 128:(t + 1) * 128], pT2[:, :], AF.Identity))
                for cop in range(2):
                    s.wait_ge(mm.sem, grpP[(t, cop)])
                    if t >= 2 and cop == 0:
                        s.wait_ge(x2.sem, x20 + 2 * t - 1)
                    sv.inc(s.activation(OFb[:, sl * 1024 + cop * 512:sl * 1024 + (cop + 1) * 512],
                                        pP[:, cop * 512:(cop + 1) * 512],
                                        AF.Identity, scale=SCL[:, 3:4]))
        blk.scalar(_)
        def _(t_):
            t_.wait_ge(x3.sem, x30 + 1)
            t_.wait_ge(ev.sem, e0 + 1)
            t_.wait_ge(sv.sem, sp)
            mt_abs = {}
            cur = x20
            for t in range(NT):
                cur += 1
                mt_abs[t] = cur
                if t >= 1:
                    cur += 1
            for t in range(NT):
                sl = t % 2
                t_.wait_ge(x2.sem, mt_abs[t])
                if t >= 1:
                    t_.wait_ge(sv.sem, xpe[t - 1])
                for c in range(32):
                    inst = t_.matmul(pA2[:, :],
                                     MT[:, sl * 4096 + c * 128:sl * 4096 + (c + 1) * 128],
                                     APJ[:, c * 16:(c + 1) * 16],
                                     start=(c == 0), stop=(c == 31))
                mm.inc(inst)
                t_.wait_ge(sv.sem, xpe[t])
                if t >= 1:
                    t_.wait_ge(sv.sem, xpt[t - 1])
                mm.inc(t_.transpose(pT2[:, :], XPe[:, sl * 16:(sl + 1) * 16], IDN2[:, :]))
                for cop in range(2):
                    if t >= 1:
                        t_.wait_ge(sv.sem, ofA[(t - 1, cop)])
                    if cop == 0:
                        t_.wait_ge(sv.sem, xpt[t])
                    for c in range(32):
                        t_.matmul(pP[:, cop * 512:(cop + 1) * 512],
                                  MT[:, sl * 4096 + c * 128:sl * 4096 + (c + 1) * 128],
                                  WI[:, c * 1024 + cop * 512:c * 1024 + (cop + 1) * 512],
                                  start=(c == 0), stop=False, skip_group_check=True)
                    t_.matmul(pP[:, cop * 512:(cop + 1) * 512],
                              XPT[:, t * 128:(t + 1) * 128],
                              BPS[:, cop * 512:(cop + 1) * 512],
                              start=False, stop=False, skip_group_check=True)
                    inst = t_.matmul(pP[:, cop * 512:(cop + 1) * 512],
                                     ONE2[0:1, :],
                                     BPR[0:1, cop * 512:(cop + 1) * 512],
                                     start=False, stop=True, skip_group_check=True)
                    mm.inc(inst)
        blk.tensor(_)
    for g_ in (pPg, pT2g, pA2g):
        g_[0].__exit__(None, None, None)
    for g_ in (gOF, gXPT, gXPe, gID2, gON2, gBPR, gBRp, gBPS, gBPf, gAP2, gMT):
        g_.__exit__(None, None, None)
    gAB.__exit__(None, None, None)
    for g_ in (gB85, gS16, gSCL, gWI):
        g_.__exit__(None, None, None)


def phase_load_x(k):
    nc, d = k.nc, k.d
    st = k.st
    with nc.Block() as blk:
        def _(sync):
            st.dma(sync.dma_start(out=d["sc_x"].ap(), in_=d["x_in"].ap()))
            st.wait(sync)
        blk.sync(_)


def store_tiles(k, T, dst, nt=NT, width=C):
    nc, st = k.nc, k.st
    with nc.Block() as blk:
        def _(sync):
            for n in range(nt):
                st.dma(sync.dma_start(out=k.d[dst].ap()[n * 128:(n + 1) * 128, 0:width],
                                      in_=T[:, n * width:(n + 1) * width]))
            st.wait(sync)
        blk.sync(_)


def build(nc, dbg=(), upto=None):
    _PHASE_IDS["ln"] = 0; _PHASE_IDS["add"] = 0
    k = K(nc, dbg=dbg)
    phase_load_x(k)
    phase_ln(k, want_f32=True)
    k.dump("nf", "sc_nf", N, C)
    if upto == "ln1":
        return k
    gH, H = sbuf(nc, "H", 128, NT * C, F32)
    if upto == "zconv":
        import os as _os
        phase_a0_conv(k, H, only=_os.environ.get("CONVONLY") or None)
        phase_a0_layer(k, H, 0, blocks=42, skip14=True)
        k.dump("l1zt", "sc_tmp", N, C)
        k.dump("nbf1", "sc_nbf", N, C, dt=BF16)
        gH.__exit__(None, None, None)
        return k
    if upto == "zctx":
        phase_a0_layer(k, H, 0, blocks=42, skip14=True)
        k.dump("l1zt", "sc_tmp", N, C)
        k.dump("nbf1", "sc_nbf", N, C, dt=BF16)
        gH.__exit__(None, None, None)
        return k
    phase_a0_conv(k, H)
    if upto == "h0":
        store_tiles(k, H, "sc_tmp")
        k.dump("h0", "sc_tmp", N, C)
        gH.__exit__(None, None, None)
        return k
    if upto == "l1q":
        phase_a0_layer(k, H, 0, blocks=3)
        k.dump("l1q", "sc_nbf", N, C, dt=BF16)
        gH.__exit__(None, None, None)
        return k
    if upto == "l1kv":
        phase_a0_layer(k, H, 0, blocks=4)
        k.dump("l1kv", "sc_misc", 1, C)
        k.dump("l1kv2", "sc_tmp", 256, C)
        gH.__exit__(None, None, None)
        return k
    if upto == "l1om":
        phase_a0_layer(k, H, 0, blocks=44)
        k.dump("l1om", "sc_misc", 1, C)
        gH.__exit__(None, None, None)
        return k
    if upto == "l1loop":
        phase_a0_layer(k, H, 0, blocks=42)
        k.dump("l1zt", "sc_tmp", N, C)
        k.dump("nbf1", "sc_nbf", N, C, dt=BF16)
        gH.__exit__(None, None, None)
        return k
    if upto == "l1mini":
        phase_a0_layer(k, H, 0, blocks=41)
        k.dump("l1zt", "sc_tmp", 384, C)
        gH.__exit__(None, None, None)
        return k
    if upto == "l1z":
        phase_a0_layer(k, H, 0, blocks=43)
        k.dump("l1z", "sc_misc", 4, C)
        k.dump("l1zt", "sc_tmp", 384, C)
        gH.__exit__(None, None, None)
        return k
    if upto == "l1of":
        phase_a0_layer(k, H, 0, blocks=45)
        k.dump("l1of", "sc_tmp", N, C)
        gH.__exit__(None, None, None)
        return k
    for li in range(4):
        phase_a0_layer(k, H, li)
        if upto == f"h{li+1}":
            store_tiles(k, H, "sc_tmp")
            k.dump(f"h{li+1}", "sc_tmp", N, C)
            gH.__exit__(None, None, None)
            return k
    phase_a0_mlp(k, H)
    store_tiles(k, H, "sc_tmp")
    if upto == "a0":
        k.dump("a0", "sc_tmp", N, C)
        gH.__exit__(None, None, None)
        return k
    gH.__exit__(None, None, None)
    phase_add(k)
    if upto == "x1":
        k.dump("x1", "sc_x", N, C)
        return k
    phase_ln(k, want_lo=True)
    phase_f0(k)
    if upto == "f0":
        k.dump("f0", "sc_tmp", N, C)
        return k
    phase_add(k)
    if upto == "x2":
        k.dump("x2", "sc_x", N, C)
        return k
    phase_ln(k, want_f32=True)
    phase_a1(k)
    if upto == "a1":
        k.dump("a1", "sc_tmp", N, C)
        return k
    phase_add(k)
    if upto == "x3":
        k.dump("x3", "sc_x", N, C)
        return k
    phase_ln(k)
    phase_f1(k)
    if upto == "f1":
        k.dump("f1", "sc_tmp", N, C)
        return k
    phase_add(k, final=True)
    return k


def kernel(**inputs):
    nc = bass.Bass("TRN2", target_bir_lowering=False)
    build(nc)
    m = host_prep(inputs)
    x = np.ascontiguousarray(np.asarray(inputs["L0_attn_x"], dtype=np.float32))
    in_maps = []
    for b in range(8):
        mm = dict(m)
        mm["x_in"] = x[b]
        in_maps.append(mm)
    res = run_bass_kernel_spmd(nc, in_maps, core_ids=list(range(8)))
    return np.stack([r["out"] for r in res.results])
